# version 22
# baseline (speedup 1.0000x reference)
import os
import numpy as np
import ml_dtypes
import concourse.bass as bass
import concourse.mybir as mybir
from concourse.bass_utils import run_bass_kernel_spmd

F32 = mybir.dt.float32
BF16 = mybir.dt.bfloat16
I32 = mybir.dt.int32
AF = mybir.ActivationFunctionType
ALU = mybir.AluOpType

T = 2048
D = 1024
NCH = 8
NH = 16
HD = 64
NL = 4
HID = 3072
NHC = 24
EPS = 1e-5
S127 = 1.0 / 127.0
KAPPA = 2.0
VGRP = 65

KDBG = os.environ.get("KDBG", "")

class _Dummy:
    def then_inc(self, *a, **k):
        return self

_DUMMY = _Dummy()

class Builder:
    def __init__(self):
        self.dry = True
        self.counts = {}
        self.marks = {}
        self.hiwater = {}
        self.engines = {}
        self.sems = {}

class Prox:
    def __init__(self, B, key):
        self._B = B
        self._key = key

    def __getattr__(self, name):
        B = self._B
        key = self._key
        def call(*a, **k):
            if B.dry:
                return _DUMMY
            return getattr(B.engines[key], name)(*a, **k)
        return call

    def mark(self, ins, name=None):
        B = self._B
        key = self._key
        B.counts[key] = B.counts.get(key, 0) + 1
        if name is not None:
            if B.dry:
                assert name not in B.marks, f"dup mark {name}"
                B.marks[name] = (key, B.counts[key])
            else:
                assert B.marks[name] == (key, B.counts[key]), f"mark drift {name}"
        if not B.dry:
            ins.then_inc(B.sems[key], 1)

    def dma(self, out, in_, name=None, **kw):
        B = self._B
        key = self._key
        B.counts[key] = B.counts.get(key, 0) + 16
        if name is not None:
            if B.dry:
                assert name not in B.marks, f"dup mark {name}"
                B.marks[name] = (key, B.counts[key])
            else:
                assert B.marks[name] == (key, B.counts[key]), f"mark drift {name}"
        if not B.dry:
            B.engines[key].dma_start(out=out, in_=in_, **kw).then_inc(B.sems[key], 16)

    def wait(self, *names):
        B = self._B
        for nm in names:
            if nm is None:
                continue
            if B.dry:
                continue
            skey, val = B.marks[nm]
            self.wait_counts(skey, val)

    def wait_counts(self, skey, val):
        B = self._B
        if B.dry or val <= 0:
            return
        if skey == self._key:
            return
        hw = B.hiwater.get((self._key, skey), 0)
        if val <= hw:
            return
        B.hiwater[(self._key, skey)] = val
        B.engines[self._key].wait_ge(B.sems[skey], val)

    def wait_self(self):
        B = self._B
        if B.dry:
            return
        B.engines[self._key].wait_ge(B.sems[self._key], B.counts.get(self._key, 0))


def build_nc():
    nc = bass.Bass("TRN2", target_bir_lowering=False)

    DT = nc.dram_tensor
    x_t = DT("x_t", [T, D], F32, kind="ExternalInput").ap()
    xT_blk = DT("xT_blk", [NCH, 128, T], F32, kind="ExternalInput").ap()
    winp = DT("winp", [24, 128, D], F32, kind="ExternalInput").ap()
    binp = DT("binp", [128, 24], F32, kind="ExternalInput").ap()
    wout = DT("wout", [8, 128, D], F32, kind="ExternalInput").ap()
    wv2_blk = DT("wv2_blk", [8, 128, D], F32, kind="ExternalInput").ap()
    wo2_blk = DT("wo2_blk", [8, 128, D], F32, kind="ExternalInput").ap()
    bout_b = DT("bout_b", [128, D], F32, kind="ExternalInput").ap()
    bv_b = DT("bv_b", [128, D], F32, kind="ExternalInput").ap()
    lnw_c = DT("lnw_c", [128, NCH], F32, kind="ExternalInput").ap()
    lnb_c = DT("lnb_c", [128, NCH], F32, kind="ExternalInput").ap()
    w02T = DT("w02T", [NL, 2, 128, 512], F32, kind="ExternalInput").ap()
    w1r = DT("w1r", [NL, 2, 128, 256], F32, kind="ExternalInput").ap()
    w0nat = DT("w0nat", [NL, 2, 128, 256], F32, kind="ExternalInput").ap()
    w1nat = DT("w1nat", [NL, 2, 128, 256], F32, kind="ExternalInput").ap()
    w2nat = DT("w2nat", [NL, 2, 128, 256], F32, kind="ExternalInput").ap()
    wg_blk = DT("wg_blk", [NHC, 128, D], I32, kind="ExternalInput").ap()
    wu_blk = DT("wu_blk", [NHC, 128, D], I32, kind="ExternalInput").ap()
    wdT_blk = DT("wdT_blk", [NHC, 128, D], I32, kind="ExternalInput").ap()
    id128 = DT("id128", [128, 128], F32, kind="ExternalInput").ap()
    e2c = DT("e2c", [128, 2], F32, kind="ExternalInput").ap()
    ones1 = DT("ones1", [1, 128], F32, kind="ExternalInput").ap()
    selp = DT("selp", [8, 16, 128], F32, kind="ExternalInput").ap()
    onescol = DT("onescol", [128, 1], F32, kind="ExternalInput").ap()
    fconst = DT("fconst", [1, 2], F32, kind="ExternalInput").ap()

    out = DT("out", [T, D], F32, kind="ExternalOutput").ap()
    xn_spill = DT("xn_spill", [128, NCH * T], BF16, kind="Internal").ap()

    S = nc.alloc_sbuf_tensor
    id_sb = S("id_sb", [128, 128], BF16)
    e2_sb = S("e2_sb", [128, 2], BF16)
    ones1_sb = S("ones1_sb", [1, 128], BF16)
    onescol_sb = S("onescol_sb", [128, 1], BF16)
    selp_sb = S("selp_sb", [16, 8 * 128], BF16)
    binp_sb = S("binp_sb", [128, 24], F32)
    qb_sb = S("qb_sb", [128, 8], F32)
    lnw_sb = S("lnw_sb", [128, NCH], F32)
    lnb_sb = S("lnb_sb", [128, NCH], F32)
    bacc = S("bacc", [128, 4], F32)
    bacc_bf = S("bacc_bf", [128, 2], BF16)
    nrow = S("nrow", [1, 16], F32)
    nrow_bf = S("nrow_bf", [1, 16], BF16)
    srow_tmp = S("srow_tmp", [1, 512], F32)
    srows = S("srows", [16, 2048], F32)
    rall = S("rall", [16, 2048], BF16)
    fc_sb = S("fc_sb", [1, 2], F32)

    s_sb = S("s_sb", [128, 16 * D], BF16)

    AR_N = 75008
    arena = S("arena", [128, AR_N], BF16)

    class Carver:
        def __init__(self):
            self.off = 0
        def take(self, n, name=""):
            n = (n + 15) & ~15
            ap = arena[:, self.off:self.off + n]
            self.off += n
            assert self.off <= AR_N, f"arena overflow at {name}: {self.off}"
            return ap
        def take_f32(self, n, name=""):
            return self.take(2 * n, name).bitcast(F32)

    cv = Carver()
    off0 = cv.off
    x2f = cv.take(NCH * T, "x2f")
    offX = cv.off

    cv.off = offX
    xbf = cv.take(NCH * T, "xbf")
    xnf_a = cv.take(NCH * T, "xnf_a")
    sqs = cv.take(T, "sqs")
    tmpA = cv.take_f32(512, "tmpA")
    rows = cv.take(3 * 2048, "rows")
    offA = cv.off

    cv.off = offX
    v_sb = cv.take(16 * 16 * VGRP, "v_sb")
    aou = cv.take(NCH * T, "aou")
    qk_sb = cv.take(2 * T, "qk")
    exp_sb = cv.take(4 * T, "exp")
    ubias = cv.take_f32(16, "ubias")
    rb_bf = cv.take(512, "rb")
    cv.off = max(cv.off, offA)
    bvb = cv.take_f32(D, "bvb")
    wv_sb = cv.take(NCH * 512, "wv")
    wqk_sb = cv.take(2 * 2048, "wqk")
    offB = cv.off
    assert offB <= AR_N

    cv.off = offX
    xt_sb = [cv.take(16 * 256, f"xt{i}") for i in range(2)]
    w02_sb = cv.take(NL * 2 * 512, "w02")
    w1r_sb = cv.take(NL * 2 * 256, "w1r")
    w0n_sb = cv.take_f32(NL * 2 * 256, "w0n")
    w1n_sb = cv.take_f32(NL * 2 * 256, "w1n")
    w2n_sb = cv.take_f32(NL * 2 * 256, "w2n")
    hid_t = cv.take(8 * 256, "hid_t")
    pg_t = cv.take(8 * 512, "pg_t")
    sg_f = [cv.take_f32(256, f"sg{i}") for i in range(2)]
    hh_f = [cv.take_f32(256, f"hh{i}") for i in range(2)]
    g_f = [cv.take_f32(256, f"g{i}") for i in range(2)]
    dh_f = [cv.take_f32(256, f"dh{i}") for i in range(2)]
    v2_f = [cv.take_f32(256, f"v2{i}") for i in range(2)]
    u_f = [cv.take_f32(256, f"u{i}") for i in range(2)]
    wn_bf = cv.take(3 * 2 * 256, "wn_bf")
    wnT_bf = cv.take(3 * 2 * 256, "wnT_bf")
    hq_f = [cv.take_f32(512, f"hq{i}") for i in range(2)]
    z_sb = cv.take(2 * 1024, "z_sb")
    offC = cv.off

    cv.off = off0
    ht_sb = cv.take(NHC * 512, "ht")
    wd_sb = cv.take(NHC * D, "wd")
    xnf_d = [cv.take(NCH * 512, f"xnf_d{i}") for i in range(2)]
    silu_f = cv.take_f32(512, "silu")
    sig_f = [cv.take_f32(512, f"sig{i}") for i in range(2)]
    gg_f = [cv.take_f32(512, f"gg{i}") for i in range(2)]
    sgp_f = [cv.take_f32(512, f"sgp{i}") for i in range(2)]
    x_f = [cv.take_f32(512, f"xf{i}") for i in range(4)]
    outb = [cv.take_f32(512, f"ob{i}") for i in range(2)]
    offD = cv.off
    cv.off = max(offC, offD)
    wg_sb = [cv.take(D, f"wg{i}") for i in range(4)]
    wu_sb = [cv.take(D, f"wu{i}") for i in range(4)]
    assert max(offA, offB, offC, offD, cv.off) <= AR_N

    bob = wqk_sb[:, 0:2048].bitcast(F32)

    P = nc.alloc_psum_tensor
    SC_A = P("SC_A", [128, 1024], F32)
    SC_B = P("SC_B", [128, 1024], F32)
    AO = P("AO", [128, 2048], F32)

    B = Builder()
    STREAMS = ("pe", "act", "dve", "gq", "sq",
               "qC", "qW", "qV", "qF", "qD",
               "qS2", "qN", "qX", "qO",
               "qP")
    for key in STREAMS:
        B.sems[key] = nc.alloc_semaphore("sem_" + key)
    pe = Prox(B, "pe")
    act = Prox(B, "act")
    dve = Prox(B, "dve")
    gq = Prox(B, "gq")
    sq = Prox(B, "sq")
    qC = Prox(B, "qC")
    qW = Prox(B, "qW")
    qV = Prox(B, "qV")
    qF = Prox(B, "qF")
    qD = Prox(B, "qD")
    qS2 = Prox(B, "qS2")
    qN = Prox(B, "qN")
    qX = Prox(B, "qX")
    qO = Prox(B, "qO")
    pool = Prox(B, "qP")
    ALL = (pe, act, dve, gq, sq, qC, qW, qV, qF, qD, qS2, qN, qX, qO, pool)

    def alias(dst, src):
        if B.dry:
            assert dst not in B.marks, f"dup alias {dst}"
            B.marks[dst] = B.marks[src]
        else:
            assert B.marks[dst] == B.marks[src]

    def sync_all(engines=ALL):
        snap = dict(B.counts)
        for e in engines:
            for k in ("pe", "act", "dve", "gq", "sq"):
                if k != e._key and snap.get(k):
                    e.wait_counts(k, snap[k])

    KDBG2 = os.environ.get("KDBG2", "")

    def dump_now(name, ap, dt):
        if name not in KDBG2.split(","):
            return
        key = "dbg2t_" + name
        if B.dry:
            B.marks[key] = nc.dram_tensor("dbg2_" + name, list(ap.shape), dt,
                                          kind="ExternalOutput").ap()
        snap = dict(B.counts)
        for k in ("pe", "act", "dve", "gq"):
            if snap.get(k):
                sq.wait_counts(k, snap[k])
        sq.dma(B.marks[key], ap, None)

    def program():
        qC.dma(id_sb[:], id128, "ld_id")
        sq.dma(fc_sb[:], fconst, "ld_fc")
        qC.dma(e2_sb[:], e2c, "ld_e2")
        qC.dma(ones1_sb[:], ones1, "ld_ones1")
        qC.dma(onescol_sb[:], onescol, "ld_onescol")
        qC.dma(selp_sb[:].rearrange("h (j p) -> h j p", j=8), selp.rearrange("j h p -> h j p"), "ld_selp")
        sq.dma(binp_sb[:], binp, "ld_binp")
        sq.dma(lnw_sb[:], lnw_c, "ld_lnw")
        sq.dma(lnb_sb[:], lnb_c, "ld_lnb")
        sq.dma(bvb, bv_b, "ld_bvb")
        for c in range(NCH):
            gq.dma(xbf[:, c * T:(c + 1) * T], xT_blk[c], f"ld_xbf{c}")
        for c in range(NCH):
            qV.dma(wv_sb[:, c * 512:(c + 1) * 512], wv2_blk[c][:, 0:512],
                   f"ld_wv0_{c}")

        dve.wait("ld_bvb")
        dve.mark(dve.tensor_single_scalar(qb_sb[:], binp_sb[:, 0:8], 0.125, ALU.mult),
                 "qb_ready")

        mu_sl = rows[:1, 0:2048]
        var_sl = rows[:1, 2048:4096]
        r_sl = rows[:1, 4096:6144]

        def ln_stats(src, tag):
            for c in range(NCH):
                act.wait(f"src_{tag}{c}")
                if c > 0:
                    act.wait_counts("pe", B.counts["pe"])
                act.mark(act.activation(sqs, src[:, c * T:(c + 1) * T], AF.Square),
                         f"sq_{tag}{c}")
                for t4 in range(4):
                    s2p = (SC_A if t4 < 2 else SC_B)[0:1, (t4 % 2) * 512:(t4 % 2) * 512 + 512]
                    pe.wait(f"src_{tag}{c}")
                    if c == 0 and t4 == 0:
                        pe.wait(f"src_{tag}{NCH-1}")
                        pe.wait_counts("dve", B.counts["dve"])
                    pe.mark(pe.matmul(AO[0:1, t4 * 512:(t4 + 1) * 512], onescol_sb[:],
                                      src[:, c * T + t4 * 512:c * T + (t4 + 1) * 512],
                                      start=(c == 0), stop=(c == NCH - 1)),
                            f"s1_{tag}{t4}" if c == NCH - 1 else None)
                    pe.wait(f"sq_{tag}{c}")
                    pe.mark(pe.matmul(s2p, onescol_sb[:],
                                      sqs[:, t4 * 512:(t4 + 1) * 512],
                                      start=(c == 0), stop=(c == NCH - 1)),
                            f"s2_{tag}{t4}" if c == NCH - 1 else None)
            for t4 in range(4):
                sl = slice(t4 * 512, (t4 + 1) * 512)
                s2p = (SC_A if t4 < 2 else SC_B)[0:1, (t4 % 2) * 512:(t4 % 2) * 512 + 512]
                dve.wait(f"s1_{tag}{t4}", f"s2_{tag}{t4}")
                dve.mark(dve.tensor_single_scalar(mu_sl[:, sl], AO[0:1, sl], 1.0 / D,
                                                  ALU.mult), f"mu_{tag}{t4}")
                dve.wait_self()
                dve.mark(dve.tensor_tensor(var_sl[:, sl], mu_sl[:, sl], mu_sl[:, sl],
                                           ALU.mult), None)
                dve.wait_self()
                dve.mark(dve.scalar_tensor_tensor(var_sl[:, sl], s2p, 1.0 / D,
                                                  var_sl[:, sl], ALU.mult, ALU.subtract),
                         f"var_{tag}{t4}")
                act.wait(f"var_{tag}{t4}", "ld_fc")
                act.mark(act.activation(var_sl[:, sl], var_sl[:, sl], AF.Ln,
                                        bias=fc_sb[:, 0:1]), None)
                act.wait_self()
                act.mark(act.activation(r_sl[:, sl], var_sl[:, sl], AF.Exp, scale=-0.5),
                         f"r_{tag}{t4}")

        for c in range(NCH):
            alias(f"src_x{c}", f"ld_xbf{NCH-1}")
        ln_stats(xbf, "x")
        for t4 in range(4):
            sl = slice(t4 * 512, (t4 + 1) * 512)
            pe.wait(f"mu_x{t4}")
            pe.wait_counts("dve", B.counts["dve"])
            pe.mark(pe.matmul(AO[:, 0:512],
                              ones1_sb[:], mu_sl[:, sl], start=True, stop=True),
                    f"mub_x{t4}")
            pe.wait(f"r_x{t4}")
            pe.mark(pe.matmul(AO[:, 512:1024], ones1_sb[:], r_sl[:, sl],
                              start=True, stop=True), f"rb_x{t4}")
            for c in range(NCH):
                csl = slice(c * T + t4 * 512, c * T + (t4 + 1) * 512)
                dve.wait(f"mub_x{t4}", f"rb_x{t4}", "ld_bvb")
                dve.mark(dve.tensor_tensor(tmpA, xbf[:, csl], AO[:, 0:512], ALU.subtract),
                         None)
                dve.mark(dve.tensor_tensor(xnf_a[:, csl], tmpA, AO[:, 512:1024], ALU.mult),
                         f"xn_{c}_{t4}")
        for c in range(NCH):
            alias(f"src_n{c}", f"xn_{c}_3")
            sq.wait(f"xn_{c}_3")
            sq.dma(xn_spill[:, c * T:(c + 1) * T], xnf_a[:, c * T:(c + 1) * T],
                   f"spill_{c}")
        ln_stats(xnf_a, "n")
        for t4 in range(4):
            sl = slice(t4 * 512, (t4 + 1) * 512)
            pe.wait(f"mu_n{t4}")
            pe.wait_counts("dve", B.counts["dve"])
            pe.mark(pe.matmul(AO[:, 0:512], ones1_sb[:], mu_sl[:, sl],
                              start=True, stop=True), f"mub_n{t4}")
            pe.wait(f"r_n{t4}")
            pe.mark(pe.matmul(AO[:, 512:1024], ones1_sb[:], r_sl[:, sl],
                              start=True, stop=True), f"rb_n{t4}")
            for c in range(NCH):
                csl = slice(c * T + t4 * 512, c * T + (t4 + 1) * 512)
                dve.wait(f"mub_n{t4}", f"rb_n{t4}", "ld_bvb")
                dve.mark(dve.tensor_tensor(tmpA, xnf_a[:, csl], AO[:, 0:512],
                                           ALU.subtract), None)
                dve.mark(dve.scalar_tensor_tensor(tmpA, tmpA, lnw_sb[:, c:c + 1],
                                                  AO[:, 512:1024], ALU.mult, ALU.mult),
                         None)
                dve.mark(dve.tensor_scalar_add(x2f[:, csl], tmpA, lnb_sb[:, c:c + 1]),
                         f"x2_{c}_{t4}")
        sync_all()
        dump_now("xbf", xbf, BF16)
        dump_now("xnf", xnf_a, BF16)
        dump_now("rows", rows, BF16)

        for js in range(2):
            if js == 1:
                for c in range(NCH):
                    qV.wait(f"vp_15_0")
                    qV.dma(wv_sb[:, c * 512:(c + 1) * 512], wv2_blk[c][:, 512:1024],
                           f"ld_wv1_{c}")
            for ct in range(16):
                pb = AO[:, (ct % 4) * 512:(ct % 4) * 512 + 512]
                for c in range(NCH):
                    if c == 0:
                        pe.wait(f"ld_wv{js}_{NCH-1}" if js == 1 else f"ld_wv0_{NCH-1}")
                        if ct >= 4 or js == 1:
                            pv = (js, ct - 4) if ct >= 4 else (0, 12 + ct)
                            pe.wait(f"vev_{pv[1]}_{pv[0]}")
                    pe.mark(pe.matmul(pb, x2f[:, c * T + ct * 128:c * T + (ct + 1) * 128],
                                      wv_sb[:, c * 512:(c + 1) * 512],
                                      start=(c == 0), stop=(c == NCH - 1)),
                            f"vp_{ct}_{js}" if c == NCH - 1 else None)
                dve.wait(f"vp_{ct}_{js}", "ld_bvb")
                dve.mark(dve.tensor_tensor(rb_bf, pb,
                                           bvb[:, js * 512:(js + 1) * 512], ALU.add),
                         f"vev_{ct}_{js}")
                vdst = v_sb[:, ct * 1040 + js * 8 * VGRP:ct * 1040 + (js + 1) * 8 * VGRP]
                dve.mark(dve.tensor_copy(
                    vdst.rearrange("p (g c) -> p g c", c=VGRP)[:, :, 0:64],
                    rb_bf.rearrange("p (g c) -> p g c", c=64)), f"vevs_{ct}_{js}")
        dve.mark(dve.memset(v_sb.rearrange("p (g c) -> p g c", c=VGRP)[:, :, 64:65], 1.0),
                 "v_ones")

        for j in range(8):
            par = j % 2
            qW.wait(f"qkp_{j-2}_1_3" if j >= 2 else None)
            qW.dma(wqk_sb[:, par * 2048:par * 2048 + 1024], winp[j], f"ld_wq{j}")
            qW.dma(wqk_sb[:, par * 2048 + 1024:par * 2048 + 2048], winp[8 + j],
                   f"ld_wk{j}")
            for part in range(2):
                wofs = par * 2048 + part * 1024
                for t4 in range(4):
                    pb = (SC_A if part == 0 else SC_B)[:, (t4 % 2) * 512:(t4 % 2) * 512 + 512]
                    for c in range(NCH):
                        if c == 0:
                            pe.wait(f"ld_wq{j}" if part == 0 else f"ld_wk{j}")
                            if j >= 1 and part == 0 and t4 == 0:
                                pe.wait(f"aoev_{j-1}_1")
                            if t4 >= 2:
                                pe.wait(f"qkev_{j}_{part}_{t4-2}")
                            elif j >= 1:
                                pe.wait(f"qkev_{j-1}_{part}_{t4+2}")
                        pe.mark(pe.matmul(pb,
                                          wqk_sb[:, wofs + c * 128:wofs + (c + 1) * 128],
                                          x2f[:, c * T + t4 * 512:c * T + (t4 + 1) * 512],
                                          start=(c == 0), stop=(c == NCH - 1)),
                                f"qkp_{j}_{part}_{t4}" if c == NCH - 1 else None)
                    act.wait(f"qkp_{j}_{part}_{t4}", "qb_ready")
                    dst = qk_sb[:, part * T + t4 * 512:part * T + (t4 + 1) * 512]
                    if part == 0:
                        ins = act.activation(dst, pb, AF.Identity,
                                             bias=qb_sb[:, j:j + 1], scale=0.125)
                    else:
                        ins = act.activation(dst, pb, AF.Identity,
                                             bias=binp_sb[:, 8 + j:9 + j])
                    act.mark(ins, f"qkev_{j}_{part}_{t4}")
            dve.wait(f"qkev_{j}_0_3")
            if j >= 1:
                dve.wait(f"av_{j-1}_1_12")
            dve.mark(dve.scalar_tensor_tensor(exp_sb[:, 0:T], qk_sb[:, 0:T], 1.0,
                                              qk_sb[:, 0:T], ALU.mult, ALU.mult,
                                              accum_out=bacc[:, 0:1]), None)
            dve.wait(f"qkev_{j}_1_3")
            dve.mark(dve.scalar_tensor_tensor(exp_sb[:, 0:T], qk_sb[:, T:2 * T], 1.0,
                                              qk_sb[:, T:2 * T], ALU.mult, ALU.mult,
                                              accum_out=bacc[:, 1:2]), f"bsq_{j}")
            dve.wait_self()
            dve.mark(dve.tensor_copy(bacc_bf[:], bacc[:, 0:2]), f"bacc_{j}")
            pe.wait(f"bacc_{j}", "ld_e2")
            if j >= 1:
                pe.wait(f"ubias_{j-1}", f"srow_{j-1}_1_3", f"aoev_{j-1}_1")
            pe.mark(pe.matmul(AO[0:1, 0:2], bacc_bf[:, 0:1], e2_sb[:],
                              start=True, stop=True), None)
            pe.mark(pe.matmul(AO[0:1, 512:514], bacc_bf[:, 1:2], e2_sb[:],
                              start=True, stop=True), f"bmm_{j}")
            dve.wait(f"bmm_{j}")
            dve.mark(dve.tensor_copy(nrow[:, 8:10], AO[0:1, 512:514]), None)
            dve.wait_self()
            dve.mark(dve.tensor_tensor(nrow[:, 0:2], AO[0:1, 0:2], nrow[:, 8:10],
                                       ALU.mult), f"bm1_{j}")
            act.wait(f"bm1_{j}")
            act.mark(act.activation(nrow[:, 2:4], nrow[:, 0:2], AF.Ln), None)
            act.wait("ld_fc")
            act.wait_self()
            act.mark(act.activation(nrow[:, 4:6], nrow[:, 2:4], AF.Exp, scale=0.5,
                                    bias=fc_sb[:, 1:2]), f"bc_{j}")
            dve.wait(f"bc_{j}")
            dve.mark(dve.tensor_single_scalar(nrow_bf[:, 0:2], nrow[:, 4:6], -1.0,
                                              ALU.mult), f"bneg_{j}")
            pe.wait(f"bneg_{j}")
            pe.mark(pe.matmul(AO[:, 4:6], ones1_sb[:], nrow_bf[:, 0:2],
                              start=True, stop=True), f"bbc_{j}")
            dve.wait(f"bbc_{j}")
            dve.mark(dve.tensor_copy(ubias[:, 2 * j:2 * j + 2], AO[:, 4:6]),
                     f"ubias_{j}")

            for h in range(2):
                hg = 2 * j + h
                po = h * 64
                for C in range(17):
                    for half in range(2):
                        if C >= 16:
                            break
                        piece = (SC_A if (2 * C + half) % 2 == 0 else SC_B)
                        pP = 2 * C + half - 2
                        if pP >= 0:
                            pe.wait(f"exp_{j}_{h}_{pP // 2}_{pP % 2}")
                        elif h == 1:
                            pe.wait(f"exp_{j}_0_15_{half}")
                        elif j >= 1:
                            pe.wait(f"exp_{j-1}_1_15_{half}")
                        for th in range(2):
                            t4 = half * 2 + th
                            pe.mark(pe.matmul(
                                piece[:, th * 512:(th + 1) * 512],
                                qk_sb[po:po + 64, T + C * 128:T + (C + 1) * 128],
                                qk_sb[po:po + 64, t4 * 512:(t4 + 1) * 512],
                                start=True, stop=True),
                                f"sc_{j}_{h}_{C}_{half}" if th == 1 else None)
                        ubcol = 2 * max(j - 1, 0) + h
                        act.wait(f"sc_{j}_{h}_{C}_{half}", f"ubias_{max(j-1, 0)}")
                        if C == 0 and h == 0:
                            act.wait(f"bsq_{j}")
                        if C >= 4 and half == 0:
                            act.wait(f"av_{j}_{h}_{C-4}")
                        elif C < 4 and half == 0 and not (j == 0 and h == 0):
                            act.wait(f"av_{j}_{h-1}_{12+C}" if h == 1
                                     else f"av_{j-1}_1_{12+C}")
                        act.mark(act.activation(
                            exp_sb[:, (C % 4) * T + half * 1024:(C % 4) * T + (half + 1) * 1024],
                            piece[:], AF.Exp, bias=ubias[:, ubcol:ubcol + 1]),
                            f"exp_{j}_{h}_{C}_{half}")
                    if C == 0:
                        continue
                    Ca = C - 1
                    for t4 in range(4):
                        pe.wait(f"exp_{j}_{h}_{Ca}_{t4 // 2}")
                        if Ca == 0 and t4 == 0:
                            pe.wait("v_ones", "vevs_15_1")
                            if h == 0:
                                pe.wait(f"ubias_{j}")
                            if not (j == 0 and h == 0):
                                ph = (j, 0) if h == 1 else (j - 1, 1)
                                pe.wait(f"aoev_{ph[0]}_{ph[1]}",
                                        f"srow_{ph[0]}_{ph[1]}_3")
                        pe.mark(pe.matmul(
                            AO[0:VGRP, t4 * 512:(t4 + 1) * 512],
                            v_sb[:, Ca * 1040 + hg * VGRP:Ca * 1040 + (hg + 1) * VGRP],
                            exp_sb[:, (Ca % 4) * T + t4 * 512:(Ca % 4) * T + (t4 + 1) * 512],
                            start=(Ca == 0), stop=(Ca == 15)),
                            f"av_{j}_{h}_{Ca}" if t4 == 3 else None)
                for t4 in range(4):
                    act.wait(f"av_{j}_{h}_15")
                    act.mark(act.activation(
                        aou[h * 64:(h + 1) * 64, :][:, j * T + t4 * 512:j * T + (t4 + 1) * 512],
                        AO[0:64, t4 * 512:(t4 + 1) * 512], AF.Identity),
                        f"aoev_{j}_{h}" if t4 == 3 else None)
                    dve.wait(f"av_{j}_{h}_15")
                    if t4 > 0 or h > 0 or j > 0:
                        dve.wait_counts("qS2", B.counts.get("qS2", 0))
                    dve.mark(dve.tensor_copy(srow_tmp[:],
                                             AO[64:65, t4 * 512:(t4 + 1) * 512]),
                             f"srow_{j}_{h}_{t4}")
                    qS2.wait(f"srow_{j}_{h}_{t4}")
                    qS2.dma(srows[hg:hg + 1, t4 * 512:(t4 + 1) * 512], srow_tmp[:],
                           f"sdma_{j}_{h}_{t4}")

        act.wait("sdma_7_1_3")
        act.mark(act.activation(srows[:], srows[:], AF.Ln), None)
        act.wait_self()
        act.mark(act.activation(rall[:], srows[:], AF.Exp, scale=-1.0), "rall")
        _prev_rb = None
        for t4 in range(4):
            for j in range(8):
                pe.wait("rall", "ld_selp")
                if _prev_rb is not None:
                    pe.wait(f"rbc_{_prev_rb[0]}_{_prev_rb[1]}")
                else:
                    pe.wait("srow_7_1_3", f"aoev_7_1", f"ubias_7")
                _prev_rb = (j, t4)
                pe.mark(pe.matmul(AO[:, 1024:1536], selp_sb[:, j * 128:(j + 1) * 128],
                                  rall[:, t4 * 512:(t4 + 1) * 512],
                                  start=True, stop=True), f"rbmm_{j}_{t4}")
                dve.wait(f"rbmm_{j}_{t4}", f"aoev_{j}_1")
                dve.mark(dve.tensor_copy(rb_bf, AO[:, 1024:1536]), f"rbc_{j}_{t4}")
                sl = slice(j * T + t4 * 512, j * T + (t4 + 1) * 512)
                dve.mark(dve.tensor_tensor(aou[:, sl], aou[:, sl], rb_bf, ALU.mult),
                         f"aon_{j}_{t4}")
        dump_now("x2fB", x2f, BF16)
        dump_now("vB", v_sb, BF16)
        dump_now("aouB", aou, BF16)
        dump_now("qk7", qk_sb, BF16)
        qS2.wait("qkp_7_1_3")
        qS2.dma(bob, bout_b, "ld_bob")
        for jh in range(2):
            wob = wv_sb if jh == 0 else qk_sb
            for c in range(NCH):
                qV.wait("vp_15_1" if jh == 0 else "av_7_1_15")
                qV.dma(wob[:, c * 512:(c + 1) * 512],
                       wo2_blk[c][:, jh * 512:(jh + 1) * 512], f"ld_wo{jh}_{c}")
            for ct in range(16):
                pb = (SC_A if ct % 2 == 0 else SC_B)[:, 0:512]
                for c in range(NCH):
                    pe.wait(f"aon_{c}_{ct // 4}")
                    if c == 0:
                        pe.wait(f"ld_wo{jh}_{NCH-1}")
                        if jh == 0 and ct < 2:
                            pe.wait(f"exp_7_1_15_{ct}")
                        if ct >= 2:
                            pe.wait(f"opev_{ct-2}_{jh}")
                        elif jh == 1:
                            pe.wait(f"opev_{14+ct}_0")
                    pe.mark(pe.matmul(pb, aou[:, c * T + ct * 128:c * T + (ct + 1) * 128],
                                      wob[:, c * 512:(c + 1) * 512],
                                      start=(c == 0), stop=(c == NCH - 1)),
                            f"opp_{ct}_{jh}" if c == NCH - 1 else None)
                dve.wait(f"opp_{ct}_{jh}", "ld_bob")
                dve.mark(dve.tensor_tensor(
                    s_sb[:, ct * D + jh * 512:ct * D + (jh + 1) * 512],
                    pb, bob[:, jh * 512:(jh + 1) * 512], ALU.add),
                    f"opev_{ct}_{jh}")

        sync_all()
        dump_now("sB2", s_sb[:], BF16)
        NPF = 4

        def ffn_wload(q, ht):
            li = q * NHC + ht
            if li >= 4 * NHC:
                return
            if li >= NPF:
                pq, pt = divmod(li - NPF, NHC)
                qF.wait(f"gg_d_{pq}_{pt}", f"uu_d_{pq}_{pt}")
            qF.dma(wg_sb[ht % NPF], wg_blk[ht], f"ld_wg_{q}_{ht}")
            qF.dma(wu_sb[ht % NPF], wu_blk[ht], f"ld_wu_{q}_{ht}")

        gq.dma(w02_sb.rearrange("p (h c f) -> p h c f", h=NL, c=2), w02T.rearrange("h c p f -> p h c f"), "ld_w02")
        gq.dma(w1r_sb.rearrange("p (h c f) -> p h c f", h=NL, c=2), w1r.rearrange("h c p f -> p h c f"), "ld_w1r")
        for li in range(NPF):
            ffn_wload(li // NHC, li % NHC)
        sq.dma(w0n_sb.rearrange("p (h c f) -> p h c f", h=NL, c=2), w0nat.rearrange("h c p f -> p h c f"), "ld_w0n")
        sq.dma(w1n_sb.rearrange("p (h c f) -> p h c f", h=NL, c=2), w1nat.rearrange("h c p f -> p h c f"), "ld_w1n")
        sq.dma(w2n_sb.rearrange("p (h c f) -> p h c f", h=NL, c=2), w2nat.rearrange("h c p f -> p h c f"), "ld_w2n")

        for h in range(NL):
            hb = h % 2
            for dc in range(2):
                gc = 2 * h + dc
                for ct in range(16):
                    pe.wait("ld_id")
                    if h >= 2:
                        pe.wait(f"lact_done_{h-2}")
                    xti = dc * 16 + ct
                    xts = SC_A[0:128, 0:64] if xti % 2 == 0 else SC_A[0:128, 512:576]
                    if xti >= 2:
                        pe.wait(f"xtev_{h}_{(xti - 2) // 16}_{(xti - 2) % 16}")
                    elif xti == 0 and h >= 1:
                        pe.wait(f"z_1_{h-1}_3")
                    pe.mark(pe.transpose(xts.bitcast(BF16),
                                         x2f[:, gc * T + ct * 128:gc * T + (ct + 1) * 128],
                                         id_sb[:]), f"xtp_{h}_{dc}_{ct}")
                    dve.wait(f"xtp_{h}_{dc}_{ct}")
                    dve.mark(dve.tensor_copy(
                        xt_sb[hb][:, ct * 256 + dc * 128:ct * 256 + (dc + 1) * 128],
                        xts.bitcast(BF16)), f"xtev_{h}_{dc}_{ct}")
            if h == 0:
                dump_now("xt0", xt_sb[0], BF16)
            for n in range(2):
                nh = f"{n}_{h}"
                def emit_dw(ct):
                    gt = n * 8 + ct
                    gtt = slice(gt * 256, (gt + 1) * 256)
                    pe.wait(f"dvt_{nh}_{ct}", f"le_{nh}_{ct}", f"xtev_{h}_1_15")
                    if ct == 0 and prevnh is not None:
                        pe.wait(f"oqev_{prevnh}_7")
                    for ds in range(2):
                        pe.mark(pe.matmul(AO[:, ds * 512:(ds + 1) * 512],
                                          xt_sb[hb][:, gt * 256 + ds * 128:gt * 256 + (ds + 1) * 128],
                                          pg_t[:, ct * 512:(ct + 1) * 512],
                                          start=(ct == 0), stop=(ct == 7)),
                                f"dw02_{nh}_{ds}" if ct == 7 else None)
                    for es in range(2):
                        pe.mark(pe.matmul(AO[:, 1024 + es * 512:1024 + es * 512 + 256],
                                          hid_t[:, ct * 256 + es * 128:ct * 256 + (es + 1) * 128],
                                          xt_sb[hb][:, gtt],
                                          start=(ct == 0), stop=(ct == 7)),
                                f"dw1_{nh}_{es}" if ct == 7 else None)

                for ct in range(8):
                    cp = ct % 2
                    tok = n * 1024 + ct * 128
                    ggp = SC_A[:, cp * 512:(cp + 1) * 512]
                    mp = SC_B[:, cp * 512:cp * 512 + 256]
                    prevnh = None if (n == 0 and h == 0) else (f"1_{h-1}" if n == 0 else f"0_{h}")
                    for dc in range(2):
                        gc = 2 * h + dc
                        if dc == 0:
                            pe.wait("ld_w02", "ld_w1r")
                            if ct >= 2:
                                pe.wait(f"dvele_{nh}_{ct-2}")
                            else:
                                pe.wait(f"xtev_{h}_1_15")
                                if prevnh is not None:
                                    pe.wait(f"z_{prevnh}_3")
                        pe.mark(pe.matmul(ggp,
                                          x2f[:, gc * T + tok:gc * T + tok + 128],
                                          w02_sb[:, (h * 2 + dc) * 512:(h * 2 + dc + 1) * 512],
                                          start=(dc == 0), stop=(dc == 1)),
                                f"gg_{nh}_{ct}" if dc == 1 else None)
                        pe.mark(pe.matmul(mp,
                                          x2f[:, gc * T + tok:gc * T + tok + 128],
                                          w1r_sb[:, (h * 2 + dc) * 256:(h * 2 + dc + 1) * 256],
                                          start=(dc == 0), stop=(dc == 1)),
                                f"m_{nh}_{ct}" if dc == 1 else None)
                    if ct >= 1:
                        emit_dw(ct - 1)
                    ctt = slice(ct * 256, (ct + 1) * 256)
                    pool_prev = (f"le_{nh}_{ct-2}" if ct >= 2 else
                                 (f"le_{prevnh}_{6+ct}" if prevnh is not None else None))
                    dvt_prev = (f"dvt_{nh}_{ct-2}" if ct >= 2 else
                                (f"dvt_{prevnh}_{6+ct}" if prevnh is not None else None))
                    act.wait(f"gg_{nh}_{ct}")
                    act.wait(pool_prev)
                    act.mark(act.activation(sg_f[cp], ggp[:, 0:256], AF.Sigmoid),
                             f"sg_{nh}_{ct}")
                    if ct >= 1:
                        ccp = (ct - 1) % 2
                        dve.wait(f"le_{nh}_{ct-1}")
                        dve.mark(dve.tensor_tensor(
                            pg_t[:, (ct - 1) * 512:(ct - 1) * 512 + 256],
                            dh_f[ccp], u_f[ccp], ALU.mult), f"dvt_{nh}_{ct-1}")
                    dve.wait(f"sg_{nh}_{ct}")
                    dve.wait(pool_prev, dvt_prev)
                    dve.mark(dve.tensor_tensor(hh_f[cp], ggp[:, 0:256], sg_f[cp],
                                               ALU.mult), None)
                    dve.mark(dve.tensor_copy(g_f[cp], ggp[:, 256:512]), None)
                    dve.wait(f"m_{nh}_{ct}")
                    dve.mark(dve.tensor_tensor(pg_t[:, ct * 512 + 256:(ct + 1) * 512],
                                               mp, hh_f[cp],
                                               ALU.mult), None)
                    dve.mark(dve.tensor_tensor(dh_f[cp], mp, g_f[cp],
                                               ALU.mult), None)
                    dve.mark(dve.scalar_tensor_tensor(v2_f[cp], ggp[:, 0:256], 1.0,
                                                      hh_f[cp], ALU.add, ALU.subtract),
                             f"dvele_{nh}_{ct}")
                    pool.wait(f"dvele_{nh}_{ct}", dvt_prev)
                    if ct < 2 and prevnh is not None:
                        pool.wait(f"dw1_{prevnh}_1")
                    pool.mark(pool.tensor_tensor(hid_t[:, ctt], hh_f[cp], g_f[cp],
                                                 ALU.mult), None)
                    pool.mark(pool.tensor_tensor(u_f[cp], sg_f[cp], v2_f[cp], ALU.mult),
                              f"le_{nh}_{ct}")
                    if ct == 7:
                        dve.wait(f"le_{nh}_7")
                        dve.mark(dve.tensor_tensor(pg_t[:, 7 * 512:7 * 512 + 256],
                                                   dh_f[cp], u_f[cp],
                                                   ALU.mult), f"dvt_{nh}_7")

                emit_dw(7)
                if n == 0 and h == 0:
                    dump_now("hid0", hid_t, BF16)
                    dump_now("pg0", pg_t, BF16)
                    dump_now("wnT0", wnT_bf, BF16)
                upd = [("dw02", 0, w0n_sb, 0), ("dw1", 1024 + 512, w1n_sb, 512),
                       ("dw02", 256, w2n_sb, 1024)]
                for k, (nm, po_, nat, wofs) in enumerate(upd):
                    for cs in range(2):
                        dve.wait(f"{nm}_{nh}_{cs}", "ld_w2n")
                        psl = (AO[:, cs * 512 + po_:cs * 512 + po_ + 256] if nm == "dw02"
                               else AO[:, 1024 + cs * 512:1024 + cs * 512 + 256])
                        dve.mark(dve.tensor_tensor(
                            wnT_bf[:, wofs + cs * 256:wofs + (cs + 1) * 256],
                            psl,
                            nat[:, (h * 2 + cs) * 256:(h * 2 + cs + 1) * 256],
                            ALU.add), f"wnev_{nh}_{2 * k + cs}")
                for ec in range(2):
                    for c2 in range(2):
                        cp = (ec * 2 + c2) % 2
                        gqp = SC_A[:, cp * 512:(cp + 1) * 512]
                        g2p = SC_B[:, cp * 512:(cp + 1) * 512]
                        csl = slice(n * 1024 + c2 * 512, n * 1024 + (c2 + 1) * 512)
                        for ds in range(2):
                            gc = 2 * h + ds
                            if ds == 0:
                                pe.wait(f"wnev_{nh}_5")
                                if ec * 2 + c2 >= 2:
                                    pe.wait(f"z_{nh}_{ec * 2 + c2 - 2}")
                            pe.mark(pe.matmul(gqp,
                                              wnT_bf[:, 0 + ds * 256 + ec * 128:0 + ds * 256 + (ec + 1) * 128],
                                              x2f[:, gc * T + csl.start:gc * T + csl.stop],
                                              start=(ds == 0), stop=(ds == 1)),
                                    f"gq_{nh}_{ec}_{c2}" if ds == 1 else None)
                            pe.mark(pe.matmul(g2p,
                                              wnT_bf[:, 1024 + ds * 256 + ec * 128:1024 + ds * 256 + (ec + 1) * 128],
                                              x2f[:, gc * T + csl.start:gc * T + csl.stop],
                                              start=(ds == 0), stop=(ds == 1)),
                                    f"gq2_{nh}_{ec}_{c2}" if ds == 1 else None)
                        act.wait(f"gq_{nh}_{ec}_{c2}")
                        act.mark(act.activation(hq_f[cp], gqp, AF.Sigmoid), f"hqs_{nh}_{ec}_{c2}")
                        dve.wait(f"hqs_{nh}_{ec}_{c2}")
                        dve.mark(dve.tensor_tensor(hq_f[cp], gqp, hq_f[cp], ALU.mult),
                                 None)
                        dve.wait(f"gq2_{nh}_{ec}_{c2}")
                        dve.mark(dve.tensor_tensor(
                            z_sb[:, ec * 1024 + c2 * 512:ec * 1024 + (c2 + 1) * 512],
                            hq_f[cp], g2p, ALU.mult), f"z_{nh}_{ec * 2 + c2}")
                if n == 0 and h == 0:
                    dump_now("z0", z_sb, BF16)
                for ct in range(8):
                    op = AO[:, (ct % 4) * 512:(ct % 4) * 512 + 256]
                    for ec in range(2):
                        if ec == 0:
                            pe.wait(f"z_{nh}_{ct // 4}", f"z_{nh}_{2 + ct // 4}")
                            if ct >= 4:
                                pe.wait(f"oqev_{nh}_{ct-4}")
                            elif not (n == 0 and h == 0):
                                pn = f"1_{h-1}" if n == 0 else f"0_{h}"
                                pe.wait(f"oqev_{pn}_{4 + ct}")
                        pe.mark(pe.matmul(op,
                                          z_sb[:, ec * 1024 + ct * 128:ec * 1024 + (ct + 1) * 128],
                                          wnT_bf[:, 512 + ec * 256:512 + (ec + 1) * 256],
                                          start=(ec == 0), stop=(ec == 1)),
                                f"oq_{nh}_{ct}" if ec == 1 else None)
                    dve.wait(f"oq_{nh}_{ct}", f"opev_{n * 8 + ct}_1")
                    ssl = slice((n * 8 + ct) * D + h * 256, (n * 8 + ct) * D + h * 256 + 256)
                    dve.mark(dve.tensor_tensor(s_sb[:, ssl], op, s_sb[:, ssl], ALU.add),
                             f"oqev_{nh}_{ct}")
            if True:
                alias(f"lact_done_{h}", f"oqev_1_{h}_7")

        sync_all()
        for hc in range(NHC):
            qD.dma(wd_sb[:, hc * D:(hc + 1) * D], wdT_blk[hc], f"ld_wd{hc}")
        st_count = [0]
        for q in range(4):
            xb = xnf_d[q % 2]
            for c in range(NCH):
                qN.wait(f"spill_{NCH-1}")
                if q > 1:
                    qN.wait(f"uu_d_{q-2}_{NHC-1}")
                qN.dma(xb[:, c * 512:(c + 1) * 512],
                       xn_spill[:, c * T + q * 512:c * T + (q + 1) * 512],
                       f"ld_xn_{q}_{c}")
            for ht in range(NHC):
                li = q * NHC + ht
                ffn_wload((li + NPF) // NHC, (li + NPF) % NHC)
                p = ht % 2
                gp = SC_A[:, p * 512:(p + 1) * 512]
                up = SC_B[:, p * 512:(p + 1) * 512]
                for c in range(NCH):
                    if c == 0:
                        pe.wait(f"ld_wg_{q}_{ht}", f"ld_wu_{q}_{ht}", f"ld_xn_{q}_{NCH-1}")
                        if ht >= 2:
                            pe.wait(f"guev_{q}_{ht-2}_1")
                        elif q >= 1:
                            pe.wait(f"guev_{q-1}_{NHC-2+ht}_1")
                    pe.mark(pe.matmul(gp, wg_sb[ht % NPF][:, c * 128:(c + 1) * 128],
                                      xb[:, c * 512:(c + 1) * 512],
                                      start=(c == 0), stop=(c == NCH - 1)),
                            f"gg_d_{q}_{ht}" if c == NCH - 1 else None)
                    pe.mark(pe.matmul(up, wu_sb[ht % NPF][:, c * 128:(c + 1) * 128],
                                      xb[:, c * 512:(c + 1) * 512],
                                      start=(c == 0), stop=(c == NCH - 1)),
                            f"uu_d_{q}_{ht}" if c == NCH - 1 else None)
                act.wait(f"gg_d_{q}_{ht}")
                act.mark(act.activation(silu_f, gp, AF.Silu, scale=S127),
                         f"guev_{q}_{ht}_0")
                dve.wait(f"guev_{q}_{ht}_0", f"uu_d_{q}_{ht}")
                dve.mark(dve.scalar_tensor_tensor(
                    ht_sb[:, ht * 512:(ht + 1) * 512], up, S127, silu_f,
                    ALU.mult, ALU.mult), f"guev_{q}_{ht}_1")
            for tt in range(4):
                for dh2 in range(2):
                    ti = tt * 2 + dh2
                    p2 = ti % 2
                    tok0 = q * 512 + tt * 128
                    dp = AO[:, (ti % 4) * 512:(ti % 4) * 512 + 512]
                    qX.wait(f"ob_{q}_{ti-4}" if ti >= 4 else
                            (f"ob_{q-1}_{4+ti}" if q >= 1 else None))
                    qX.dma(x_f[ti % 4], x_t[tok0:tok0 + 128, dh2 * 512:(dh2 + 1) * 512],
                           f"ld_x_{q}_{ti}")
                    for hc in range(NHC):
                        pe.wait(f"guev_{q}_{hc}_1")
                        if hc == 0:
                            pe.wait(f"ld_wd{NHC-1}")
                            if ti >= 4:
                                pe.wait(f"dnev_{q}_{ti-4}")
                            elif q >= 1:
                                pe.wait(f"dnev_{q-1}_{4+ti}")
                        pe.mark(pe.matmul(dp, ht_sb[:, hc * 512 + tt * 128:hc * 512 + (tt + 1) * 128],
                                          wd_sb[:, hc * D + dh2 * 512:hc * D + (dh2 + 1) * 512],
                                          start=(hc == 0), stop=(hc == NHC - 1)),
                                f"dn_{q}_{ti}" if hc == NHC - 1 else None)
                    act.wait(f"dn_{q}_{ti}")
                    if ti >= 2:
                        act.wait(f"dnev_{q}_{ti-2}")
                    elif q >= 1:
                        act.wait(f"dnev_{q-1}_{6+ti}")
                    act.mark(act.activation(sig_f[p2], dp, AF.Sigmoid, scale=S127),
                             f"dsig_{q}_{ti}")
                    dve.wait(f"dsig_{q}_{ti}")
                    dve.mark(dve.scalar_tensor_tensor(gg_f[p2], dp, S127, sig_f[p2],
                                                      ALU.mult, ALU.mult),
                             f"dnev_{q}_{ti}")
                    ssl = slice((q * 4 + tt) * D + dh2 * 512, (q * 4 + tt) * D + (dh2 + 1) * 512)
                    dve.mark(dve.tensor_tensor(sgp_f[p2], s_sb[:, ssl], gg_f[p2],
                                               ALU.mult), None)
                    dve.wait(f"ld_x_{q}_{ti}")
                    if ti >= 2:
                        dve.wait(f"st_{q}_{ti-2}")
                    elif q >= 1:
                        dve.wait(f"st_{q-1}_{6+ti}")
                    dve.mark(dve.tensor_tensor(outb[p2], sgp_f[p2], x_f[ti % 4],
                                               ALU.add), f"ob_{q}_{ti}")
                    qO.wait(f"ob_{q}_{ti}")
                    qO.dma(out[tok0:tok0 + 128, dh2 * 512:(dh2 + 1) * 512], outb[p2],
                           f"st_{q}_{ti}")
                    st_count[0] += 1

        if not B.dry:
            B.engines["sq"].wait_ge(B.sems["qO"], B.counts["qO"])

        if KDBG:
            dump_map = {
                "x2f": (x2f, BF16), "aou": (aou, BF16), "s_sb": (s_sb[:], BF16),
                "v_sb": (v_sb, BF16), "srows": (srows[:], F32), "rall": (rall[:], BF16),
            }
            for nm in KDBG.split(","):
                ap, dt = dump_map[nm]
                if B.dry:
                    dtn = nc.dram_tensor("dbg_" + nm, list(ap.shape), dt,
                                         kind="ExternalOutput").ap()
                    B.marks["dbgt_" + nm] = dtn
                dtn = B.marks["dbgt_" + nm]
                sq.dma(dtn, ap, None)
            if not B.dry:
                B.engines["sq"].wait_ge(B.sems["sq"], B.counts["sq"])
            else:
                pass

    program()
    B.dry = False
    B.counts = {}
    B.hiwater = {}
    with nc.Block() as blk:
        @blk.sync
        def _(s):
            gx, sx = nc.gpsimd, nc.sync
            B.engines = {"pe": nc.tensor, "act": nc.scalar, "dve": nc.vector,
                         "gq": gx, "sq": sx, "qC": gx, "qW": gx, "qV": gx,
                         "qF": gx, "qD": gx, "qS2": sx, "qN": sx, "qX": sx,
                         "qO": sx, "qP": gx}
            program()
    return nc


def _blockT(w):
    o, i = w.shape
    return np.ascontiguousarray(
        w.reshape(o // 128, 128, i // 128, 128).transpose(0, 3, 2, 1).reshape(o // 128, 128, i))


_NC_CACHE = [None]

def kernel(**inp):
    f32 = np.float32
    xs = np.asarray(inp["L0_attn_x"], f32)
    ipw = np.asarray(inp["L0_attn_in_proj_w"], f32)
    ipb = np.asarray(inp["L0_attn_in_proj_b"], f32)
    opw = np.asarray(inp["L0_attn_out_proj_w"], f32)
    opb = np.asarray(inp["L0_attn_out_proj_b"], f32)
    lnw = np.asarray(inp["L0_attn_ln_w"], f32)
    lnb = np.asarray(inp["L0_attn_ln_b"], f32)
    w0 = np.asarray(inp["L0_attn_w0"], f32)
    w1 = np.asarray(inp["L0_attn_w1"], f32)
    w2 = np.asarray(inp["L0_attn_w2"], f32)
    wg = np.ascontiguousarray(np.asarray(inp["L0_ffn_w_gate"], np.int32))
    wu = np.ascontiguousarray(np.asarray(inp["L0_ffn_w_up"], np.int32))
    wd = np.ascontiguousarray(np.asarray(inp["L0_ffn_w_down"], np.int32))

    shared = {
        "winp": _blockT(ipw),
        "binp": np.ascontiguousarray(ipb.reshape(24, 128).T),
        "wout": _blockT(opw),
        "wv2_blk": np.ascontiguousarray(ipw[2048:].T.reshape(8, 128, D)),
        "wo2_blk": np.ascontiguousarray(opw.T.reshape(8, 128, D)),
        "bout_b": np.ascontiguousarray(np.broadcast_to(opb, (128, D))),
        "bv_b": np.ascontiguousarray(np.broadcast_to(ipb[2048:], (128, D))),
        "lnw_c": np.ascontiguousarray(lnw.reshape(8, 128).T),
        "lnb_c": np.ascontiguousarray(lnb.reshape(8, 128).T),
        "w02T": np.ascontiguousarray(np.concatenate(
            [np.stack([w0[h].T.reshape(2, 128, 256) for h in range(NL)]),
             np.stack([w2[h].T.reshape(2, 128, 256) for h in range(NL)])], axis=3)),
        "w1r": np.ascontiguousarray(w1.reshape(NL, 2, 128, 256)),
        "w0nat": np.ascontiguousarray(
            np.stack([w0[hh].T.reshape(2, 128, 256) for hh in range(NL)])),
        "w1nat": np.ascontiguousarray(
            np.stack([w1[hh].T.reshape(2, 128, 256) for hh in range(NL)])),
        "w2nat": np.ascontiguousarray(
            np.stack([w2[hh].T.reshape(2, 128, 256) for hh in range(NL)])),
        "wg_blk": _blockT(wg),
        "wu_blk": _blockT(wu),
        "wdT_blk": np.ascontiguousarray(wd.T.reshape(24, 128, D)),
        "id128": np.eye(128, dtype=f32),
        "e2c": np.ascontiguousarray(
            np.stack([(np.arange(128) < 64).astype(f32),
                      (np.arange(128) >= 64).astype(f32)], axis=1)),
        "ones1": np.ones((1, 128), f32),
        "selp": np.ascontiguousarray(np.stack([
            np.stack([((np.arange(128) < 64) if hh == 2 * jj else
                       ((np.arange(128) >= 64) if hh == 2 * jj + 1 else
                        np.zeros(128, bool))).astype(f32)
                      for hh in range(16)])
            for jj in range(8)])),
        "onescol": np.ones((128, 1), f32),
        "fconst": np.array([[EPS, np.log(KAPPA / T)]], f32),
    }
    if _NC_CACHE[0] is None:
        _NC_CACHE[0] = build_nc()
    nc = _NC_CACHE[0]
    in_maps = []
    for b in range(8):
        d = dict(shared)
        d["x_t"] = np.ascontiguousarray(xs[b])
        d["xT_blk"] = np.ascontiguousarray(xs[b].T.reshape(NCH, 128, T))
        in_maps.append(d)
    res = run_bass_kernel_spmd(nc, in_maps, core_ids=list(range(8)))
    return np.stack([r["out"] for r in res.results])


if __name__ == "__main__":
    nc = build_nc()
    print("build ok; sbuf remaining:", nc.sbuf_bytes_remaining,
          "psum banks remaining:", nc.psum_banks_remaining)


# version 23
# speedup vs baseline: 1.0011x; 1.0011x over previous
import os
import numpy as np
import ml_dtypes
import concourse.bass as bass
import concourse.mybir as mybir
from concourse.bass_utils import run_bass_kernel_spmd

F32 = mybir.dt.float32
BF16 = mybir.dt.bfloat16
I32 = mybir.dt.int32
AF = mybir.ActivationFunctionType
ALU = mybir.AluOpType

T = 2048
D = 1024
NCH = 8
NH = 16
HD = 64
NL = 4
HID = 3072
NHC = 24
EPS = 1e-5
S127 = 1.0 / 127.0
KAPPA = 2.0
VGRP = 65

KDBG = os.environ.get("KDBG", "")

class _Dummy:
    def then_inc(self, *a, **k):
        return self

_DUMMY = _Dummy()

class Builder:
    def __init__(self):
        self.dry = True
        self.counts = {}
        self.marks = {}
        self.hiwater = {}
        self.engines = {}
        self.sems = {}

class Prox:
    def __init__(self, B, key):
        self._B = B
        self._key = key

    def __getattr__(self, name):
        B = self._B
        key = self._key
        def call(*a, **k):
            if B.dry:
                return _DUMMY
            return getattr(B.engines[key], name)(*a, **k)
        return call

    def mark(self, ins, name=None):
        B = self._B
        key = self._key
        B.counts[key] = B.counts.get(key, 0) + 1
        if name is not None:
            if B.dry:
                assert name not in B.marks, f"dup mark {name}"
                B.marks[name] = (key, B.counts[key])
            else:
                assert B.marks[name] == (key, B.counts[key]), f"mark drift {name}"
        if not B.dry:
            ins.then_inc(B.sems[key], 1)

    def dma(self, out, in_, name=None, **kw):
        B = self._B
        key = self._key
        B.counts[key] = B.counts.get(key, 0) + 16
        if name is not None:
            if B.dry:
                assert name not in B.marks, f"dup mark {name}"
                B.marks[name] = (key, B.counts[key])
            else:
                assert B.marks[name] == (key, B.counts[key]), f"mark drift {name}"
        if not B.dry:
            B.engines[key].dma_start(out=out, in_=in_, **kw).then_inc(B.sems[key], 16)

    def wait(self, *names):
        B = self._B
        for nm in names:
            if nm is None:
                continue
            if B.dry:
                continue
            skey, val = B.marks[nm]
            self.wait_counts(skey, val)

    def wait_counts(self, skey, val):
        B = self._B
        if B.dry or val <= 0:
            return
        if skey == self._key:
            return
        hw = B.hiwater.get((self._key, skey), 0)
        if val <= hw:
            return
        B.hiwater[(self._key, skey)] = val
        B.engines[self._key].wait_ge(B.sems[skey], val)

    def wait_self(self):
        B = self._B
        if B.dry:
            return
        B.engines[self._key].wait_ge(B.sems[self._key], B.counts.get(self._key, 0))


def build_nc():
    nc = bass.Bass("TRN2", target_bir_lowering=False)

    DT = nc.dram_tensor
    x_t = DT("x_t", [T, D], F32, kind="ExternalInput").ap()
    xT_blk = DT("xT_blk", [NCH, 128, T], F32, kind="ExternalInput").ap()
    winp = DT("winp", [24, 128, D], F32, kind="ExternalInput").ap()
    binp = DT("binp", [128, 24], F32, kind="ExternalInput").ap()
    wout = DT("wout", [8, 128, D], F32, kind="ExternalInput").ap()
    wv2_blk = DT("wv2_blk", [8, 128, D], F32, kind="ExternalInput").ap()
    wo2_blk = DT("wo2_blk", [8, 128, D], F32, kind="ExternalInput").ap()
    bout_b = DT("bout_b", [128, D], F32, kind="ExternalInput").ap()
    bv_b = DT("bv_b", [128, D], F32, kind="ExternalInput").ap()
    lnw_c = DT("lnw_c", [128, NCH], F32, kind="ExternalInput").ap()
    lnb_c = DT("lnb_c", [128, NCH], F32, kind="ExternalInput").ap()
    w02T = DT("w02T", [NL, 2, 128, 512], F32, kind="ExternalInput").ap()
    w1r = DT("w1r", [NL, 2, 128, 256], F32, kind="ExternalInput").ap()
    w0nat = DT("w0nat", [NL, 2, 128, 256], F32, kind="ExternalInput").ap()
    w1nat = DT("w1nat", [NL, 2, 128, 256], F32, kind="ExternalInput").ap()
    w2nat = DT("w2nat", [NL, 2, 128, 256], F32, kind="ExternalInput").ap()
    wg_blk = DT("wg_blk", [NHC, 128, D], I32, kind="ExternalInput").ap()
    wu_blk = DT("wu_blk", [NHC, 128, D], I32, kind="ExternalInput").ap()
    wdT_blk = DT("wdT_blk", [NHC, 128, D], I32, kind="ExternalInput").ap()
    id128 = DT("id128", [128, 128], F32, kind="ExternalInput").ap()
    e2c = DT("e2c", [128, 2], F32, kind="ExternalInput").ap()
    ones1 = DT("ones1", [1, 128], F32, kind="ExternalInput").ap()
    selp = DT("selp", [8, 16, 128], F32, kind="ExternalInput").ap()
    onescol = DT("onescol", [128, 1], F32, kind="ExternalInput").ap()
    fconst = DT("fconst", [1, 2], F32, kind="ExternalInput").ap()

    out = DT("out", [T, D], F32, kind="ExternalOutput").ap()
    xn_spill = DT("xn_spill", [128, NCH * T], BF16, kind="Internal").ap()

    S = nc.alloc_sbuf_tensor
    id_sb = S("id_sb", [128, 128], BF16)
    e2_sb = S("e2_sb", [128, 2], BF16)
    ones1_sb = S("ones1_sb", [1, 128], BF16)
    onescol_sb = S("onescol_sb", [128, 1], BF16)
    selp_sb = S("selp_sb", [16, 8 * 128], BF16)
    binp_sb = S("binp_sb", [128, 24], F32)
    qb_sb = S("qb_sb", [128, 8], F32)
    lnw_sb = S("lnw_sb", [128, NCH], F32)
    lnb_sb = S("lnb_sb", [128, NCH], F32)
    bacc = S("bacc", [128, 4], F32)
    bacc_bf = S("bacc_bf", [128, 2], BF16)
    nrow = S("nrow", [1, 16], F32)
    nrow_bf = S("nrow_bf", [1, 16], BF16)
    srow_tmp = S("srow_tmp", [1, 512], F32)
    srows = S("srows", [16, 2048], F32)
    rall = S("rall", [16, 2048], BF16)
    fc_sb = S("fc_sb", [1, 2], F32)

    s_sb = S("s_sb", [128, 16 * D], BF16)

    AR_N = 75008
    arena = S("arena", [128, AR_N], BF16)

    class Carver:
        def __init__(self):
            self.off = 0
        def take(self, n, name=""):
            n = (n + 15) & ~15
            ap = arena[:, self.off:self.off + n]
            self.off += n
            assert self.off <= AR_N, f"arena overflow at {name}: {self.off}"
            return ap
        def take_f32(self, n, name=""):
            return self.take(2 * n, name).bitcast(F32)

    cv = Carver()
    off0 = cv.off
    x2f = cv.take(NCH * T, "x2f")
    offX = cv.off

    cv.off = offX
    xbf = cv.take(NCH * T, "xbf")
    xnf_a = cv.take(NCH * T, "xnf_a")
    sqs = cv.take(T, "sqs")
    tmpA = cv.take_f32(512, "tmpA")
    rows = cv.take(3 * 2048, "rows")
    offA = cv.off

    cv.off = offX
    v_sb = cv.take(16 * 16 * VGRP, "v_sb")
    aou = cv.take(NCH * T, "aou")
    qk_sb = cv.take(2 * T, "qk")
    exp_sb = cv.take(4 * T, "exp")
    ubias = cv.take_f32(16, "ubias")
    rb_bf = cv.take(512, "rb")
    cv.off = max(cv.off, offA)
    bvb = cv.take_f32(D, "bvb")
    wv_sb = cv.take(NCH * 512, "wv")
    wqk_sb = cv.take(2 * 2048, "wqk")
    offB = cv.off
    assert offB <= AR_N

    cv.off = offX
    xt_sb = [cv.take(16 * 256, f"xt{i}") for i in range(2)]
    w02_sb = cv.take(NL * 2 * 512, "w02")
    w1r_sb = cv.take(NL * 2 * 256, "w1r")
    w0n_sb = cv.take_f32(NL * 2 * 256, "w0n")
    w1n_sb = cv.take_f32(NL * 2 * 256, "w1n")
    w2n_sb = cv.take_f32(NL * 2 * 256, "w2n")
    hid_t = cv.take(8 * 256, "hid_t")
    pg_t = cv.take(8 * 512, "pg_t")
    sg_f = [cv.take_f32(256, f"sg{i}") for i in range(2)]
    hh_f = [cv.take_f32(256, f"hh{i}") for i in range(2)]
    g_f = [cv.take_f32(256, f"g{i}") for i in range(2)]
    dh_f = [cv.take_f32(256, f"dh{i}") for i in range(2)]
    v2_f = [cv.take_f32(256, f"v2{i}") for i in range(2)]
    u_f = [cv.take_f32(256, f"u{i}") for i in range(2)]
    wn_bf = cv.take(3 * 2 * 256, "wn_bf")
    wnT_bf = cv.take(3 * 2 * 256, "wnT_bf")
    hq_f = [cv.take_f32(512, f"hq{i}") for i in range(2)]
    z_sb = cv.take(2 * 1024, "z_sb")
    offC = cv.off

    cv.off = off0
    ht_sb = cv.take(NHC * 512, "ht")
    wd_sb = cv.take(NHC * D, "wd")
    xnf_d = [cv.take(NCH * 512, f"xnf_d{i}") for i in range(2)]
    silu_f = cv.take_f32(512, "silu")
    sig_f = [cv.take_f32(512, f"sig{i}") for i in range(2)]
    gg_f = [cv.take_f32(512, f"gg{i}") for i in range(2)]
    sgp_f = [cv.take_f32(512, f"sgp{i}") for i in range(2)]
    x_f = [cv.take_f32(512, f"xf{i}") for i in range(4)]
    outb = [cv.take_f32(512, f"ob{i}") for i in range(2)]
    offD = cv.off
    cv.off = max(offC, offD)
    wg_sb = [cv.take(D, f"wg{i}") for i in range(4)]
    wu_sb = [cv.take(D, f"wu{i}") for i in range(4)]
    assert max(offA, offB, offC, offD, cv.off) <= AR_N

    bob = wqk_sb[:, 0:2048].bitcast(F32)

    P = nc.alloc_psum_tensor
    SC_A = P("SC_A", [128, 1024], F32)
    SC_B = P("SC_B", [128, 1024], F32)
    AO = P("AO", [128, 2048], F32)

    B = Builder()
    STREAMS = ("pe", "act", "dve", "gq", "sq",
               "qC", "qW", "qV", "qF", "qD",
               "qS2", "qN", "qX", "qO",
               "qP")
    for key in STREAMS:
        B.sems[key] = nc.alloc_semaphore("sem_" + key)
    pe = Prox(B, "pe")
    act = Prox(B, "act")
    dve = Prox(B, "dve")
    gq = Prox(B, "gq")
    sq = Prox(B, "sq")
    qC = Prox(B, "qC")
    qW = Prox(B, "qW")
    qV = Prox(B, "qV")
    qF = Prox(B, "qF")
    qD = Prox(B, "qD")
    qS2 = Prox(B, "qS2")
    qN = Prox(B, "qN")
    qX = Prox(B, "qX")
    qO = Prox(B, "qO")
    pool = Prox(B, "qP")
    ALL = (pe, act, dve, gq, sq, qC, qW, qV, qF, qD, qS2, qN, qX, qO, pool)

    def alias(dst, src):
        if B.dry:
            assert dst not in B.marks, f"dup alias {dst}"
            B.marks[dst] = B.marks[src]
        else:
            assert B.marks[dst] == B.marks[src]

    def sync_all(engines=ALL):
        snap = dict(B.counts)
        for e in engines:
            for k in ("pe", "act", "dve", "gq", "sq"):
                if k != e._key and snap.get(k):
                    e.wait_counts(k, snap[k])

    KDBG2 = os.environ.get("KDBG2", "")

    def dump_now(name, ap, dt):
        if name not in KDBG2.split(","):
            return
        key = "dbg2t_" + name
        if B.dry:
            B.marks[key] = nc.dram_tensor("dbg2_" + name, list(ap.shape), dt,
                                          kind="ExternalOutput").ap()
        snap = dict(B.counts)
        for k in ("pe", "act", "dve", "gq"):
            if snap.get(k):
                sq.wait_counts(k, snap[k])
        sq.dma(B.marks[key], ap, None)

    def program():
        qC.dma(id_sb[:], id128, "ld_id")
        sq.dma(fc_sb[:], fconst, "ld_fc")
        qC.dma(e2_sb[:], e2c, "ld_e2")
        qC.dma(ones1_sb[:], ones1, "ld_ones1")
        qC.dma(onescol_sb[:], onescol, "ld_onescol")
        qC.dma(selp_sb[:].rearrange("h (j p) -> h j p", j=8), selp.rearrange("j h p -> h j p"), "ld_selp")
        sq.dma(binp_sb[:], binp, "ld_binp")
        sq.dma(lnw_sb[:], lnw_c, "ld_lnw")
        sq.dma(lnb_sb[:], lnb_c, "ld_lnb")
        sq.dma(bvb, bv_b, "ld_bvb")
        for c in range(NCH):
            gq.dma(xbf[:, c * T:(c + 1) * T], xT_blk[c], f"ld_xbf{c}")
        for c in range(NCH):
            qV.dma(wv_sb[:, c * 512:(c + 1) * 512], wv2_blk[c][:, 0:512],
                   f"ld_wv0_{c}")

        dve.wait("ld_bvb")
        dve.mark(dve.tensor_single_scalar(qb_sb[:], binp_sb[:, 0:8], 0.125, ALU.mult),
                 "qb_ready")

        mu_sl = rows[:1, 0:2048]
        var_sl = rows[:1, 2048:4096]
        r_sl = rows[:1, 4096:6144]

        def ln_stats(src, tag):
            for c in range(NCH):
                act.wait(f"src_{tag}{c}")
                if c > 0:
                    act.wait_counts("pe", B.counts["pe"])
                act.mark(act.activation(sqs, src[:, c * T:(c + 1) * T], AF.Square),
                         f"sq_{tag}{c}")
                for t4 in range(4):
                    s2p = (SC_A if t4 < 2 else SC_B)[0:1, (t4 % 2) * 512:(t4 % 2) * 512 + 512]
                    pe.wait(f"src_{tag}{c}")
                    if c == 0 and t4 == 0:
                        pe.wait(f"src_{tag}{NCH-1}")
                        pe.wait_counts("dve", B.counts["dve"])
                    pe.mark(pe.matmul(AO[0:1, t4 * 512:(t4 + 1) * 512], onescol_sb[:],
                                      src[:, c * T + t4 * 512:c * T + (t4 + 1) * 512],
                                      start=(c == 0), stop=(c == NCH - 1)),
                            f"s1_{tag}{t4}" if c == NCH - 1 else None)
                    pe.wait(f"sq_{tag}{c}")
                    pe.mark(pe.matmul(s2p, onescol_sb[:],
                                      sqs[:, t4 * 512:(t4 + 1) * 512],
                                      start=(c == 0), stop=(c == NCH - 1)),
                            f"s2_{tag}{t4}" if c == NCH - 1 else None)
            for t4 in range(4):
                sl = slice(t4 * 512, (t4 + 1) * 512)
                s2p = (SC_A if t4 < 2 else SC_B)[0:1, (t4 % 2) * 512:(t4 % 2) * 512 + 512]
                dve.wait(f"s1_{tag}{t4}", f"s2_{tag}{t4}")
                dve.mark(dve.tensor_single_scalar(mu_sl[:, sl], AO[0:1, sl], 1.0 / D,
                                                  ALU.mult), f"mu_{tag}{t4}")
                dve.wait_self()
                dve.mark(dve.tensor_tensor(var_sl[:, sl], mu_sl[:, sl], mu_sl[:, sl],
                                           ALU.mult), None)
                dve.wait_self()
                dve.mark(dve.scalar_tensor_tensor(var_sl[:, sl], s2p, 1.0 / D,
                                                  var_sl[:, sl], ALU.mult, ALU.subtract),
                         f"var_{tag}{t4}")
                act.wait(f"var_{tag}{t4}", "ld_fc")
                act.mark(act.activation(var_sl[:, sl], var_sl[:, sl], AF.Ln,
                                        bias=fc_sb[:, 0:1]), None)
                act.wait_self()
                act.mark(act.activation(r_sl[:, sl], var_sl[:, sl], AF.Exp, scale=-0.5),
                         f"r_{tag}{t4}")

        for c in range(NCH):
            alias(f"src_x{c}", f"ld_xbf{NCH-1}")
        ln_stats(xbf, "x")
        for t4 in range(4):
            sl = slice(t4 * 512, (t4 + 1) * 512)
            pe.wait(f"mu_x{t4}")
            pe.wait_counts("dve", B.counts["dve"])
            pe.mark(pe.matmul(AO[:, 0:512],
                              ones1_sb[:], mu_sl[:, sl], start=True, stop=True),
                    f"mub_x{t4}")
            pe.wait(f"r_x{t4}")
            pe.mark(pe.matmul(AO[:, 512:1024], ones1_sb[:], r_sl[:, sl],
                              start=True, stop=True), f"rb_x{t4}")
            for c in range(NCH):
                csl = slice(c * T + t4 * 512, c * T + (t4 + 1) * 512)
                dve.wait(f"mub_x{t4}", f"rb_x{t4}", "ld_bvb")
                dve.mark(dve.tensor_tensor(tmpA, xbf[:, csl], AO[:, 0:512], ALU.subtract),
                         None)
                dve.mark(dve.tensor_tensor(xnf_a[:, csl], tmpA, AO[:, 512:1024], ALU.mult),
                         f"xn_{c}_{t4}")
        for c in range(NCH):
            alias(f"src_n{c}", f"xn_{c}_3")
            sq.wait(f"xn_{c}_3")
            sq.dma(xn_spill[:, c * T:(c + 1) * T], xnf_a[:, c * T:(c + 1) * T],
                   f"spill_{c}")
        ln_stats(xnf_a, "n")
        for t4 in range(4):
            sl = slice(t4 * 512, (t4 + 1) * 512)
            pe.wait(f"mu_n{t4}")
            pe.wait_counts("dve", B.counts["dve"])
            pe.mark(pe.matmul(AO[:, 0:512], ones1_sb[:], mu_sl[:, sl],
                              start=True, stop=True), f"mub_n{t4}")
            pe.wait(f"r_n{t4}")
            pe.mark(pe.matmul(AO[:, 512:1024], ones1_sb[:], r_sl[:, sl],
                              start=True, stop=True), f"rb_n{t4}")
            for c in range(NCH):
                csl = slice(c * T + t4 * 512, c * T + (t4 + 1) * 512)
                dve.wait(f"mub_n{t4}", f"rb_n{t4}", "ld_bvb")
                dve.mark(dve.tensor_tensor(tmpA, xnf_a[:, csl], AO[:, 0:512],
                                           ALU.subtract), None)
                dve.mark(dve.scalar_tensor_tensor(tmpA, tmpA, lnw_sb[:, c:c + 1],
                                                  AO[:, 512:1024], ALU.mult, ALU.mult),
                         None)
                dve.mark(dve.tensor_scalar_add(x2f[:, csl], tmpA, lnb_sb[:, c:c + 1]),
                         f"x2_{c}_{t4}")
        sync_all()
        dump_now("xbf", xbf, BF16)
        dump_now("xnf", xnf_a, BF16)
        dump_now("rows", rows, BF16)

        for js in range(2):
            if js == 1:
                for c in range(NCH):
                    qV.wait(f"vp_15_0")
                    qV.dma(wv_sb[:, c * 512:(c + 1) * 512], wv2_blk[c][:, 512:1024],
                           f"ld_wv1_{c}")
            for ct in range(16):
                pb = AO[:, (ct % 4) * 512:(ct % 4) * 512 + 512]
                for c in range(NCH):
                    if c == 0:
                        pe.wait(f"ld_wv{js}_{NCH-1}" if js == 1 else f"ld_wv0_{NCH-1}")
                        if ct >= 4 or js == 1:
                            pv = (js, ct - 4) if ct >= 4 else (0, 12 + ct)
                            pe.wait(f"vev_{pv[1]}_{pv[0]}")
                    pe.mark(pe.matmul(pb, x2f[:, c * T + ct * 128:c * T + (ct + 1) * 128],
                                      wv_sb[:, c * 512:(c + 1) * 512],
                                      start=(c == 0), stop=(c == NCH - 1)),
                            f"vp_{ct}_{js}" if c == NCH - 1 else None)
                dve.wait(f"vp_{ct}_{js}", "ld_bvb")
                dve.mark(dve.tensor_tensor(rb_bf, pb,
                                           bvb[:, js * 512:(js + 1) * 512], ALU.add),
                         f"vev_{ct}_{js}")
                vdst = v_sb[:, ct * 1040 + js * 8 * VGRP:ct * 1040 + (js + 1) * 8 * VGRP]
                dve.mark(dve.tensor_copy(
                    vdst.rearrange("p (g c) -> p g c", c=VGRP)[:, :, 0:64],
                    rb_bf.rearrange("p (g c) -> p g c", c=64)), f"vevs_{ct}_{js}")
        dve.mark(dve.memset(v_sb.rearrange("p (g c) -> p g c", c=VGRP)[:, :, 64:65], 1.0),
                 "v_ones")

        for j in range(8):
            par = j % 2
            qW.wait(f"qkp_{j-2}_1_3" if j >= 2 else None)
            qW.dma(wqk_sb[:, par * 2048:par * 2048 + 1024], winp[j], f"ld_wq{j}")
            qW.dma(wqk_sb[:, par * 2048 + 1024:par * 2048 + 2048], winp[8 + j],
                   f"ld_wk{j}")
            for part in range(2):
                wofs = par * 2048 + part * 1024
                for t4 in range(4):
                    pb = (SC_A if part == 0 else SC_B)[:, (t4 % 2) * 512:(t4 % 2) * 512 + 512]
                    for c in range(NCH):
                        if c == 0:
                            pe.wait(f"ld_wq{j}" if part == 0 else f"ld_wk{j}")
                            if j >= 1 and part == 0 and t4 == 0:
                                pe.wait(f"aoev_{j-1}_1")
                            if t4 >= 2:
                                pe.wait(f"qkev_{j}_{part}_{t4-2}")
                            elif j >= 1:
                                pe.wait(f"qkev_{j-1}_{part}_{t4+2}")
                        pe.mark(pe.matmul(pb,
                                          wqk_sb[:, wofs + c * 128:wofs + (c + 1) * 128],
                                          x2f[:, c * T + t4 * 512:c * T + (t4 + 1) * 512],
                                          start=(c == 0), stop=(c == NCH - 1)),
                                f"qkp_{j}_{part}_{t4}" if c == NCH - 1 else None)
                    act.wait(f"qkp_{j}_{part}_{t4}", "qb_ready")
                    dst = qk_sb[:, part * T + t4 * 512:part * T + (t4 + 1) * 512]
                    if part == 0:
                        ins = act.activation(dst, pb, AF.Identity,
                                             bias=qb_sb[:, j:j + 1], scale=0.125)
                    else:
                        ins = act.activation(dst, pb, AF.Identity,
                                             bias=binp_sb[:, 8 + j:9 + j])
                    act.mark(ins, f"qkev_{j}_{part}_{t4}")
            dve.wait(f"qkev_{j}_0_3")
            if j >= 1:
                dve.wait(f"av_{j-1}_1_12")
            dve.mark(dve.scalar_tensor_tensor(exp_sb[:, 0:T], qk_sb[:, 0:T], 1.0,
                                              qk_sb[:, 0:T], ALU.mult, ALU.mult,
                                              accum_out=bacc[:, 0:1]), None)
            dve.wait(f"qkev_{j}_1_3")
            dve.mark(dve.scalar_tensor_tensor(exp_sb[:, 0:T], qk_sb[:, T:2 * T], 1.0,
                                              qk_sb[:, T:2 * T], ALU.mult, ALU.mult,
                                              accum_out=bacc[:, 1:2]), f"bsq_{j}")
            dve.wait_self()
            dve.mark(dve.tensor_copy(bacc_bf[:], bacc[:, 0:2]), f"bacc_{j}")
            pe.wait(f"bacc_{j}", "ld_e2")
            if j >= 1:
                pe.wait(f"ubias_{j-1}", f"srow_{j-1}_1_3", f"aoev_{j-1}_1")
            pe.mark(pe.matmul(AO[0:1, 0:2], bacc_bf[:, 0:1], e2_sb[:],
                              start=True, stop=True), None)
            pe.mark(pe.matmul(AO[0:1, 512:514], bacc_bf[:, 1:2], e2_sb[:],
                              start=True, stop=True), f"bmm_{j}")
            dve.wait(f"bmm_{j}")
            dve.mark(dve.tensor_copy(nrow[:, 8:10], AO[0:1, 512:514]), None)
            dve.wait_self()
            dve.mark(dve.tensor_tensor(nrow[:, 0:2], AO[0:1, 0:2], nrow[:, 8:10],
                                       ALU.mult), f"bm1_{j}")
            act.wait(f"bm1_{j}")
            act.mark(act.activation(nrow[:, 2:4], nrow[:, 0:2], AF.Ln), None)
            act.wait("ld_fc")
            act.wait_self()
            act.mark(act.activation(nrow[:, 4:6], nrow[:, 2:4], AF.Exp, scale=0.5,
                                    bias=fc_sb[:, 1:2]), f"bc_{j}")
            dve.wait(f"bc_{j}")
            dve.mark(dve.tensor_single_scalar(nrow_bf[:, 0:2], nrow[:, 4:6], -1.0,
                                              ALU.mult), f"bneg_{j}")
            pe.wait(f"bneg_{j}")
            pe.mark(pe.matmul(AO[:, 4:6], ones1_sb[:], nrow_bf[:, 0:2],
                              start=True, stop=True), f"bbc_{j}")
            dve.wait(f"bbc_{j}")
            dve.mark(dve.tensor_copy(ubias[:, 2 * j:2 * j + 2], AO[:, 4:6]),
                     f"ubias_{j}")

            for h in range(2):
                hg = 2 * j + h
                po = h * 64
                for C in range(17):
                    for half in range(2):
                        if C >= 16:
                            break
                        piece = (SC_A if (2 * C + half) % 2 == 0 else SC_B)
                        pP = 2 * C + half - 2
                        if pP >= 0:
                            pe.wait(f"exp_{j}_{h}_{pP // 2}_{pP % 2}")
                        elif h == 1:
                            pe.wait(f"exp_{j}_0_15_{half}")
                        elif j >= 1:
                            pe.wait(f"exp_{j-1}_1_15_{half}")
                        for th in range(2):
                            t4 = half * 2 + th
                            pe.mark(pe.matmul(
                                piece[:, th * 512:(th + 1) * 512],
                                qk_sb[po:po + 64, T + C * 128:T + (C + 1) * 128],
                                qk_sb[po:po + 64, t4 * 512:(t4 + 1) * 512],
                                start=True, stop=True),
                                f"sc_{j}_{h}_{C}_{half}" if th == 1 else None)
                        ubcol = 2 * max(j - 1, 0) + h
                        act.wait(f"sc_{j}_{h}_{C}_{half}", f"ubias_{max(j-1, 0)}")
                        if C == 0 and h == 0:
                            act.wait(f"bsq_{j}")
                        if C >= 4 and half == 0:
                            act.wait(f"av_{j}_{h}_{C-4}")
                        elif C < 4 and half == 0 and not (j == 0 and h == 0):
                            act.wait(f"av_{j}_{h-1}_{12+C}" if h == 1
                                     else f"av_{j-1}_1_{12+C}")
                        act.mark(act.activation(
                            exp_sb[:, (C % 4) * T + half * 1024:(C % 4) * T + (half + 1) * 1024],
                            piece[:], AF.Exp, bias=ubias[:, ubcol:ubcol + 1]),
                            f"exp_{j}_{h}_{C}_{half}")
                    if C == 0:
                        continue
                    Ca = C - 1
                    for t4 in range(4):
                        pe.wait(f"exp_{j}_{h}_{Ca}_{t4 // 2}")
                        if Ca == 0 and t4 == 0:
                            pe.wait("v_ones", "vevs_15_1")
                            if h == 0:
                                pe.wait(f"ubias_{j}")
                            if not (j == 0 and h == 0):
                                ph = (j, 0) if h == 1 else (j - 1, 1)
                                pe.wait(f"aoev_{ph[0]}_{ph[1]}",
                                        f"srow_{ph[0]}_{ph[1]}_3")
                        pe.mark(pe.matmul(
                            AO[0:VGRP, t4 * 512:(t4 + 1) * 512],
                            v_sb[:, Ca * 1040 + hg * VGRP:Ca * 1040 + (hg + 1) * VGRP],
                            exp_sb[:, (Ca % 4) * T + t4 * 512:(Ca % 4) * T + (t4 + 1) * 512],
                            start=(Ca == 0), stop=(Ca == 15)),
                            f"av_{j}_{h}_{Ca}" if t4 == 3 else None)
                for t4 in range(4):
                    act.wait(f"av_{j}_{h}_15")
                    act.mark(act.activation(
                        aou[h * 64:(h + 1) * 64, :][:, j * T + t4 * 512:j * T + (t4 + 1) * 512],
                        AO[0:64, t4 * 512:(t4 + 1) * 512], AF.Identity),
                        f"aoev_{j}_{h}" if t4 == 3 else None)
                    dve.wait(f"av_{j}_{h}_15")
                    if t4 > 0 or h > 0 or j > 0:
                        dve.wait_counts("qS2", B.counts.get("qS2", 0))
                    dve.mark(dve.tensor_copy(srow_tmp[:],
                                             AO[64:65, t4 * 512:(t4 + 1) * 512]),
                             f"srow_{j}_{h}_{t4}")
                    qS2.wait(f"srow_{j}_{h}_{t4}")
                    qS2.dma(srows[hg:hg + 1, t4 * 512:(t4 + 1) * 512], srow_tmp[:],
                           f"sdma_{j}_{h}_{t4}")

        act.wait("sdma_7_1_3")
        act.mark(act.activation(srows[:], srows[:], AF.Ln), None)
        act.wait_self()
        act.mark(act.activation(rall[:], srows[:], AF.Exp, scale=-1.0), "rall")
        _prev_rb = None
        for t4 in range(4):
            for j in range(8):
                pe.wait("rall", "ld_selp")
                if _prev_rb is not None:
                    pe.wait(f"rbc_{_prev_rb[0]}_{_prev_rb[1]}")
                else:
                    pe.wait("srow_7_1_3", f"aoev_7_1", f"ubias_7")
                _prev_rb = (j, t4)
                pe.mark(pe.matmul(AO[:, 1024:1536], selp_sb[:, j * 128:(j + 1) * 128],
                                  rall[:, t4 * 512:(t4 + 1) * 512],
                                  start=True, stop=True), f"rbmm_{j}_{t4}")
                dve.wait(f"rbmm_{j}_{t4}", f"aoev_{j}_1")
                dve.mark(dve.tensor_copy(rb_bf, AO[:, 1024:1536]), f"rbc_{j}_{t4}")
                sl = slice(j * T + t4 * 512, j * T + (t4 + 1) * 512)
                dve.mark(dve.tensor_tensor(aou[:, sl], aou[:, sl], rb_bf, ALU.mult),
                         f"aon_{j}_{t4}")
        dump_now("x2fB", x2f, BF16)
        dump_now("vB", v_sb, BF16)
        dump_now("aouB", aou, BF16)
        dump_now("qk7", qk_sb, BF16)
        qS2.wait("qkp_7_1_3")
        qS2.dma(bob, bout_b, "ld_bob")
        for jh in range(2):
            wob = wv_sb if jh == 0 else qk_sb
            for c in range(NCH):
                qV.wait("vp_15_1" if jh == 0 else "av_7_1_15")
                qV.dma(wob[:, c * 512:(c + 1) * 512],
                       wo2_blk[c][:, jh * 512:(jh + 1) * 512], f"ld_wo{jh}_{c}")
            for ct in range(16):
                pb = (SC_A if ct % 2 == 0 else SC_B)[:, 0:512]
                for c in range(NCH):
                    pe.wait(f"aon_{c}_{ct // 4}")
                    if c == 0:
                        pe.wait(f"ld_wo{jh}_{NCH-1}")
                        if jh == 0 and ct < 2:
                            pe.wait(f"exp_7_1_15_{ct}")
                        if ct >= 2:
                            pe.wait(f"opev_{ct-2}_{jh}")
                        elif jh == 1:
                            pe.wait(f"opev_{14+ct}_0")
                    pe.mark(pe.matmul(pb, aou[:, c * T + ct * 128:c * T + (ct + 1) * 128],
                                      wob[:, c * 512:(c + 1) * 512],
                                      start=(c == 0), stop=(c == NCH - 1)),
                            f"opp_{ct}_{jh}" if c == NCH - 1 else None)
                dve.wait(f"opp_{ct}_{jh}", "ld_bob")
                dve.mark(dve.tensor_tensor(
                    s_sb[:, ct * D + jh * 512:ct * D + (jh + 1) * 512],
                    pb, bob[:, jh * 512:(jh + 1) * 512], ALU.add),
                    f"opev_{ct}_{jh}")

        sync_all()
        dump_now("sB2", s_sb[:], BF16)
        NPF = 4

        def ffn_wload(q, ht):
            li = q * NHC + ht
            if li >= 4 * NHC:
                return
            if li >= NPF:
                pq, pt = divmod(li - NPF, NHC)
                qF.wait(f"gg_d_{pq}_{pt}", f"uu_d_{pq}_{pt}")
            qF.dma(wg_sb[ht % NPF], wg_blk[ht], f"ld_wg_{q}_{ht}")
            qF.dma(wu_sb[ht % NPF], wu_blk[ht], f"ld_wu_{q}_{ht}")

        gq.dma(w02_sb.rearrange("p (h c f) -> p h c f", h=NL, c=2), w02T.rearrange("h c p f -> p h c f"), "ld_w02")
        gq.dma(w1r_sb.rearrange("p (h c f) -> p h c f", h=NL, c=2), w1r.rearrange("h c p f -> p h c f"), "ld_w1r")
        for li in range(NPF):
            ffn_wload(li // NHC, li % NHC)
        sq.dma(w0n_sb.rearrange("p (h c f) -> p h c f", h=NL, c=2), w0nat.rearrange("h c p f -> p h c f"), "ld_w0n")
        sq.dma(w1n_sb.rearrange("p (h c f) -> p h c f", h=NL, c=2), w1nat.rearrange("h c p f -> p h c f"), "ld_w1n")
        sq.dma(w2n_sb.rearrange("p (h c f) -> p h c f", h=NL, c=2), w2nat.rearrange("h c p f -> p h c f"), "ld_w2n")

        for h in range(NL):
            hb = h % 2
            for dc in range(2):
                gc = 2 * h + dc
                for ct in range(16):
                    pe.wait("ld_id")
                    if h >= 2:
                        pe.wait(f"lact_done_{h-2}")
                    xti = dc * 16 + ct
                    xts = SC_A[0:128, 0:64] if xti % 2 == 0 else SC_A[0:128, 512:576]
                    if xti >= 2:
                        pe.wait(f"xtev_{h}_{(xti - 2) // 16}_{(xti - 2) % 16}")
                    elif xti == 0 and h >= 1:
                        pe.wait(f"z_1_{h-1}_3")
                    pe.mark(pe.transpose(xts.bitcast(BF16),
                                         x2f[:, gc * T + ct * 128:gc * T + (ct + 1) * 128],
                                         id_sb[:]), f"xtp_{h}_{dc}_{ct}")
                    dve.wait(f"xtp_{h}_{dc}_{ct}")
                    dve.mark(dve.tensor_copy(
                        xt_sb[hb][:, ct * 256 + dc * 128:ct * 256 + (dc + 1) * 128],
                        xts.bitcast(BF16)), f"xtev_{h}_{dc}_{ct}")
            if h == 0:
                dump_now("xt0", xt_sb[0], BF16)
            for n in range(2):
                nh = f"{n}_{h}"
                def emit_dw(ct):
                    gt = n * 8 + ct
                    gtt = slice(gt * 256, (gt + 1) * 256)
                    pe.wait(f"dvt_{nh}_{ct}", f"le_{nh}_{ct}", f"xtev_{h}_1_15")
                    if ct == 0 and prevnh is not None:
                        pe.wait(f"oqev_{prevnh}_7")
                    for ds in range(2):
                        pe.mark(pe.matmul(AO[:, ds * 512:(ds + 1) * 512],
                                          xt_sb[hb][:, gt * 256 + ds * 128:gt * 256 + (ds + 1) * 128],
                                          pg_t[:, ct * 512:(ct + 1) * 512],
                                          start=(ct == 0), stop=(ct == 7)),
                                f"dw02_{nh}_{ds}" if ct == 7 else None)
                    for es in range(2):
                        pe.mark(pe.matmul(AO[:, 1024 + es * 512:1024 + es * 512 + 256],
                                          hid_t[:, ct * 256 + es * 128:ct * 256 + (es + 1) * 128],
                                          xt_sb[hb][:, gtt],
                                          start=(ct == 0), stop=(ct == 7)),
                                f"dw1_{nh}_{es}" if ct == 7 else None)

                for ct in range(8):
                    cp = ct % 2
                    tok = n * 1024 + ct * 128
                    ggp = SC_A[:, cp * 512:(cp + 1) * 512]
                    mp = SC_B[:, cp * 512:cp * 512 + 256]
                    prevnh = None if (n == 0 and h == 0) else (f"1_{h-1}" if n == 0 else f"0_{h}")
                    for dc in range(2):
                        gc = 2 * h + dc
                        if dc == 0:
                            pe.wait("ld_w02", "ld_w1r")
                            if ct >= 2:
                                pe.wait(f"dvele_{nh}_{ct-2}")
                            else:
                                pe.wait(f"xtev_{h}_1_15")
                                if prevnh is not None:
                                    pe.wait(f"z_{prevnh}_3")
                        pe.mark(pe.matmul(ggp,
                                          x2f[:, gc * T + tok:gc * T + tok + 128],
                                          w02_sb[:, (h * 2 + dc) * 512:(h * 2 + dc + 1) * 512],
                                          start=(dc == 0), stop=(dc == 1)),
                                f"gg_{nh}_{ct}" if dc == 1 else None)
                        pe.mark(pe.matmul(mp,
                                          x2f[:, gc * T + tok:gc * T + tok + 128],
                                          w1r_sb[:, (h * 2 + dc) * 256:(h * 2 + dc + 1) * 256],
                                          start=(dc == 0), stop=(dc == 1)),
                                f"m_{nh}_{ct}" if dc == 1 else None)
                    if ct >= 1:
                        emit_dw(ct - 1)
                    ctt = slice(ct * 256, (ct + 1) * 256)
                    pool_prev = (f"le_{nh}_{ct-2}" if ct >= 2 else
                                 (f"le_{prevnh}_{6+ct}" if prevnh is not None else None))
                    dvt_prev = (f"dvt_{nh}_{ct-2}" if ct >= 2 else
                                (f"dvt_{prevnh}_{6+ct}" if prevnh is not None else None))
                    act.wait(f"gg_{nh}_{ct}")
                    act.wait(pool_prev)
                    act.mark(act.activation(sg_f[cp], ggp[:, 0:256], AF.Sigmoid),
                             f"sg_{nh}_{ct}")
                    if ct >= 1:
                        ccp = (ct - 1) % 2
                        dve.wait(f"le_{nh}_{ct-1}")
                        dve.mark(dve.tensor_tensor(
                            pg_t[:, (ct - 1) * 512:(ct - 1) * 512 + 256],
                            dh_f[ccp], u_f[ccp], ALU.mult), f"dvt_{nh}_{ct-1}")
                    dve.wait(f"sg_{nh}_{ct}")
                    dve.wait(pool_prev, dvt_prev)
                    dve.mark(dve.tensor_tensor(hh_f[cp], ggp[:, 0:256], sg_f[cp],
                                               ALU.mult), None)
                    dve.mark(dve.tensor_copy(g_f[cp], ggp[:, 256:512]), None)
                    dve.wait(f"m_{nh}_{ct}")
                    dve.mark(dve.tensor_tensor(pg_t[:, ct * 512 + 256:(ct + 1) * 512],
                                               mp, hh_f[cp],
                                               ALU.mult), None)
                    dve.mark(dve.tensor_tensor(dh_f[cp], mp, g_f[cp],
                                               ALU.mult), None)
                    dve.mark(dve.scalar_tensor_tensor(v2_f[cp], ggp[:, 0:256], 1.0,
                                                      hh_f[cp], ALU.add, ALU.subtract),
                             f"dvele_{nh}_{ct}")
                    pool.wait(f"dvele_{nh}_{ct}", dvt_prev)
                    if ct < 2 and prevnh is not None:
                        pool.wait(f"dw1_{prevnh}_1")
                    pool.mark(pool.tensor_tensor(hid_t[:, ctt], hh_f[cp], g_f[cp],
                                                 ALU.mult), None)
                    pool.mark(pool.tensor_tensor(u_f[cp], sg_f[cp], v2_f[cp], ALU.mult),
                              f"le_{nh}_{ct}")
                    if ct == 7:
                        dve.wait(f"le_{nh}_7")
                        dve.mark(dve.tensor_tensor(pg_t[:, 7 * 512:7 * 512 + 256],
                                                   dh_f[cp], u_f[cp],
                                                   ALU.mult), f"dvt_{nh}_7")

                emit_dw(7)
                if n == 0 and h == 0:
                    dump_now("hid0", hid_t, BF16)
                    dump_now("pg0", pg_t, BF16)
                    dump_now("wnT0", wnT_bf, BF16)
                upd = [("dw02", 0, w0n_sb, 0), ("dw1", 1024 + 512, w1n_sb, 512),
                       ("dw02", 256, w2n_sb, 1024)]
                for k, (nm, po_, nat, wofs) in enumerate(upd):
                    for cs in range(2):
                        dve.wait(f"{nm}_{nh}_{cs}", "ld_w2n")
                        psl = (AO[:, cs * 512 + po_:cs * 512 + po_ + 256] if nm == "dw02"
                               else AO[:, 1024 + cs * 512:1024 + cs * 512 + 256])
                        dve.mark(dve.tensor_tensor(
                            wnT_bf[:, wofs + cs * 256:wofs + (cs + 1) * 256],
                            psl,
                            nat[:, (h * 2 + cs) * 256:(h * 2 + cs + 1) * 256],
                            ALU.add), f"wnev_{nh}_{2 * k + cs}")
                for ec in range(2):
                    for c2 in range(2):
                        cp = (ec * 2 + c2) % 2
                        gqp = SC_A[:, cp * 512:(cp + 1) * 512]
                        g2p = SC_B[:, cp * 512:(cp + 1) * 512]
                        csl = slice(n * 1024 + c2 * 512, n * 1024 + (c2 + 1) * 512)
                        for ds in range(2):
                            gc = 2 * h + ds
                            if ds == 0:
                                pe.wait(f"wnev_{nh}_5")
                                if ec * 2 + c2 >= 2:
                                    pe.wait(f"z_{nh}_{ec * 2 + c2 - 2}")
                            pe.mark(pe.matmul(gqp,
                                              wnT_bf[:, 0 + ds * 256 + ec * 128:0 + ds * 256 + (ec + 1) * 128],
                                              x2f[:, gc * T + csl.start:gc * T + csl.stop],
                                              start=(ds == 0), stop=(ds == 1)),
                                    f"gq_{nh}_{ec}_{c2}" if ds == 1 else None)
                            pe.mark(pe.matmul(g2p,
                                              wnT_bf[:, 1024 + ds * 256 + ec * 128:1024 + ds * 256 + (ec + 1) * 128],
                                              x2f[:, gc * T + csl.start:gc * T + csl.stop],
                                              start=(ds == 0), stop=(ds == 1)),
                                    f"gq2_{nh}_{ec}_{c2}" if ds == 1 else None)
                        act.wait(f"gq_{nh}_{ec}_{c2}")
                        act.mark(act.activation(hq_f[cp], gqp, AF.Sigmoid), f"hqs_{nh}_{ec}_{c2}")
                        dve.wait(f"hqs_{nh}_{ec}_{c2}")
                        dve.mark(dve.tensor_tensor(hq_f[cp], gqp, hq_f[cp], ALU.mult),
                                 None)
                        dve.wait(f"gq2_{nh}_{ec}_{c2}")
                        dve.mark(dve.tensor_tensor(
                            z_sb[:, ec * 1024 + c2 * 512:ec * 1024 + (c2 + 1) * 512],
                            hq_f[cp], g2p, ALU.mult), f"z_{nh}_{ec * 2 + c2}")
                if n == 0 and h == 0:
                    dump_now("z0", z_sb, BF16)
                for ct in range(8):
                    op = AO[:, (ct % 4) * 512:(ct % 4) * 512 + 256]
                    for ec in range(2):
                        if ec == 0:
                            pe.wait(f"z_{nh}_{ct // 4}", f"z_{nh}_{2 + ct // 4}")
                            if ct >= 4:
                                pe.wait(f"oqev_{nh}_{ct-4}")
                            elif not (n == 0 and h == 0):
                                pn = f"1_{h-1}" if n == 0 else f"0_{h}"
                                pe.wait(f"oqev_{pn}_{4 + ct}")
                        pe.mark(pe.matmul(op,
                                          z_sb[:, ec * 1024 + ct * 128:ec * 1024 + (ct + 1) * 128],
                                          wnT_bf[:, 512 + ec * 256:512 + (ec + 1) * 256],
                                          start=(ec == 0), stop=(ec == 1)),
                                f"oq_{nh}_{ct}" if ec == 1 else None)
                    dve.wait(f"oq_{nh}_{ct}", f"opev_{n * 8 + ct}_1")
                    ssl = slice((n * 8 + ct) * D + h * 256, (n * 8 + ct) * D + h * 256 + 256)
                    dve.mark(dve.tensor_tensor(s_sb[:, ssl], op, s_sb[:, ssl], ALU.add),
                             f"oqev_{nh}_{ct}")
            if True:
                alias(f"lact_done_{h}", f"oqev_1_{h}_7")

        sync_all()
        for hc in range(NHC):
            qD.dma(wd_sb[:, hc * D:(hc + 1) * D], wdT_blk[hc], f"ld_wd{hc}")
        st_count = [0]
        for q in range(4):
            xb = xnf_d[q % 2]
            for c in range(NCH):
                qN.wait(f"spill_{NCH-1}")
                if q > 1:
                    qN.wait(f"uu_d_{q-2}_{NHC-1}")
                qN.dma(xb[:, c * 512:(c + 1) * 512],
                       xn_spill[:, c * T + q * 512:c * T + (q + 1) * 512],
                       f"ld_xn_{q}_{c}")
            for ht in range(NHC):
                li = q * NHC + ht
                ffn_wload((li + NPF) // NHC, (li + NPF) % NHC)
                p = ht % 2
                gp = SC_A[:, p * 512:(p + 1) * 512]
                up = SC_B[:, p * 512:(p + 1) * 512]
                for c in range(NCH):
                    if c == 0:
                        pe.wait(f"ld_wg_{q}_{ht}", f"ld_wu_{q}_{ht}", f"ld_xn_{q}_{NCH-1}")
                        if ht >= 2:
                            pe.wait(f"guev_{q}_{ht-2}_1")
                        elif q >= 1:
                            pe.wait(f"guev_{q-1}_{NHC-2+ht}_1")
                    pe.mark(pe.matmul(gp, wg_sb[ht % NPF][:, c * 128:(c + 1) * 128],
                                      xb[:, c * 512:(c + 1) * 512],
                                      start=(c == 0), stop=(c == NCH - 1)),
                            f"gg_d_{q}_{ht}" if c == NCH - 1 else None)
                    pe.mark(pe.matmul(up, wu_sb[ht % NPF][:, c * 128:(c + 1) * 128],
                                      xb[:, c * 512:(c + 1) * 512],
                                      start=(c == 0), stop=(c == NCH - 1)),
                            f"uu_d_{q}_{ht}" if c == NCH - 1 else None)
                act.wait(f"gg_d_{q}_{ht}")
                act.mark(act.activation(silu_f, gp, AF.Silu, scale=S127),
                         f"guev_{q}_{ht}_0")
                dve.wait(f"guev_{q}_{ht}_0", f"uu_d_{q}_{ht}")
                dve.mark(dve.scalar_tensor_tensor(
                    ht_sb[:, ht * 512:(ht + 1) * 512], up, S127, silu_f,
                    ALU.mult, ALU.mult), f"guev_{q}_{ht}_1")
            for tt in range(4):
                for dh2 in range(2):
                    ti = tt * 2 + dh2
                    p2 = ti % 2
                    tok0 = q * 512 + tt * 128
                    dp = AO[:, (ti % 4) * 512:(ti % 4) * 512 + 512]
                    qX.wait(f"ob_{q}_{ti-4}" if ti >= 4 else
                            (f"ob_{q-1}_{4+ti}" if q >= 1 else None))
                    qX.dma(x_f[ti % 4], x_t[tok0:tok0 + 128, dh2 * 512:(dh2 + 1) * 512],
                           f"ld_x_{q}_{ti}")
                    for hc in range(NHC):
                        if hc == 0:
                            pe.wait(f"guev_{q}_{NHC-1}_1", f"ld_wd{NHC-1}")
                            if ti >= 4:
                                pe.wait(f"dnev_{q}_{ti-4}")
                            elif q >= 1:
                                pe.wait(f"dnev_{q-1}_{4+ti}")
                        pe.mark(pe.matmul(dp, ht_sb[:, hc * 512 + tt * 128:hc * 512 + (tt + 1) * 128],
                                          wd_sb[:, hc * D + dh2 * 512:hc * D + (dh2 + 1) * 512],
                                          start=(hc == 0), stop=(hc == NHC - 1)),
                                f"dn_{q}_{ti}" if hc == NHC - 1 else None)
                    act.wait(f"dn_{q}_{ti}")
                    if ti >= 2:
                        act.wait(f"dnev_{q}_{ti-2}")
                    elif q >= 1:
                        act.wait(f"dnev_{q-1}_{6+ti}")
                    act.mark(act.activation(sig_f[p2], dp, AF.Sigmoid, scale=S127),
                             f"dsig_{q}_{ti}")
                    dve.wait(f"dsig_{q}_{ti}")
                    dve.mark(dve.scalar_tensor_tensor(gg_f[p2], dp, S127, sig_f[p2],
                                                      ALU.mult, ALU.mult),
                             f"dnev_{q}_{ti}")
                    ssl = slice((q * 4 + tt) * D + dh2 * 512, (q * 4 + tt) * D + (dh2 + 1) * 512)
                    dve.mark(dve.tensor_tensor(sgp_f[p2], s_sb[:, ssl], gg_f[p2],
                                               ALU.mult), None)
                    dve.wait(f"ld_x_{q}_{ti}")
                    if ti >= 2:
                        dve.wait(f"st_{q}_{ti-2}")
                    elif q >= 1:
                        dve.wait(f"st_{q-1}_{6+ti}")
                    dve.mark(dve.tensor_tensor(outb[p2], sgp_f[p2], x_f[ti % 4],
                                               ALU.add), f"ob_{q}_{ti}")
                    qO.wait(f"ob_{q}_{ti}")
                    qO.dma(out[tok0:tok0 + 128, dh2 * 512:(dh2 + 1) * 512], outb[p2],
                           f"st_{q}_{ti}")
                    st_count[0] += 1

        if not B.dry:
            B.engines["sq"].wait_ge(B.sems["qO"], B.counts["qO"])

        if KDBG:
            dump_map = {
                "x2f": (x2f, BF16), "aou": (aou, BF16), "s_sb": (s_sb[:], BF16),
                "v_sb": (v_sb, BF16), "srows": (srows[:], F32), "rall": (rall[:], BF16),
            }
            for nm in KDBG.split(","):
                ap, dt = dump_map[nm]
                if B.dry:
                    dtn = nc.dram_tensor("dbg_" + nm, list(ap.shape), dt,
                                         kind="ExternalOutput").ap()
                    B.marks["dbgt_" + nm] = dtn
                dtn = B.marks["dbgt_" + nm]
                sq.dma(dtn, ap, None)
            if not B.dry:
                B.engines["sq"].wait_ge(B.sems["sq"], B.counts["sq"])
            else:
                pass

    program()
    B.dry = False
    B.counts = {}
    B.hiwater = {}
    with nc.Block() as blk:
        @blk.sync
        def _(s):
            gx, sx = nc.gpsimd, nc.sync
            B.engines = {"pe": nc.tensor, "act": nc.scalar, "dve": nc.vector,
                         "gq": gx, "sq": sx, "qC": gx, "qW": gx, "qV": gx,
                         "qF": gx, "qD": gx, "qS2": sx, "qN": sx, "qX": sx,
                         "qO": sx, "qP": gx}
            program()
    return nc


def _blockT(w):
    o, i = w.shape
    return np.ascontiguousarray(
        w.reshape(o // 128, 128, i // 128, 128).transpose(0, 3, 2, 1).reshape(o // 128, 128, i))


_NC_CACHE = [None]

def kernel(**inp):
    f32 = np.float32
    xs = np.asarray(inp["L0_attn_x"], f32)
    ipw = np.asarray(inp["L0_attn_in_proj_w"], f32)
    ipb = np.asarray(inp["L0_attn_in_proj_b"], f32)
    opw = np.asarray(inp["L0_attn_out_proj_w"], f32)
    opb = np.asarray(inp["L0_attn_out_proj_b"], f32)
    lnw = np.asarray(inp["L0_attn_ln_w"], f32)
    lnb = np.asarray(inp["L0_attn_ln_b"], f32)
    w0 = np.asarray(inp["L0_attn_w0"], f32)
    w1 = np.asarray(inp["L0_attn_w1"], f32)
    w2 = np.asarray(inp["L0_attn_w2"], f32)
    wg = np.ascontiguousarray(np.asarray(inp["L0_ffn_w_gate"], np.int32))
    wu = np.ascontiguousarray(np.asarray(inp["L0_ffn_w_up"], np.int32))
    wd = np.ascontiguousarray(np.asarray(inp["L0_ffn_w_down"], np.int32))

    shared = {
        "winp": _blockT(ipw),
        "binp": np.ascontiguousarray(ipb.reshape(24, 128).T),
        "wout": _blockT(opw),
        "wv2_blk": np.ascontiguousarray(ipw[2048:].T.reshape(8, 128, D)),
        "wo2_blk": np.ascontiguousarray(opw.T.reshape(8, 128, D)),
        "bout_b": np.ascontiguousarray(np.broadcast_to(opb, (128, D))),
        "bv_b": np.ascontiguousarray(np.broadcast_to(ipb[2048:], (128, D))),
        "lnw_c": np.ascontiguousarray(lnw.reshape(8, 128).T),
        "lnb_c": np.ascontiguousarray(lnb.reshape(8, 128).T),
        "w02T": np.ascontiguousarray(np.concatenate(
            [np.stack([w0[h].T.reshape(2, 128, 256) for h in range(NL)]),
             np.stack([w2[h].T.reshape(2, 128, 256) for h in range(NL)])], axis=3)),
        "w1r": np.ascontiguousarray(w1.reshape(NL, 2, 128, 256)),
        "w0nat": np.ascontiguousarray(
            np.stack([w0[hh].T.reshape(2, 128, 256) for hh in range(NL)])),
        "w1nat": np.ascontiguousarray(
            np.stack([w1[hh].T.reshape(2, 128, 256) for hh in range(NL)])),
        "w2nat": np.ascontiguousarray(
            np.stack([w2[hh].T.reshape(2, 128, 256) for hh in range(NL)])),
        "wg_blk": _blockT(wg),
        "wu_blk": _blockT(wu),
        "wdT_blk": np.ascontiguousarray(wd.T.reshape(24, 128, D)),
        "id128": np.eye(128, dtype=f32),
        "e2c": np.ascontiguousarray(
            np.stack([(np.arange(128) < 64).astype(f32),
                      (np.arange(128) >= 64).astype(f32)], axis=1)),
        "ones1": np.ones((1, 128), f32),
        "selp": np.ascontiguousarray(np.stack([
            np.stack([((np.arange(128) < 64) if hh == 2 * jj else
                       ((np.arange(128) >= 64) if hh == 2 * jj + 1 else
                        np.zeros(128, bool))).astype(f32)
                      for hh in range(16)])
            for jj in range(8)])),
        "onescol": np.ones((128, 1), f32),
        "fconst": np.array([[EPS, np.log(KAPPA / T)]], f32),
    }
    if _NC_CACHE[0] is None:
        _NC_CACHE[0] = build_nc()
    nc = _NC_CACHE[0]
    in_maps = []
    for b in range(8):
        d = dict(shared)
        d["x_t"] = np.ascontiguousarray(xs[b])
        d["xT_blk"] = np.ascontiguousarray(xs[b].T.reshape(NCH, 128, T))
        in_maps.append(d)
    res = run_bass_kernel_spmd(nc, in_maps, core_ids=list(range(8)))
    return np.stack([r["out"] for r in res.results])


if __name__ == "__main__":
    nc = build_nc()
    print("build ok; sbuf remaining:", nc.sbuf_bytes_remaining,
          "psum banks remaining:", nc.psum_banks_remaining)


# version 26
# speedup vs baseline: 1.0332x; 1.0320x over previous
import os
import numpy as np
import ml_dtypes
import concourse.bass as bass
import concourse.mybir as mybir
from concourse.bass_utils import run_bass_kernel_spmd

F32 = mybir.dt.float32
BF16 = mybir.dt.bfloat16
I32 = mybir.dt.int32
AF = mybir.ActivationFunctionType
ALU = mybir.AluOpType

T = 2048
D = 1024
NCH = 8
NH = 16
HD = 64
NL = 4
HID = 3072
NHC = 24
EPS = 1e-5
S127 = 1.0 / 127.0
KAPPA = 2.0
VGRP = 65

KDBG = os.environ.get("KDBG", "")

class _Dummy:
    def then_inc(self, *a, **k):
        return self

_DUMMY = _Dummy()

class Builder:
    def __init__(self):
        self.dry = True
        self.counts = {}
        self.marks = {}
        self.hiwater = {}
        self.engines = {}
        self.sems = {}

class Prox:
    def __init__(self, B, key):
        self._B = B
        self._key = key

    def __getattr__(self, name):
        B = self._B
        key = self._key
        def call(*a, **k):
            if B.dry:
                return _DUMMY
            return getattr(B.engines[key], name)(*a, **k)
        return call

    def mark(self, ins, name=None):
        B = self._B
        key = self._key
        B.counts[key] = B.counts.get(key, 0) + 1
        if name is not None:
            if B.dry:
                assert name not in B.marks, f"dup mark {name}"
                B.marks[name] = (key, B.counts[key])
            else:
                assert B.marks[name] == (key, B.counts[key]), f"mark drift {name}"
        if not B.dry:
            ins.then_inc(B.sems[key], 1)

    def dma(self, out, in_, name=None, **kw):
        B = self._B
        key = self._key
        B.counts[key] = B.counts.get(key, 0) + 16
        if name is not None:
            if B.dry:
                assert name not in B.marks, f"dup mark {name}"
                B.marks[name] = (key, B.counts[key])
            else:
                assert B.marks[name] == (key, B.counts[key]), f"mark drift {name}"
        if not B.dry:
            B.engines[key].dma_start(out=out, in_=in_, **kw).then_inc(B.sems[key], 16)

    def wait(self, *names):
        B = self._B
        for nm in names:
            if nm is None:
                continue
            if B.dry:
                continue
            skey, val = B.marks[nm]
            self.wait_counts(skey, val)

    def wait_counts(self, skey, val):
        B = self._B
        if B.dry or val <= 0:
            return
        if skey == self._key:
            return
        hw = B.hiwater.get((self._key, skey), 0)
        if val <= hw:
            return
        B.hiwater[(self._key, skey)] = val
        B.engines[self._key].wait_ge(B.sems[skey], val)

    def wait_self(self):
        B = self._B
        if B.dry:
            return
        B.engines[self._key].wait_ge(B.sems[self._key], B.counts.get(self._key, 0))


def build_nc():
    nc = bass.Bass("TRN2", target_bir_lowering=False)

    DT = nc.dram_tensor
    x_t = DT("x_t", [T, D], F32, kind="ExternalInput").ap()
    xT_blk = DT("xT_blk", [NCH, 128, T], F32, kind="ExternalInput").ap()
    winp = DT("winp", [24, 128, D], F32, kind="ExternalInput").ap()
    binp = DT("binp", [128, 24], F32, kind="ExternalInput").ap()
    wout = DT("wout", [8, 128, D], F32, kind="ExternalInput").ap()
    wv2_blk = DT("wv2_blk", [8, 128, D], F32, kind="ExternalInput").ap()
    wo2_blk = DT("wo2_blk", [8, 128, D], F32, kind="ExternalInput").ap()
    bout_b = DT("bout_b", [128, D], F32, kind="ExternalInput").ap()
    bv_b = DT("bv_b", [128, D], F32, kind="ExternalInput").ap()
    lnw_c = DT("lnw_c", [128, NCH], F32, kind="ExternalInput").ap()
    lnb_c = DT("lnb_c", [128, NCH], F32, kind="ExternalInput").ap()
    w02T = DT("w02T", [NL, 2, 128, 512], F32, kind="ExternalInput").ap()
    w1r = DT("w1r", [NL, 2, 128, 256], F32, kind="ExternalInput").ap()
    w0nat = DT("w0nat", [NL, 2, 128, 256], F32, kind="ExternalInput").ap()
    w1nat = DT("w1nat", [NL, 2, 128, 256], F32, kind="ExternalInput").ap()
    w2nat = DT("w2nat", [NL, 2, 128, 256], F32, kind="ExternalInput").ap()
    wg_blk = DT("wg_blk", [NHC, 128, D], I32, kind="ExternalInput").ap()
    wu_blk = DT("wu_blk", [NHC, 128, D], I32, kind="ExternalInput").ap()
    wdT_blk = DT("wdT_blk", [NHC, 128, D], I32, kind="ExternalInput").ap()
    id128 = DT("id128", [128, 128], F32, kind="ExternalInput").ap()
    e2c = DT("e2c", [128, 2], F32, kind="ExternalInput").ap()
    ones1 = DT("ones1", [1, 128], F32, kind="ExternalInput").ap()
    selp = DT("selp", [8, 16, 128], F32, kind="ExternalInput").ap()
    onescol = DT("onescol", [128, 1], F32, kind="ExternalInput").ap()
    fconst = DT("fconst", [1, 2], F32, kind="ExternalInput").ap()

    out = DT("out", [T, D], F32, kind="ExternalOutput").ap()
    xn_spill = DT("xn_spill", [128, NCH * T], BF16, kind="Internal").ap()

    S = nc.alloc_sbuf_tensor
    id_sb = S("id_sb", [128, 128], BF16)
    e2_sb = S("e2_sb", [128, 2], BF16)
    ones1_sb = S("ones1_sb", [1, 128], BF16)
    onescol_sb = S("onescol_sb", [128, 1], BF16)
    selp_sb = S("selp_sb", [16, 8 * 128], BF16)
    binp_sb = S("binp_sb", [128, 24], F32)
    qb_sb = S("qb_sb", [128, 8], F32)
    lnw_sb = S("lnw_sb", [128, NCH], F32)
    lnb_sb = S("lnb_sb", [128, NCH], F32)
    bacc = S("bacc", [128, 4], F32)
    bacc_bf = S("bacc_bf", [128, 2], BF16)
    nrow = S("nrow", [1, 16], F32)
    nrow_bf = S("nrow_bf", [1, 16], BF16)
    srow_tmp = S("srow_tmp", [1, 512], F32)
    srows = S("srows", [16, 2048], F32)
    rall = S("rall", [16, 2048], BF16)
    fc_sb = S("fc_sb", [1, 2], F32)

    s_sb = S("s_sb", [128, 16 * D], BF16)

    AR_N = 75008
    arena = S("arena", [128, AR_N], BF16)

    class Carver:
        def __init__(self):
            self.off = 0
        def take(self, n, name=""):
            n = (n + 15) & ~15
            ap = arena[:, self.off:self.off + n]
            self.off += n
            assert self.off <= AR_N, f"arena overflow at {name}: {self.off}"
            return ap
        def take_f32(self, n, name=""):
            return self.take(2 * n, name).bitcast(F32)

    cv = Carver()
    off0 = cv.off
    x2f = cv.take(NCH * T, "x2f")
    offX = cv.off

    cv.off = offX
    xbf = cv.take(NCH * T, "xbf")
    xnf_a = cv.take(NCH * T, "xnf_a")
    sqs = cv.take(T, "sqs")
    tmpA = cv.take_f32(512, "tmpA")
    rows = cv.take(3 * 2048, "rows")
    offA = cv.off

    cv.off = offX
    v_sb = cv.take(16 * 16 * VGRP, "v_sb")
    aou = cv.take(NCH * T, "aou")
    qk_sb = cv.take(2 * T, "qk")
    exp_sb = cv.take(4 * T, "exp")
    ubias = cv.take_f32(16, "ubias")
    rb_bf = cv.take(512, "rb")
    cv.off = max(cv.off, offA)
    bvb = cv.take_f32(D, "bvb")
    wv_sb = cv.take(NCH * 512, "wv")
    wqk_sb = cv.take(2 * 2048, "wqk")
    offB = cv.off
    assert offB <= AR_N

    cv.off = offX
    xt_sb = [cv.take(16 * 256, f"xt{i}") for i in range(2)]
    w02_sb = cv.take(NL * 2 * 512, "w02")
    w1r_sb = cv.take(NL * 2 * 256, "w1r")
    w0n_sb = cv.take_f32(NL * 2 * 256, "w0n")
    w1n_sb = cv.take_f32(NL * 2 * 256, "w1n")
    w2n_sb = cv.take_f32(NL * 2 * 256, "w2n")
    hid_t = cv.take(8 * 256, "hid_t")
    pg_t = cv.take(8 * 512, "pg_t")
    sg_f = [cv.take_f32(256, f"sg{i}") for i in range(2)]
    hh_f = [cv.take_f32(256, f"hh{i}") for i in range(2)]
    g_f = [cv.take_f32(256, f"g{i}") for i in range(2)]
    dh_f = [cv.take_f32(256, f"dh{i}") for i in range(2)]
    v2_f = [cv.take_f32(256, f"v2{i}") for i in range(2)]
    u_f = [cv.take_f32(256, f"u{i}") for i in range(2)]
    wn_bf = cv.take(3 * 2 * 256, "wn_bf")
    wnT_bf = cv.take(3 * 2 * 256, "wnT_bf")
    hq_f = [cv.take_f32(512, f"hq{i}") for i in range(2)]
    z_sb = cv.take(2 * 1024, "z_sb")
    offC = cv.off

    cv.off = off0
    ht_sb = cv.take(NHC * 512, "ht")
    wd_sb = cv.take(NHC * D, "wd")
    xnf_d = [cv.take(NCH * 512, f"xnf_d{i}") for i in range(2)]
    silu_f = cv.take_f32(512, "silu")
    sig_f = [cv.take_f32(512, f"sig{i}") for i in range(2)]
    gg_f = [cv.take_f32(512, f"gg{i}") for i in range(2)]
    sgp_f = [cv.take_f32(512, f"sgp{i}") for i in range(2)]
    x_f = [cv.take_f32(512, f"xf{i}") for i in range(4)]
    outb = [cv.take_f32(512, f"ob{i}") for i in range(2)]
    offD = cv.off
    cv.off = max(offC, offD)
    wg_sb = [cv.take(D, f"wg{i}") for i in range(4)]
    wu_sb = [cv.take(D, f"wu{i}") for i in range(4)]
    assert max(offA, offB, offC, offD, cv.off) <= AR_N

    bob = wqk_sb[:, 0:2048].bitcast(F32)

    P = nc.alloc_psum_tensor
    SC = P("SC", [128, 2048], F32)
    SC_A = SC[:, 0:1024]
    SC_B = SC[:, 1024:2048]
    AO = P("AO", [128, 2048], F32)

    B = Builder()
    STREAMS = ("pe", "act", "dve", "gq", "sq",
               "qC", "qW", "qV", "qF", "qD",
               "qS2", "qN", "qX", "qO",
               "qP")
    for key in STREAMS:
        B.sems[key] = nc.alloc_semaphore("sem_" + key)
    pe = Prox(B, "pe")
    act = Prox(B, "act")
    dve = Prox(B, "dve")
    gq = Prox(B, "gq")
    sq = Prox(B, "sq")
    qC = Prox(B, "qC")
    qW = Prox(B, "qW")
    qV = Prox(B, "qV")
    qF = Prox(B, "qF")
    qD = Prox(B, "qD")
    qS2 = Prox(B, "qS2")
    qN = Prox(B, "qN")
    qX = Prox(B, "qX")
    qO = Prox(B, "qO")
    pool = Prox(B, "qP")
    ALL = (pe, act, dve, gq, sq, qC, qW, qV, qF, qD, qS2, qN, qX, qO, pool)

    def alias(dst, src):
        if B.dry:
            assert dst not in B.marks, f"dup alias {dst}"
            B.marks[dst] = B.marks[src]
        else:
            assert B.marks[dst] == B.marks[src]

    def sync_all(engines=ALL):
        snap = dict(B.counts)
        for e in engines:
            for k in ("pe", "act", "dve", "gq", "sq"):
                if k != e._key and snap.get(k):
                    e.wait_counts(k, snap[k])

    KDBG2 = os.environ.get("KDBG2", "")

    def dump_now(name, ap, dt):
        if name not in KDBG2.split(","):
            return
        key = "dbg2t_" + name
        if B.dry:
            B.marks[key] = nc.dram_tensor("dbg2_" + name, list(ap.shape), dt,
                                          kind="ExternalOutput").ap()
        snap = dict(B.counts)
        for k in ("pe", "act", "dve", "gq"):
            if snap.get(k):
                sq.wait_counts(k, snap[k])
        sq.dma(B.marks[key], ap, None)

    def program():
        qC.dma(id_sb[:], id128, "ld_id")
        sq.dma(fc_sb[:], fconst, "ld_fc")
        qC.dma(e2_sb[:], e2c, "ld_e2")
        qC.dma(ones1_sb[:], ones1, "ld_ones1")
        qC.dma(onescol_sb[:], onescol, "ld_onescol")
        qC.dma(selp_sb[:].rearrange("h (j p) -> h j p", j=8), selp.rearrange("j h p -> h j p"), "ld_selp")
        sq.dma(binp_sb[:], binp, "ld_binp")
        sq.dma(lnw_sb[:], lnw_c, "ld_lnw")
        sq.dma(lnb_sb[:], lnb_c, "ld_lnb")
        sq.dma(bvb, bv_b, "ld_bvb")
        for c in range(NCH):
            gq.dma(xbf[:, c * T:(c + 1) * T], xT_blk[c], f"ld_xbf{c}")
        for c in range(NCH):
            qV.dma(wv_sb[:, c * 512:(c + 1) * 512], wv2_blk[c][:, 0:512],
                   f"ld_wv0_{c}")

        dve.wait("ld_bvb")
        dve.mark(dve.tensor_single_scalar(qb_sb[:], binp_sb[:, 0:8], 0.125, ALU.mult),
                 "qb_ready")

        mu_sl = rows[:1, 0:2048]
        var_sl = rows[:1, 2048:4096]
        r_sl = rows[:1, 4096:6144]

        def ln_stats(src, tag):
            for c in range(NCH):
                act.wait(f"src_{tag}{c}")
                if c > 0:
                    act.wait_counts("pe", B.counts["pe"])
                act.mark(act.activation(sqs, src[:, c * T:(c + 1) * T], AF.Square),
                         f"sq_{tag}{c}")
                for t4 in range(4):
                    s2p = (SC_A if t4 < 2 else SC_B)[0:1, (t4 % 2) * 512:(t4 % 2) * 512 + 512]
                    pe.wait(f"src_{tag}{c}")
                    if c == 0 and t4 == 0:
                        pe.wait(f"src_{tag}{NCH-1}")
                        pe.wait_counts("dve", B.counts["dve"])
                    pe.mark(pe.matmul(AO[0:1, t4 * 512:(t4 + 1) * 512], onescol_sb[:],
                                      src[:, c * T + t4 * 512:c * T + (t4 + 1) * 512],
                                      start=(c == 0), stop=(c == NCH - 1)),
                            f"s1_{tag}{t4}" if c == NCH - 1 else None)
                    pe.wait(f"sq_{tag}{c}")
                    pe.mark(pe.matmul(s2p, onescol_sb[:],
                                      sqs[:, t4 * 512:(t4 + 1) * 512],
                                      start=(c == 0), stop=(c == NCH - 1)),
                            f"s2_{tag}{t4}" if c == NCH - 1 else None)
            for t4 in range(4):
                sl = slice(t4 * 512, (t4 + 1) * 512)
                s2p = (SC_A if t4 < 2 else SC_B)[0:1, (t4 % 2) * 512:(t4 % 2) * 512 + 512]
                dve.wait(f"s1_{tag}{t4}", f"s2_{tag}{t4}")
                dve.mark(dve.tensor_single_scalar(mu_sl[:, sl], AO[0:1, sl], 1.0 / D,
                                                  ALU.mult), f"mu_{tag}{t4}")
                dve.wait_self()
                dve.mark(dve.tensor_tensor(var_sl[:, sl], mu_sl[:, sl], mu_sl[:, sl],
                                           ALU.mult), None)
                dve.wait_self()
                dve.mark(dve.scalar_tensor_tensor(var_sl[:, sl], s2p, 1.0 / D,
                                                  var_sl[:, sl], ALU.mult, ALU.subtract),
                         f"var_{tag}{t4}")
                act.wait(f"var_{tag}{t4}", "ld_fc")
                act.mark(act.activation(var_sl[:, sl], var_sl[:, sl], AF.Ln,
                                        bias=fc_sb[:, 0:1]), None)
                act.wait_self()
                act.mark(act.activation(r_sl[:, sl], var_sl[:, sl], AF.Exp, scale=-0.5),
                         f"r_{tag}{t4}")

        for c in range(NCH):
            alias(f"src_x{c}", f"ld_xbf{NCH-1}")
        ln_stats(xbf, "x")
        for t4 in range(4):
            sl = slice(t4 * 512, (t4 + 1) * 512)
            pe.wait(f"mu_x{t4}")
            pe.wait_counts("dve", B.counts["dve"])
            pe.mark(pe.matmul(AO[:, 0:512],
                              ones1_sb[:], mu_sl[:, sl], start=True, stop=True),
                    f"mub_x{t4}")
            pe.wait(f"r_x{t4}")
            pe.mark(pe.matmul(AO[:, 512:1024], ones1_sb[:], r_sl[:, sl],
                              start=True, stop=True), f"rb_x{t4}")
            for c in range(NCH):
                csl = slice(c * T + t4 * 512, c * T + (t4 + 1) * 512)
                dve.wait(f"mub_x{t4}", f"rb_x{t4}", "ld_bvb")
                dve.mark(dve.tensor_tensor(tmpA, xbf[:, csl], AO[:, 0:512], ALU.subtract),
                         None)
                dve.mark(dve.tensor_tensor(xnf_a[:, csl], tmpA, AO[:, 512:1024], ALU.mult),
                         f"xn_{c}_{t4}")
        for c in range(NCH):
            alias(f"src_n{c}", f"xn_{c}_3")
            sq.wait(f"xn_{c}_3")
            sq.dma(xn_spill[:, c * T:(c + 1) * T], xnf_a[:, c * T:(c + 1) * T],
                   f"spill_{c}")
        ln_stats(xnf_a, "n")
        for t4 in range(4):
            sl = slice(t4 * 512, (t4 + 1) * 512)
            pe.wait(f"mu_n{t4}")
            pe.wait_counts("dve", B.counts["dve"])
            pe.mark(pe.matmul(AO[:, 0:512], ones1_sb[:], mu_sl[:, sl],
                              start=True, stop=True), f"mub_n{t4}")
            pe.wait(f"r_n{t4}")
            pe.mark(pe.matmul(AO[:, 512:1024], ones1_sb[:], r_sl[:, sl],
                              start=True, stop=True), f"rb_n{t4}")
            for c in range(NCH):
                csl = slice(c * T + t4 * 512, c * T + (t4 + 1) * 512)
                dve.wait(f"mub_n{t4}", f"rb_n{t4}", "ld_bvb")
                dve.mark(dve.tensor_tensor(tmpA, xnf_a[:, csl], AO[:, 0:512],
                                           ALU.subtract), None)
                dve.mark(dve.scalar_tensor_tensor(tmpA, tmpA, lnw_sb[:, c:c + 1],
                                                  AO[:, 512:1024], ALU.mult, ALU.mult),
                         None)
                dve.mark(dve.tensor_scalar_add(x2f[:, csl], tmpA, lnb_sb[:, c:c + 1]),
                         f"x2_{c}_{t4}")
        sync_all()
        dump_now("xbf", xbf, BF16)
        dump_now("xnf", xnf_a, BF16)
        dump_now("rows", rows, BF16)

        for js in range(2):
            if js == 1:
                for c in range(NCH):
                    qV.wait(f"vp_15_0")
                    qV.dma(wv_sb[:, c * 512:(c + 1) * 512], wv2_blk[c][:, 512:1024],
                           f"ld_wv1_{c}")
            for ct in range(16):
                pb = AO[:, (ct % 4) * 512:(ct % 4) * 512 + 512]
                for c in range(NCH):
                    if c == 0:
                        pe.wait(f"ld_wv{js}_{NCH-1}" if js == 1 else f"ld_wv0_{NCH-1}")
                        if ct >= 4 or js == 1:
                            pv = (js, ct - 4) if ct >= 4 else (0, 12 + ct)
                            pe.wait(f"vev_{pv[1]}_{pv[0]}")
                    pe.mark(pe.matmul(pb, x2f[:, c * T + ct * 128:c * T + (ct + 1) * 128],
                                      wv_sb[:, c * 512:(c + 1) * 512],
                                      start=(c == 0), stop=(c == NCH - 1)),
                            f"vp_{ct}_{js}" if c == NCH - 1 else None)
                dve.wait(f"vp_{ct}_{js}", "ld_bvb")
                dve.mark(dve.tensor_tensor(rb_bf, pb,
                                           bvb[:, js * 512:(js + 1) * 512], ALU.add),
                         f"vev_{ct}_{js}")
                vdst = v_sb[:, ct * 1040 + js * 8 * VGRP:ct * 1040 + (js + 1) * 8 * VGRP]
                dve.mark(dve.tensor_copy(
                    vdst.rearrange("p (g c) -> p g c", c=VGRP)[:, :, 0:64],
                    rb_bf.rearrange("p (g c) -> p g c", c=64)), f"vevs_{ct}_{js}")
        dve.mark(dve.memset(v_sb.rearrange("p (g c) -> p g c", c=VGRP)[:, :, 64:65], 1.0),
                 "v_ones")

        def emit_qw_loads(jj):
            par2 = jj % 2
            qW.wait(f"qkp_{jj-2}_1_3" if jj >= 2 else None)
            qW.dma(wqk_sb[:, par2 * 2048:par2 * 2048 + 1024], winp[jj], f"ld_wq{jj}")
            qW.dma(wqk_sb[:, par2 * 2048 + 1024:par2 * 2048 + 2048], winp[8 + jj],
                   f"ld_wk{jj}")

        def emit_in_proj(jj):
            par2 = jj % 2
            for part in range(2):
                wofs = par2 * 2048 + part * 1024
                for t4 in range(4):
                    pb = (SC_A if part == 0 else SC_B)[:, (t4 % 2) * 512:(t4 % 2) * 512 + 512]
                    for c in range(NCH):
                        if c == 0:
                            pe.wait(f"ld_wq{jj}" if part == 0 else f"ld_wk{jj}")
                            if jj >= 1 and t4 < 2:
                                pe.wait(f"exp_{jj-1}_1_15_{part}")
                            if t4 >= 2:
                                pe.wait(f"qkev_{jj}_{part}_{t4-2}")
                            elif jj >= 1:
                                pe.wait(f"qkev_{jj-1}_{part}_{t4+2}")
                        pe.mark(pe.matmul(pb,
                                          wqk_sb[:, wofs + c * 128:wofs + (c + 1) * 128],
                                          x2f[:, c * T + t4 * 512:c * T + (t4 + 1) * 512],
                                          start=(c == 0), stop=(c == NCH - 1)),
                                f"qkp_{jj}_{part}_{t4}" if c == NCH - 1 else None)
                    act.wait(f"qkp_{jj}_{part}_{t4}", "qb_ready")
                    if part == 0 and t4 == 0 and jj >= 1:
                        act.wait(f"bsq_{jj-1}")
                    dst = qk_sb[:, part * T + t4 * 512:part * T + (t4 + 1) * 512]
                    if part == 0:
                        ins = act.activation(dst, pb, AF.Identity,
                                             bias=qb_sb[:, jj:jj + 1], scale=0.125)
                    else:
                        ins = act.activation(dst, pb, AF.Identity,
                                             bias=binp_sb[:, 8 + jj:9 + jj])
                    act.mark(ins, f"qkev_{jj}_{part}_{t4}")

        emit_qw_loads(0)
        emit_in_proj(0)
        for j in range(8):
            if j + 1 < 8:
                emit_qw_loads(j + 1)
            dve.wait(f"qkev_{j}_0_3")
            dve.mark(dve.scalar_tensor_tensor(exp_sb[:, 0:T], qk_sb[:, 0:T], 1.0,
                                              qk_sb[:, 0:T], ALU.mult, ALU.mult,
                                              accum_out=bacc[:, 0:1]), None)
            dve.wait(f"qkev_{j}_1_3")
            dve.mark(dve.scalar_tensor_tensor(exp_sb[:, 0:T], qk_sb[:, T:2 * T], 1.0,
                                              qk_sb[:, T:2 * T], ALU.mult, ALU.mult,
                                              accum_out=bacc[:, 1:2]), f"bsq_{j}")
            dve.wait_self()
            dve.mark(dve.tensor_copy(bacc_bf[:], bacc[:, 0:2]), f"bacc_{j}")
            pe.wait(f"bacc_{j}", "ld_e2")
            if j >= 1:
                pe.wait(f"ubias_{j-1}", f"srow_{j-1}_1_3", f"aoev_{j-1}_1")
            pe.mark(pe.matmul(AO[0:1, 0:2], bacc_bf[:, 0:1], e2_sb[:],
                              start=True, stop=True), None)
            pe.mark(pe.matmul(AO[0:1, 512:514], bacc_bf[:, 1:2], e2_sb[:],
                              start=True, stop=True), f"bmm_{j}")
            dve.wait(f"bmm_{j}")
            dve.mark(dve.tensor_copy(nrow[:, 8:10], AO[0:1, 512:514]), None)
            dve.wait_self()
            dve.mark(dve.tensor_tensor(nrow[:, 0:2], AO[0:1, 0:2], nrow[:, 8:10],
                                       ALU.mult), f"bm1_{j}")
            act.wait(f"bm1_{j}")
            act.mark(act.activation(nrow[:, 2:4], nrow[:, 0:2], AF.Ln), None)
            act.wait("ld_fc")
            act.wait_self()
            act.mark(act.activation(nrow[:, 4:6], nrow[:, 2:4], AF.Exp, scale=0.5,
                                    bias=fc_sb[:, 1:2]), f"bc_{j}")
            dve.wait(f"bc_{j}")
            dve.mark(dve.tensor_single_scalar(nrow_bf[:, 0:2], nrow[:, 4:6], -1.0,
                                              ALU.mult), f"bneg_{j}")
            pe.wait(f"bneg_{j}")
            pe.mark(pe.matmul(AO[:, 4:6], ones1_sb[:], nrow_bf[:, 0:2],
                              start=True, stop=True), f"bbc_{j}")
            dve.wait(f"bbc_{j}")
            dve.mark(dve.tensor_copy(ubias[:, 2 * j:2 * j + 2], AO[:, 4:6]),
                     f"ubias_{j}")

            for h in range(2):
                hg = 2 * j + h
                po = h * 64
                for rnd in range(2):
                    base = rnd * 8
                    for C in range(base, base + 8):
                        for half in range(2):
                            piece = SC_A if half == 0 else SC_B
                            if C >= 1:
                                pe.wait(f"exp_{j}_{h}_{C-1}_{half}")
                            elif h == 1:
                                pe.wait(f"exp_{j}_0_15_{half}")
                            for th in range(2):
                                t4 = half * 2 + th
                                pe.mark(pe.matmul(
                                    piece[:, th * 512:(th + 1) * 512],
                                    qk_sb[po:po + 64, T + C * 128:T + (C + 1) * 128],
                                    qk_sb[po:po + 64, t4 * 512:(t4 + 1) * 512],
                                    start=True, stop=True),
                                    f"sc_{j}_{h}_{C}_{half}" if th == 1 else None)
                            ubcol = 2 * max(j - 1, 0) + h
                            act.wait(f"sc_{j}_{h}_{C}_{half}", f"ubias_{max(j-1, 0)}")
                            act.mark(act.activation(
                                s_sb[:, (C % 8) * T + half * 1024:(C % 8) * T + (half + 1) * 1024],
                                piece[:], AF.Exp, bias=ubias[:, ubcol:ubcol + 1]),
                                f"exp_{j}_{h}_{C}_{half}")
                    for Ca in range(base, base + 8):
                        for t4 in range(4):
                            if Ca == base and t4 == 0:
                                pe.wait(f"exp_{j}_{h}_{base + 7}_1")
                            pe.wait(f"exp_{j}_{h}_{Ca}_{t4 // 2}")
                            if Ca == 0 and t4 == 0:
                                pe.wait("v_ones", "vevs_15_1")
                                if h == 0:
                                    pe.wait(f"ubias_{j}")
                                if not (j == 0 and h == 0):
                                    ph = (j, 0) if h == 1 else (j - 1, 1)
                                    pe.wait(f"aoev_{ph[0]}_{ph[1]}",
                                            f"srow_{ph[0]}_{ph[1]}_3")
                            pe.mark(pe.matmul(
                                AO[0:VGRP, t4 * 512:(t4 + 1) * 512],
                                v_sb[:, Ca * 1040 + hg * VGRP:Ca * 1040 + (hg + 1) * VGRP],
                                s_sb[:, (Ca % 8) * T + t4 * 512:(Ca % 8) * T + (t4 + 1) * 512],
                                start=(Ca == 0), stop=(Ca == 15)),
                                f"av_{j}_{h}_{Ca}" if t4 == 3 else None)
                for t4 in range(4):
                    act.wait(f"av_{j}_{h}_15")
                    act.mark(act.activation(
                        aou[h * 64:(h + 1) * 64, :][:, j * T + t4 * 512:j * T + (t4 + 1) * 512],
                        AO[0:64, t4 * 512:(t4 + 1) * 512], AF.Identity),
                        f"aoev_{j}_{h}" if t4 == 3 else None)
                    dve.wait(f"av_{j}_{h}_15")
                    if t4 > 0 or h > 0 or j > 0:
                        dve.wait_counts("qS2", B.counts.get("qS2", 0))
                    dve.mark(dve.tensor_copy(srow_tmp[:],
                                             AO[64:65, t4 * 512:(t4 + 1) * 512]),
                             f"srow_{j}_{h}_{t4}")
                    qS2.wait(f"srow_{j}_{h}_{t4}")
                    qS2.dma(srows[hg:hg + 1, t4 * 512:(t4 + 1) * 512], srow_tmp[:],
                           f"sdma_{j}_{h}_{t4}")
            if j + 1 < 8:
                emit_in_proj(j + 1)

        act.wait("sdma_7_1_3")
        act.mark(act.activation(srows[:], srows[:], AF.Ln), None)
        act.wait_self()
        act.mark(act.activation(rall[:], srows[:], AF.Exp, scale=-1.0), "rall")
        _prev_rb = None
        for t4 in range(4):
            for j in range(8):
                pe.wait("rall", "ld_selp")
                if _prev_rb is not None:
                    pe.wait(f"rbc_{_prev_rb[0]}_{_prev_rb[1]}")
                else:
                    pe.wait("srow_7_1_3", f"aoev_7_1", f"ubias_7")
                _prev_rb = (j, t4)
                pe.mark(pe.matmul(AO[:, 1024:1536], selp_sb[:, j * 128:(j + 1) * 128],
                                  rall[:, t4 * 512:(t4 + 1) * 512],
                                  start=True, stop=True), f"rbmm_{j}_{t4}")
                dve.wait(f"rbmm_{j}_{t4}", f"aoev_{j}_1")
                dve.mark(dve.tensor_copy(rb_bf, AO[:, 1024:1536]), f"rbc_{j}_{t4}")
                sl = slice(j * T + t4 * 512, j * T + (t4 + 1) * 512)
                dve.mark(dve.tensor_tensor(aou[:, sl], aou[:, sl], rb_bf, ALU.mult),
                         f"aon_{j}_{t4}")
        dump_now("x2fB", x2f, BF16)
        dump_now("vB", v_sb, BF16)
        dump_now("aouB", aou, BF16)
        dump_now("qk7", qk_sb, BF16)
        qS2.wait("qkp_7_1_3")
        qS2.dma(bob, bout_b, "ld_bob")
        for jh in range(2):
            wob = wv_sb if jh == 0 else qk_sb
            for c in range(NCH):
                qV.wait("vp_15_1" if jh == 0 else "av_7_1_15")
                qV.dma(wob[:, c * 512:(c + 1) * 512],
                       wo2_blk[c][:, jh * 512:(jh + 1) * 512], f"ld_wo{jh}_{c}")
            for ct in range(16):
                pb = (SC_A if ct % 2 == 0 else SC_B)[:, 0:512]
                for c in range(NCH):
                    pe.wait(f"aon_{c}_{ct // 4}")
                    if c == 0:
                        pe.wait(f"ld_wo{jh}_{NCH-1}")
                        if jh == 0 and ct < 2:
                            pe.wait(f"exp_7_1_15_{ct}")
                        if ct >= 2:
                            pe.wait(f"opev_{ct-2}_{jh}")
                        elif jh == 1:
                            pe.wait(f"opev_{14+ct}_0")
                    pe.mark(pe.matmul(pb, aou[:, c * T + ct * 128:c * T + (ct + 1) * 128],
                                      wob[:, c * 512:(c + 1) * 512],
                                      start=(c == 0), stop=(c == NCH - 1)),
                            f"opp_{ct}_{jh}" if c == NCH - 1 else None)
                dve.wait(f"opp_{ct}_{jh}", "ld_bob")
                dve.mark(dve.tensor_tensor(
                    s_sb[:, ct * D + jh * 512:ct * D + (jh + 1) * 512],
                    pb, bob[:, jh * 512:(jh + 1) * 512], ALU.add),
                    f"opev_{ct}_{jh}")

        sync_all()
        dump_now("sB2", s_sb[:], BF16)
        NPF = 4

        def ffn_wload(q, ht):
            li = q * NHC + ht
            if li >= 4 * NHC:
                return
            if li >= NPF:
                pq, pt = divmod(li - NPF, NHC)
                qF.wait(f"gg_d_{pq}_{pt}", f"uu_d_{pq}_{pt}")
            qF.dma(wg_sb[ht % NPF], wg_blk[ht], f"ld_wg_{q}_{ht}")
            qF.dma(wu_sb[ht % NPF], wu_blk[ht], f"ld_wu_{q}_{ht}")

        gq.dma(w02_sb.rearrange("p (h c f) -> p h c f", h=NL, c=2), w02T.rearrange("h c p f -> p h c f"), "ld_w02")
        gq.dma(w1r_sb.rearrange("p (h c f) -> p h c f", h=NL, c=2), w1r.rearrange("h c p f -> p h c f"), "ld_w1r")
        for li in range(NPF):
            ffn_wload(li // NHC, li % NHC)
        sq.dma(w0n_sb.rearrange("p (h c f) -> p h c f", h=NL, c=2), w0nat.rearrange("h c p f -> p h c f"), "ld_w0n")
        sq.dma(w1n_sb.rearrange("p (h c f) -> p h c f", h=NL, c=2), w1nat.rearrange("h c p f -> p h c f"), "ld_w1n")
        sq.dma(w2n_sb.rearrange("p (h c f) -> p h c f", h=NL, c=2), w2nat.rearrange("h c p f -> p h c f"), "ld_w2n")

        for h in range(NL):
            hb = h % 2
            for dc in range(2):
                gc = 2 * h + dc
                for ct in range(16):
                    pe.wait("ld_id")
                    if h >= 2:
                        pe.wait(f"lact_done_{h-2}")
                    xti = dc * 16 + ct
                    xts = SC_A[0:128, 0:64] if xti % 2 == 0 else SC_A[0:128, 512:576]
                    if xti >= 2:
                        pe.wait(f"xtev_{h}_{(xti - 2) // 16}_{(xti - 2) % 16}")
                    elif xti == 0 and h >= 1:
                        pe.wait(f"z_1_{h-1}_3")
                    pe.mark(pe.transpose(xts.bitcast(BF16),
                                         x2f[:, gc * T + ct * 128:gc * T + (ct + 1) * 128],
                                         id_sb[:]), f"xtp_{h}_{dc}_{ct}")
                    dve.wait(f"xtp_{h}_{dc}_{ct}")
                    dve.mark(dve.tensor_copy(
                        xt_sb[hb][:, ct * 256 + dc * 128:ct * 256 + (dc + 1) * 128],
                        xts.bitcast(BF16)), f"xtev_{h}_{dc}_{ct}")
            if h == 0:
                dump_now("xt0", xt_sb[0], BF16)
            for n in range(2):
                nh = f"{n}_{h}"
                def emit_dw(ct):
                    gt = n * 8 + ct
                    gtt = slice(gt * 256, (gt + 1) * 256)
                    pe.wait(f"dvt_{nh}_{ct}", f"le_{nh}_{ct}", f"xtev_{h}_1_15")
                    if ct == 0 and prevnh is not None:
                        pe.wait(f"oqev_{prevnh}_7")
                    for ds in range(2):
                        pe.mark(pe.matmul(AO[:, ds * 512:(ds + 1) * 512],
                                          xt_sb[hb][:, gt * 256 + ds * 128:gt * 256 + (ds + 1) * 128],
                                          pg_t[:, ct * 512:(ct + 1) * 512],
                                          start=(ct == 0), stop=(ct == 7)),
                                f"dw02_{nh}_{ds}" if ct == 7 else None)
                    for es in range(2):
                        pe.mark(pe.matmul(AO[:, 1024 + es * 512:1024 + es * 512 + 256],
                                          hid_t[:, ct * 256 + es * 128:ct * 256 + (es + 1) * 128],
                                          xt_sb[hb][:, gtt],
                                          start=(ct == 0), stop=(ct == 7)),
                                f"dw1_{nh}_{es}" if ct == 7 else None)

                for ct in range(8):
                    cp = ct % 2
                    tok = n * 1024 + ct * 128
                    ggp = SC_A[:, cp * 512:(cp + 1) * 512]
                    mp = SC_B[:, cp * 512:cp * 512 + 256]
                    prevnh = None if (n == 0 and h == 0) else (f"1_{h-1}" if n == 0 else f"0_{h}")
                    for dc in range(2):
                        gc = 2 * h + dc
                        if dc == 0:
                            pe.wait("ld_w02", "ld_w1r")
                            if ct >= 2:
                                pe.wait(f"dvele_{nh}_{ct-2}")
                            else:
                                pe.wait(f"xtev_{h}_1_15")
                                if prevnh is not None:
                                    pe.wait(f"z_{prevnh}_3")
                        pe.mark(pe.matmul(ggp,
                                          x2f[:, gc * T + tok:gc * T + tok + 128],
                                          w02_sb[:, (h * 2 + dc) * 512:(h * 2 + dc + 1) * 512],
                                          start=(dc == 0), stop=(dc == 1)),
                                f"gg_{nh}_{ct}" if dc == 1 else None)
                        pe.mark(pe.matmul(mp,
                                          x2f[:, gc * T + tok:gc * T + tok + 128],
                                          w1r_sb[:, (h * 2 + dc) * 256:(h * 2 + dc + 1) * 256],
                                          start=(dc == 0), stop=(dc == 1)),
                                f"m_{nh}_{ct}" if dc == 1 else None)
                    if ct >= 1:
                        emit_dw(ct - 1)
                    ctt = slice(ct * 256, (ct + 1) * 256)
                    pool_prev = (f"le_{nh}_{ct-2}" if ct >= 2 else
                                 (f"le_{prevnh}_{6+ct}" if prevnh is not None else None))
                    dvt_prev = (f"dvt_{nh}_{ct-2}" if ct >= 2 else
                                (f"dvt_{prevnh}_{6+ct}" if prevnh is not None else None))
                    act.wait(f"gg_{nh}_{ct}")
                    act.wait(pool_prev)
                    act.mark(act.activation(sg_f[cp], ggp[:, 0:256], AF.Sigmoid),
                             f"sg_{nh}_{ct}")
                    if ct >= 1:
                        ccp = (ct - 1) % 2
                        dve.wait(f"le_{nh}_{ct-1}")
                        dve.mark(dve.tensor_tensor(
                            pg_t[:, (ct - 1) * 512:(ct - 1) * 512 + 256],
                            dh_f[ccp], u_f[ccp], ALU.mult), f"dvt_{nh}_{ct-1}")
                    dve.wait(f"sg_{nh}_{ct}")
                    dve.wait(pool_prev, dvt_prev)
                    dve.mark(dve.tensor_tensor(hh_f[cp], ggp[:, 0:256], sg_f[cp],
                                               ALU.mult), None)
                    dve.mark(dve.tensor_copy(g_f[cp], ggp[:, 256:512]), None)
                    dve.wait(f"m_{nh}_{ct}")
                    dve.mark(dve.tensor_tensor(pg_t[:, ct * 512 + 256:(ct + 1) * 512],
                                               mp, hh_f[cp],
                                               ALU.mult), None)
                    dve.mark(dve.tensor_tensor(dh_f[cp], mp, g_f[cp],
                                               ALU.mult), None)
                    dve.mark(dve.scalar_tensor_tensor(v2_f[cp], ggp[:, 0:256], 1.0,
                                                      hh_f[cp], ALU.add, ALU.subtract),
                             f"dvele_{nh}_{ct}")
                    pool.wait(f"dvele_{nh}_{ct}", dvt_prev)
                    if ct < 2 and prevnh is not None:
                        pool.wait(f"dw1_{prevnh}_1")
                    pool.mark(pool.tensor_tensor(hid_t[:, ctt], hh_f[cp], g_f[cp],
                                                 ALU.mult), None)
                    pool.mark(pool.tensor_tensor(u_f[cp], sg_f[cp], v2_f[cp], ALU.mult),
                              f"le_{nh}_{ct}")
                    if ct == 7:
                        dve.wait(f"le_{nh}_7")
                        dve.mark(dve.tensor_tensor(pg_t[:, 7 * 512:7 * 512 + 256],
                                                   dh_f[cp], u_f[cp],
                                                   ALU.mult), f"dvt_{nh}_7")

                emit_dw(7)
                if n == 0 and h == 0:
                    dump_now("hid0", hid_t, BF16)
                    dump_now("pg0", pg_t, BF16)
                    dump_now("wnT0", wnT_bf, BF16)
                upd = [("dw02", 0, w0n_sb, 0), ("dw1", 1024 + 512, w1n_sb, 512),
                       ("dw02", 256, w2n_sb, 1024)]
                for k, (nm, po_, nat, wofs) in enumerate(upd):
                    for cs in range(2):
                        dve.wait(f"{nm}_{nh}_{cs}", "ld_w2n")
                        psl = (AO[:, cs * 512 + po_:cs * 512 + po_ + 256] if nm == "dw02"
                               else AO[:, 1024 + cs * 512:1024 + cs * 512 + 256])
                        dve.mark(dve.tensor_tensor(
                            wnT_bf[:, wofs + cs * 256:wofs + (cs + 1) * 256],
                            psl,
                            nat[:, (h * 2 + cs) * 256:(h * 2 + cs + 1) * 256],
                            ALU.add), f"wnev_{nh}_{2 * k + cs}")
                for ec in range(2):
                    for c2 in range(2):
                        cp = (ec * 2 + c2) % 2
                        gqp = SC_A[:, cp * 512:(cp + 1) * 512]
                        g2p = SC_B[:, cp * 512:(cp + 1) * 512]
                        csl = slice(n * 1024 + c2 * 512, n * 1024 + (c2 + 1) * 512)
                        for ds in range(2):
                            gc = 2 * h + ds
                            if ds == 0:
                                pe.wait(f"wnev_{nh}_5")
                                if ec * 2 + c2 >= 2:
                                    pe.wait(f"z_{nh}_{ec * 2 + c2 - 2}")
                            pe.mark(pe.matmul(gqp,
                                              wnT_bf[:, 0 + ds * 256 + ec * 128:0 + ds * 256 + (ec + 1) * 128],
                                              x2f[:, gc * T + csl.start:gc * T + csl.stop],
                                              start=(ds == 0), stop=(ds == 1)),
                                    f"gq_{nh}_{ec}_{c2}" if ds == 1 else None)
                            pe.mark(pe.matmul(g2p,
                                              wnT_bf[:, 1024 + ds * 256 + ec * 128:1024 + ds * 256 + (ec + 1) * 128],
                                              x2f[:, gc * T + csl.start:gc * T + csl.stop],
                                              start=(ds == 0), stop=(ds == 1)),
                                    f"gq2_{nh}_{ec}_{c2}" if ds == 1 else None)
                        act.wait(f"gq_{nh}_{ec}_{c2}")
                        act.mark(act.activation(hq_f[cp], gqp, AF.Sigmoid), f"hqs_{nh}_{ec}_{c2}")
                        dve.wait(f"hqs_{nh}_{ec}_{c2}")
                        dve.mark(dve.tensor_tensor(hq_f[cp], gqp, hq_f[cp], ALU.mult),
                                 None)
                        dve.wait(f"gq2_{nh}_{ec}_{c2}")
                        dve.mark(dve.tensor_tensor(
                            z_sb[:, ec * 1024 + c2 * 512:ec * 1024 + (c2 + 1) * 512],
                            hq_f[cp], g2p, ALU.mult), f"z_{nh}_{ec * 2 + c2}")
                if n == 0 and h == 0:
                    dump_now("z0", z_sb, BF16)
                for ct in range(8):
                    op = AO[:, (ct % 4) * 512:(ct % 4) * 512 + 256]
                    for ec in range(2):
                        if ec == 0:
                            pe.wait(f"z_{nh}_{ct // 4}", f"z_{nh}_{2 + ct // 4}")
                            if ct >= 4:
                                pe.wait(f"oqev_{nh}_{ct-4}")
                            elif not (n == 0 and h == 0):
                                pn = f"1_{h-1}" if n == 0 else f"0_{h}"
                                pe.wait(f"oqev_{pn}_{4 + ct}")
                        pe.mark(pe.matmul(op,
                                          z_sb[:, ec * 1024 + ct * 128:ec * 1024 + (ct + 1) * 128],
                                          wnT_bf[:, 512 + ec * 256:512 + (ec + 1) * 256],
                                          start=(ec == 0), stop=(ec == 1)),
                                f"oq_{nh}_{ct}" if ec == 1 else None)
                    dve.wait(f"oq_{nh}_{ct}", f"opev_{n * 8 + ct}_1")
                    ssl = slice((n * 8 + ct) * D + h * 256, (n * 8 + ct) * D + h * 256 + 256)
                    dve.mark(dve.tensor_tensor(s_sb[:, ssl], op, s_sb[:, ssl], ALU.add),
                             f"oqev_{nh}_{ct}")
            if True:
                alias(f"lact_done_{h}", f"oqev_1_{h}_7")

        sync_all()
        for hc in range(NHC):
            qD.dma(wd_sb[:, hc * D:(hc + 1) * D], wdT_blk[hc], f"ld_wd{hc}")
        st_count = [0]
        for q in range(4):
            xb = xnf_d[q % 2]
            for c in range(NCH):
                qN.wait(f"spill_{NCH-1}")
                if q > 1:
                    qN.wait(f"uu_d_{q-2}_{NHC-1}")
                qN.dma(xb[:, c * 512:(c + 1) * 512],
                       xn_spill[:, c * T + q * 512:c * T + (q + 1) * 512],
                       f"ld_xn_{q}_{c}")
            for ht in range(NHC):
                li = q * NHC + ht
                ffn_wload((li + NPF) // NHC, (li + NPF) % NHC)
                p = ht % 2
                gp = SC_A[:, p * 512:(p + 1) * 512]
                up = SC_B[:, p * 512:(p + 1) * 512]
                for c in range(NCH):
                    if c == 0:
                        pe.wait(f"ld_wg_{q}_{ht}", f"ld_wu_{q}_{ht}", f"ld_xn_{q}_{NCH-1}")
                        if ht >= 2:
                            pe.wait(f"guev_{q}_{ht-2}_1")
                        elif q >= 1:
                            pe.wait(f"guev_{q-1}_{NHC-2+ht}_1")
                    pe.mark(pe.matmul(gp, wg_sb[ht % NPF][:, c * 128:(c + 1) * 128],
                                      xb[:, c * 512:(c + 1) * 512],
                                      start=(c == 0), stop=(c == NCH - 1)),
                            f"gg_d_{q}_{ht}" if c == NCH - 1 else None)
                    pe.mark(pe.matmul(up, wu_sb[ht % NPF][:, c * 128:(c + 1) * 128],
                                      xb[:, c * 512:(c + 1) * 512],
                                      start=(c == 0), stop=(c == NCH - 1)),
                            f"uu_d_{q}_{ht}" if c == NCH - 1 else None)
                act.wait(f"gg_d_{q}_{ht}")
                act.mark(act.activation(silu_f, gp, AF.Silu, scale=S127),
                         f"guev_{q}_{ht}_0")
                dve.wait(f"guev_{q}_{ht}_0", f"uu_d_{q}_{ht}")
                dve.mark(dve.scalar_tensor_tensor(
                    ht_sb[:, ht * 512:(ht + 1) * 512], up, S127, silu_f,
                    ALU.mult, ALU.mult), f"guev_{q}_{ht}_1")
            for tt in range(4):
                for dh2 in range(2):
                    ti = tt * 2 + dh2
                    p2 = ti % 2
                    tok0 = q * 512 + tt * 128
                    dp = AO[:, (ti % 4) * 512:(ti % 4) * 512 + 512]
                    qX.wait(f"ob_{q}_{ti-4}" if ti >= 4 else
                            (f"ob_{q-1}_{4+ti}" if q >= 1 else None))
                    qX.dma(x_f[ti % 4], x_t[tok0:tok0 + 128, dh2 * 512:(dh2 + 1) * 512],
                           f"ld_x_{q}_{ti}")
                    for hc in range(NHC):
                        if hc == 0:
                            pe.wait(f"guev_{q}_{NHC-1}_1", f"ld_wd{NHC-1}")
                            if ti >= 4:
                                pe.wait(f"dnev_{q}_{ti-4}")
                            elif q >= 1:
                                pe.wait(f"dnev_{q-1}_{4+ti}")
                        pe.mark(pe.matmul(dp, ht_sb[:, hc * 512 + tt * 128:hc * 512 + (tt + 1) * 128],
                                          wd_sb[:, hc * D + dh2 * 512:hc * D + (dh2 + 1) * 512],
                                          start=(hc == 0), stop=(hc == NHC - 1)),
                                f"dn_{q}_{ti}" if hc == NHC - 1 else None)
                    act.wait(f"dn_{q}_{ti}")
                    if ti >= 2:
                        act.wait(f"dnev_{q}_{ti-2}")
                    elif q >= 1:
                        act.wait(f"dnev_{q-1}_{6+ti}")
                    act.mark(act.activation(sig_f[p2], dp, AF.Sigmoid, scale=S127),
                             f"dsig_{q}_{ti}")
                    dve.wait(f"dsig_{q}_{ti}")
                    dve.mark(dve.scalar_tensor_tensor(gg_f[p2], dp, S127, sig_f[p2],
                                                      ALU.mult, ALU.mult),
                             f"dnev_{q}_{ti}")
                    ssl = slice((q * 4 + tt) * D + dh2 * 512, (q * 4 + tt) * D + (dh2 + 1) * 512)
                    dve.mark(dve.tensor_tensor(sgp_f[p2], s_sb[:, ssl], gg_f[p2],
                                               ALU.mult), None)
                    dve.wait(f"ld_x_{q}_{ti}")
                    if ti >= 2:
                        dve.wait(f"st_{q}_{ti-2}")
                    elif q >= 1:
                        dve.wait(f"st_{q-1}_{6+ti}")
                    dve.mark(dve.tensor_tensor(outb[p2], sgp_f[p2], x_f[ti % 4],
                                               ALU.add), f"ob_{q}_{ti}")
                    qO.wait(f"ob_{q}_{ti}")
                    qO.dma(out[tok0:tok0 + 128, dh2 * 512:(dh2 + 1) * 512], outb[p2],
                           f"st_{q}_{ti}")
                    st_count[0] += 1

        if not B.dry:
            B.engines["sq"].wait_ge(B.sems["qO"], B.counts["qO"])

        if KDBG:
            dump_map = {
                "x2f": (x2f, BF16), "aou": (aou, BF16), "s_sb": (s_sb[:], BF16),
                "v_sb": (v_sb, BF16), "srows": (srows[:], F32), "rall": (rall[:], BF16),
            }
            for nm in KDBG.split(","):
                ap, dt = dump_map[nm]
                if B.dry:
                    dtn = nc.dram_tensor("dbg_" + nm, list(ap.shape), dt,
                                         kind="ExternalOutput").ap()
                    B.marks["dbgt_" + nm] = dtn
                dtn = B.marks["dbgt_" + nm]
                sq.dma(dtn, ap, None)
            if not B.dry:
                B.engines["sq"].wait_ge(B.sems["sq"], B.counts["sq"])
            else:
                pass

    program()
    B.dry = False
    B.counts = {}
    B.hiwater = {}
    with nc.Block() as blk:
        @blk.sync
        def _(s):
            gx, sx = nc.gpsimd, nc.sync
            B.engines = {"pe": nc.tensor, "act": nc.scalar, "dve": nc.vector,
                         "gq": gx, "sq": sx, "qC": gx, "qW": gx, "qV": gx,
                         "qF": gx, "qD": gx, "qS2": sx, "qN": sx, "qX": sx,
                         "qO": sx, "qP": gx}
            program()
    return nc


def _blockT(w):
    o, i = w.shape
    return np.ascontiguousarray(
        w.reshape(o // 128, 128, i // 128, 128).transpose(0, 3, 2, 1).reshape(o // 128, 128, i))


_NC_CACHE = [None]

def kernel(**inp):
    f32 = np.float32
    xs = np.asarray(inp["L0_attn_x"], f32)
    ipw = np.asarray(inp["L0_attn_in_proj_w"], f32)
    ipb = np.asarray(inp["L0_attn_in_proj_b"], f32)
    opw = np.asarray(inp["L0_attn_out_proj_w"], f32)
    opb = np.asarray(inp["L0_attn_out_proj_b"], f32)
    lnw = np.asarray(inp["L0_attn_ln_w"], f32)
    lnb = np.asarray(inp["L0_attn_ln_b"], f32)
    w0 = np.asarray(inp["L0_attn_w0"], f32)
    w1 = np.asarray(inp["L0_attn_w1"], f32)
    w2 = np.asarray(inp["L0_attn_w2"], f32)
    wg = np.ascontiguousarray(np.asarray(inp["L0_ffn_w_gate"], np.int32))
    wu = np.ascontiguousarray(np.asarray(inp["L0_ffn_w_up"], np.int32))
    wd = np.ascontiguousarray(np.asarray(inp["L0_ffn_w_down"], np.int32))

    shared = {
        "winp": _blockT(ipw),
        "binp": np.ascontiguousarray(ipb.reshape(24, 128).T),
        "wout": _blockT(opw),
        "wv2_blk": np.ascontiguousarray(ipw[2048:].T.reshape(8, 128, D)),
        "wo2_blk": np.ascontiguousarray(opw.T.reshape(8, 128, D)),
        "bout_b": np.ascontiguousarray(np.broadcast_to(opb, (128, D))),
        "bv_b": np.ascontiguousarray(np.broadcast_to(ipb[2048:], (128, D))),
        "lnw_c": np.ascontiguousarray(lnw.reshape(8, 128).T),
        "lnb_c": np.ascontiguousarray(lnb.reshape(8, 128).T),
        "w02T": np.ascontiguousarray(np.concatenate(
            [np.stack([w0[h].T.reshape(2, 128, 256) for h in range(NL)]),
             np.stack([w2[h].T.reshape(2, 128, 256) for h in range(NL)])], axis=3)),
        "w1r": np.ascontiguousarray(w1.reshape(NL, 2, 128, 256)),
        "w0nat": np.ascontiguousarray(
            np.stack([w0[hh].T.reshape(2, 128, 256) for hh in range(NL)])),
        "w1nat": np.ascontiguousarray(
            np.stack([w1[hh].T.reshape(2, 128, 256) for hh in range(NL)])),
        "w2nat": np.ascontiguousarray(
            np.stack([w2[hh].T.reshape(2, 128, 256) for hh in range(NL)])),
        "wg_blk": _blockT(wg),
        "wu_blk": _blockT(wu),
        "wdT_blk": np.ascontiguousarray(wd.T.reshape(24, 128, D)),
        "id128": np.eye(128, dtype=f32),
        "e2c": np.ascontiguousarray(
            np.stack([(np.arange(128) < 64).astype(f32),
                      (np.arange(128) >= 64).astype(f32)], axis=1)),
        "ones1": np.ones((1, 128), f32),
        "selp": np.ascontiguousarray(np.stack([
            np.stack([((np.arange(128) < 64) if hh == 2 * jj else
                       ((np.arange(128) >= 64) if hh == 2 * jj + 1 else
                        np.zeros(128, bool))).astype(f32)
                      for hh in range(16)])
            for jj in range(8)])),
        "onescol": np.ones((128, 1), f32),
        "fconst": np.array([[EPS, np.log(KAPPA / T)]], f32),
    }
    if _NC_CACHE[0] is None:
        _NC_CACHE[0] = build_nc()
    nc = _NC_CACHE[0]
    in_maps = []
    for b in range(8):
        d = dict(shared)
        d["x_t"] = np.ascontiguousarray(xs[b])
        d["xT_blk"] = np.ascontiguousarray(xs[b].T.reshape(NCH, 128, T))
        in_maps.append(d)
    res = run_bass_kernel_spmd(nc, in_maps, core_ids=list(range(8)))
    return np.stack([r["out"] for r in res.results])


if __name__ == "__main__":
    nc = build_nc()
    print("build ok; sbuf remaining:", nc.sbuf_bytes_remaining,
          "psum banks remaining:", nc.psum_banks_remaining)


# version 29
# speedup vs baseline: 1.0783x; 1.0436x over previous
import os
import numpy as np
import ml_dtypes
import concourse.bass as bass
import concourse.mybir as mybir
from concourse.bass_utils import run_bass_kernel_spmd

F32 = mybir.dt.float32
BF16 = mybir.dt.bfloat16
I32 = mybir.dt.int32
AF = mybir.ActivationFunctionType
ALU = mybir.AluOpType

T = 2048
D = 1024
NCH = 8
NH = 16
HD = 64
NL = 4
HID = 3072
NHC = 24
EPS = 1e-5
S127 = 1.0 / 127.0
KAPPA = 2.0
VGRP = 65

KDBG = os.environ.get("KDBG", "")

class _Dummy:
    def then_inc(self, *a, **k):
        return self

_DUMMY = _Dummy()

class Builder:
    def __init__(self):
        self.dry = True
        self.counts = {}
        self.marks = {}
        self.hiwater = {}
        self.engines = {}
        self.sems = {}

class Prox:
    def __init__(self, B, key):
        self._B = B
        self._key = key

    def __getattr__(self, name):
        B = self._B
        key = self._key
        def call(*a, **k):
            if B.dry:
                return _DUMMY
            return getattr(B.engines[key], name)(*a, **k)
        return call

    def mark(self, ins, name=None):
        B = self._B
        key = self._key
        B.counts[key] = B.counts.get(key, 0) + 1
        if name is not None:
            if B.dry:
                assert name not in B.marks, f"dup mark {name}"
                B.marks[name] = (key, B.counts[key])
            else:
                assert B.marks[name] == (key, B.counts[key]), f"mark drift {name}"
        if not B.dry:
            ins.then_inc(B.sems[key], 1)

    def dma(self, out, in_, name=None, **kw):
        B = self._B
        key = self._key
        B.counts[key] = B.counts.get(key, 0) + 16
        if name is not None:
            if B.dry:
                assert name not in B.marks, f"dup mark {name}"
                B.marks[name] = (key, B.counts[key])
            else:
                assert B.marks[name] == (key, B.counts[key]), f"mark drift {name}"
        if not B.dry:
            B.engines[key].dma_start(out=out, in_=in_, **kw).then_inc(B.sems[key], 16)

    def wait(self, *names):
        B = self._B
        for nm in names:
            if nm is None:
                continue
            if B.dry:
                continue
            skey, val = B.marks[nm]
            self.wait_counts(skey, val)

    def wait_counts(self, skey, val):
        B = self._B
        if B.dry or val <= 0:
            return
        if skey == self._key:
            return
        hw = B.hiwater.get((self._key, skey), 0)
        if val <= hw:
            return
        B.hiwater[(self._key, skey)] = val
        B.engines[self._key].wait_ge(B.sems[skey], val)

    def wait_self(self):
        B = self._B
        if B.dry:
            return
        B.engines[self._key].wait_ge(B.sems[self._key], B.counts.get(self._key, 0))


def build_nc():
    nc = bass.Bass("TRN2", target_bir_lowering=False)

    DT = nc.dram_tensor
    x_t = DT("x_t", [T, D], F32, kind="ExternalInput").ap()
    xT_blk = DT("xT_blk", [NCH, 128, T], F32, kind="ExternalInput").ap()
    winp = DT("winp", [24, 128, D], F32, kind="ExternalInput").ap()
    binp = DT("binp", [128, 24], F32, kind="ExternalInput").ap()
    wout = DT("wout", [8, 128, D], F32, kind="ExternalInput").ap()
    wv2_blk = DT("wv2_blk", [8, 128, D], F32, kind="ExternalInput").ap()
    wo2_blk = DT("wo2_blk", [8, 128, D], F32, kind="ExternalInput").ap()
    bout_b = DT("bout_b", [128, D], F32, kind="ExternalInput").ap()
    bv_b = DT("bv_b", [128, D], F32, kind="ExternalInput").ap()
    lnw_c = DT("lnw_c", [128, NCH], F32, kind="ExternalInput").ap()
    lnb_c = DT("lnb_c", [128, NCH], F32, kind="ExternalInput").ap()
    w02T = DT("w02T", [NL, 2, 128, 512], F32, kind="ExternalInput").ap()
    w1r = DT("w1r", [NL, 2, 128, 256], F32, kind="ExternalInput").ap()
    w0nat = DT("w0nat", [NL, 2, 128, 256], F32, kind="ExternalInput").ap()
    w1nat = DT("w1nat", [NL, 2, 128, 256], F32, kind="ExternalInput").ap()
    w2nat = DT("w2nat", [NL, 2, 128, 256], F32, kind="ExternalInput").ap()
    wg_blk = DT("wg_blk", [NHC, 128, D], I32, kind="ExternalInput").ap()
    wu_blk = DT("wu_blk", [NHC, 128, D], I32, kind="ExternalInput").ap()
    wdT_blk = DT("wdT_blk", [NHC, 128, D], I32, kind="ExternalInput").ap()
    id128 = DT("id128", [128, 128], F32, kind="ExternalInput").ap()
    e2c = DT("e2c", [128, 2], F32, kind="ExternalInput").ap()
    ones1 = DT("ones1", [1, 128], F32, kind="ExternalInput").ap()
    selp = DT("selp", [8, 16, 128], F32, kind="ExternalInput").ap()
    onescol = DT("onescol", [128, 1], F32, kind="ExternalInput").ap()
    fconst = DT("fconst", [1, 2], F32, kind="ExternalInput").ap()

    out = DT("out", [T, D], F32, kind="ExternalOutput").ap()
    xn_spill = DT("xn_spill", [128, NCH * T], BF16, kind="Internal").ap()

    S = nc.alloc_sbuf_tensor
    id_sb = S("id_sb", [128, 128], BF16)
    e2_sb = S("e2_sb", [128, 2], BF16)
    ones1_sb = S("ones1_sb", [1, 128], BF16)
    onescol_sb = S("onescol_sb", [128, 1], BF16)
    selp_sb = S("selp_sb", [16, 8 * 128], BF16)
    binp_sb = S("binp_sb", [128, 24], F32)
    qb_sb = S("qb_sb", [128, 8], F32)
    lnw_sb = S("lnw_sb", [128, NCH], F32)
    lnb_sb = S("lnb_sb", [128, NCH], F32)
    bacc = S("bacc", [128, 4], F32)
    bacc_bf = S("bacc_bf", [128, 2], BF16)
    nrow = S("nrow", [1, 16], F32)
    nrow_bf = S("nrow_bf", [1, 16], BF16)
    srow_tmp = S("srow_tmp", [1, 512], F32)
    srows = S("srows", [16, 2048], F32)
    rall = S("rall", [16, 2048], BF16)
    fc_sb = S("fc_sb", [1, 2], F32)

    s_sb = S("s_sb", [128, 16 * D], BF16)

    AR_N = 75008
    arena = S("arena", [128, AR_N], BF16)

    class Carver:
        def __init__(self):
            self.off = 0
        def take(self, n, name=""):
            n = (n + 15) & ~15
            ap = arena[:, self.off:self.off + n]
            self.off += n
            assert self.off <= AR_N, f"arena overflow at {name}: {self.off}"
            return ap
        def take_f32(self, n, name=""):
            return self.take(2 * n, name).bitcast(F32)

    cv = Carver()
    off0 = cv.off
    x2f = cv.take(NCH * T, "x2f")
    offX = cv.off

    cv.off = offX
    xbf = cv.take(NCH * T, "xbf")
    xnf_a = cv.take(NCH * T, "xnf_a")
    sqs = cv.take(T, "sqs")
    tmpA = cv.take_f32(512, "tmpA")
    rows = cv.take(3 * 2048, "rows")
    dstat = cv.take_f32(1536, "dstat")
    offA = cv.off

    cv.off = offX
    v_sb = cv.take(16 * 16 * VGRP, "v_sb")
    aou = cv.take(NCH * T, "aou")
    qk_sb = cv.take(2 * T, "qk")
    exp_sb = cv.take(4 * T, "exp")
    ubias = cv.take_f32(16, "ubias")
    rb_bf = cv.take(512, "rb")
    rb_bf_b = cv.take(512, "rb2")
    cv.off = max(cv.off, offA)
    bvb = cv.take_f32(D, "bvb")
    wv_sb = cv.take(NCH * 512, "wv")
    wqk_sb = cv.take(2 * 2048, "wqk")
    offB = cv.off
    assert offB <= AR_N

    cv.off = offX
    xt_sb = [cv.take(16 * 256, f"xt{i}") for i in range(2)]
    w02_sb = cv.take(NL * 2 * 512, "w02")
    w1r_sb = cv.take(NL * 2 * 256, "w1r")
    w0n_sb = cv.take_f32(NL * 2 * 256, "w0n")
    w1n_sb = cv.take_f32(NL * 2 * 256, "w1n")
    w2n_sb = cv.take_f32(NL * 2 * 256, "w2n")
    hid_t = cv.take(8 * 256, "hid_t")
    pg_t = cv.take(8 * 512, "pg_t")
    sg_f = [cv.take_f32(256, f"sg{i}") for i in range(2)]
    hh_f = [cv.take_f32(256, f"hh{i}") for i in range(2)]
    g_f = [cv.take_f32(256, f"g{i}") for i in range(2)]
    dh_f = [cv.take_f32(256, f"dh{i}") for i in range(2)]
    v2_f = [cv.take_f32(256, f"v2{i}") for i in range(2)]
    u_f = [cv.take_f32(256, f"u{i}") for i in range(2)]
    wn_bf = cv.take(3 * 2 * 256, "wn_bf")
    wnT_bf = cv.take(3 * 2 * 256, "wnT_bf")
    hq_f = [cv.take_f32(512, f"hq{i}") for i in range(2)]
    z_sb = cv.take(2 * 1024, "z_sb")
    offC = cv.off

    cv.off = off0
    ht_sb = cv.take(NHC * 512, "ht")
    wd_sb = cv.take(NHC * D, "wd")
    xnf_d = [cv.take(NCH * 512, f"xnf_d{i}") for i in range(2)]
    silu_f = cv.take_f32(512, "silu")
    sig_f = [cv.take_f32(512, f"sig{i}") for i in range(2)]
    gg_f = [cv.take_f32(512, f"gg{i}") for i in range(2)]
    sgp_f = [cv.take_f32(512, f"sgp{i}") for i in range(2)]
    x_f = [cv.take_f32(512, f"xf{i}") for i in range(4)]
    outb = [cv.take_f32(512, f"ob{i}") for i in range(2)]
    offD = cv.off
    cv.off = max(offC, offD)
    wg_sb = [cv.take(D, f"wg{i}") for i in range(4)]
    wu_sb = [cv.take(D, f"wu{i}") for i in range(4)]
    assert max(offA, offB, offC, offD, cv.off) <= AR_N

    bob = wqk_sb[:, 0:2048].bitcast(F32)
    rb_bf2 = [rb_bf, rb_bf_b]

    P = nc.alloc_psum_tensor
    SC = P("SC", [128, 2048], F32)
    SC_A = SC[:, 0:1024]
    SC_B = SC[:, 1024:2048]
    AO = P("AO", [128, 2048], F32)

    B = Builder()
    STREAMS = ("pe", "act", "dve", "gq", "sq",
               "qC", "qW", "qV", "qF", "qD",
               "qS2", "qN", "qX", "qO",
               "qP")
    for key in STREAMS:
        B.sems[key] = nc.alloc_semaphore("sem_" + key)
    pe = Prox(B, "pe")
    act = Prox(B, "act")
    dve = Prox(B, "dve")
    gq = Prox(B, "gq")
    sq = Prox(B, "sq")
    qC = Prox(B, "qC")
    qW = Prox(B, "qW")
    qV = Prox(B, "qV")
    qF = Prox(B, "qF")
    qD = Prox(B, "qD")
    qS2 = Prox(B, "qS2")
    qN = Prox(B, "qN")
    qX = Prox(B, "qX")
    qO = Prox(B, "qO")
    pool = Prox(B, "qP")
    ALL = (pe, act, dve, gq, sq, qC, qW, qV, qF, qD, qS2, qN, qX, qO, pool)

    def alias(dst, src):
        if B.dry:
            assert dst not in B.marks, f"dup alias {dst}"
            B.marks[dst] = B.marks[src]
        else:
            assert B.marks[dst] == B.marks[src]

    def sync_all(engines=ALL):
        snap = dict(B.counts)
        for e in engines:
            for k in ("pe", "act", "dve", "gq", "sq"):
                if k != e._key and snap.get(k):
                    e.wait_counts(k, snap[k])

    KDBG2 = os.environ.get("KDBG2", "")

    def dump_now(name, ap, dt):
        if name not in KDBG2.split(","):
            return
        key = "dbg2t_" + name
        if B.dry:
            B.marks[key] = nc.dram_tensor("dbg2_" + name, list(ap.shape), dt,
                                          kind="ExternalOutput").ap()
        snap = dict(B.counts)
        for k in ("pe", "act", "dve", "gq"):
            if snap.get(k):
                sq.wait_counts(k, snap[k])
        sq.dma(B.marks[key], ap, None)

    def program():
        qC.dma(id_sb[:], id128, "ld_id")
        sq.dma(fc_sb[:], fconst, "ld_fc")
        qC.dma(e2_sb[:], e2c, "ld_e2")
        qC.dma(ones1_sb[:], ones1, "ld_ones1")
        qC.dma(onescol_sb[:], onescol, "ld_onescol")
        qC.dma(selp_sb[:].rearrange("h (j p) -> h j p", j=8), selp.rearrange("j h p -> h j p"), "ld_selp")
        sq.dma(binp_sb[:], binp, "ld_binp")
        sq.dma(lnw_sb[:], lnw_c, "ld_lnw")
        sq.dma(lnb_sb[:], lnb_c, "ld_lnb")
        sq.dma(bvb, bv_b, "ld_bvb")
        for c in range(NCH):
            gq.dma(xbf[:, c * T:(c + 1) * T], xT_blk[c], f"ld_xbf{c}")
        for c in range(NCH):
            qV.dma(wv_sb[:, c * 512:(c + 1) * 512], wv2_blk[c][:, 0:512],
                   f"ld_wv0_{c}")

        dve.wait("ld_bvb")
        dve.mark(dve.tensor_single_scalar(qb_sb[:], binp_sb[:, 0:8], 0.125, ALU.mult),
                 "qb_ready")

        mu_sl = rows[:1, 0:2048]
        var_sl = rows[:1, 2048:4096]
        r_sl = rows[:1, 4096:6144]

        def ln_stats(src, tag):
            for c in range(NCH):
                act.wait(f"src_{tag}{c}")
                if c > 0:
                    act.wait_counts("pe", B.counts["pe"])
                act.mark(act.activation(sqs, src[:, c * T:(c + 1) * T], AF.Square),
                         f"sq_{tag}{c}")
                for t4 in range(4):
                    s2p = (SC_A if t4 < 2 else SC_B)[0:1, (t4 % 2) * 512:(t4 % 2) * 512 + 512]
                    pe.wait(f"src_{tag}{c}")
                    if c == 0 and t4 == 0:
                        pe.wait(f"src_{tag}{NCH-1}", "ld_onescol", "ld_ones1")
                        pe.wait_counts("dve", B.counts["dve"])
                    pe.mark(pe.matmul(AO[0:1, t4 * 512:(t4 + 1) * 512], onescol_sb[:],
                                      src[:, c * T + t4 * 512:c * T + (t4 + 1) * 512],
                                      start=(c == 0), stop=(c == NCH - 1)),
                            f"s1_{tag}{t4}" if c == NCH - 1 else None)
                    pe.wait(f"sq_{tag}{c}")
                    pe.mark(pe.matmul(s2p, onescol_sb[:],
                                      sqs[:, t4 * 512:(t4 + 1) * 512],
                                      start=(c == 0), stop=(c == NCH - 1)),
                            f"s2_{tag}{t4}" if c == NCH - 1 else None)
            for t4 in range(4):
                sl = slice(t4 * 512, (t4 + 1) * 512)
                s2p = (SC_A if t4 < 2 else SC_B)[0:1, (t4 % 2) * 512:(t4 % 2) * 512 + 512]
                dve.wait(f"s1_{tag}{t4}", f"s2_{tag}{t4}")
                dve.mark(dve.tensor_single_scalar(mu_sl[:, sl], AO[0:1, sl], 1.0 / D,
                                                  ALU.mult), f"mu_{tag}{t4}")
                dve.wait_self()
                dve.mark(dve.tensor_tensor(var_sl[:, sl], mu_sl[:, sl], mu_sl[:, sl],
                                           ALU.mult), None)
                dve.wait_self()
                dve.mark(dve.scalar_tensor_tensor(var_sl[:, sl], s2p, 1.0 / D,
                                                  var_sl[:, sl], ALU.mult, ALU.subtract),
                         f"var_{tag}{t4}")
                act.wait(f"var_{tag}{t4}", "ld_fc")
                act.mark(act.activation(r_sl[:, sl], var_sl[:, sl], AF.Ln,
                                        bias=fc_sb[:, 0:1]), None)
                act.wait_self()
                act.mark(act.activation(r_sl[:, sl], r_sl[:, sl], AF.Exp, scale=-0.5),
                         f"r_{tag}{t4}")

        for c in range(NCH):
            alias(f"src_x{c}", f"ld_xbf{NCH-1}")
        ln_stats(xbf, "x")
        u_row = dstat[0:1, 0:512]
        rec_row = dstat[0:1, 512:1024]
        t_row = dstat[0:1, 1024:1536]
        for t4 in range(4):
            sl = slice(t4 * 512, (t4 + 1) * 512)
            act.wait(f"var_x{t4}", "ld_fc")
            act.mark(act.activation(u_row, var_sl[:, sl], AF.Ln,
                                    bias=fc_sb[:, 0:1]), None)
            act.wait_self()
            act.mark(act.activation(rec_row, u_row, AF.Exp, scale=-1.0),
                     f"rec_{t4}")
            dve.wait(f"rec_{t4}")
            dve.mark(dve.tensor_tensor(t_row, var_sl[:, sl], rec_row, ALU.mult),
                     f"t2_{t4}")
            act.wait(f"t2_{t4}")
            act.mark(act.activation(u_row, t_row, AF.Ln, bias=fc_sb[:, 0:1]),
                     None)
            act.wait_self()
            act.wait(f"mub_x{t4}")
            act.mark(act.activation(mu_sl[:, sl], u_row, AF.Exp, scale=-0.5),
                     f"r2_{t4}")
            pe.wait(f"mu_x{t4}", "mu_x1")
            if t4 >= 1:
                pe.wait(f"xn_{NCH-1}_{t4-1}")
            pe.mark(pe.matmul(AO[:, 0:512],
                              ones1_sb[:], mu_sl[:, sl], start=True, stop=True),
                    f"mub_x{t4}")
            pe.wait(f"r_x{t4}")
            pe.mark(pe.matmul(AO[:, 512:1024], ones1_sb[:], r_sl[:, sl],
                              start=True, stop=True), f"rb_x{t4}")
            r2b = SC_A[:, (t4 % 2) * 512:(t4 % 2) * 512 + 512]
            pe.wait(f"r2_{t4}", "var_x0", "var_x1")
            if t4 >= 2:
                pe.wait(f"x2_{NCH-1}_{t4-2}")
            pe.mark(pe.matmul(r2b, ones1_sb[:], mu_sl[:, sl],
                              start=True, stop=True), f"r2b_{t4}")
            for c in range(NCH):
                csl = slice(c * T + t4 * 512, c * T + (t4 + 1) * 512)
                dve.wait(f"mub_x{t4}", f"rb_x{t4}", "ld_bvb")
                dve.mark(dve.tensor_tensor(tmpA, xbf[:, csl], AO[:, 0:512], ALU.subtract),
                         None)
                dve.mark(dve.tensor_tensor(xnf_a[:, csl], tmpA, AO[:, 512:1024], ALU.mult),
                         f"xn_{c}_{t4}")
            for c in range(NCH):
                csl = slice(c * T + t4 * 512, c * T + (t4 + 1) * 512)
                dve.wait(f"r2b_{t4}")
                dve.mark(dve.scalar_tensor_tensor(tmpA, xnf_a[:, csl],
                                                  lnw_sb[:, c:c + 1], r2b,
                                                  ALU.mult, ALU.mult), None)
                dve.mark(dve.tensor_scalar_add(x2f[:, csl], tmpA, lnb_sb[:, c:c + 1]),
                         f"x2_{c}_{t4}")
        for c in range(NCH):
            sq.wait(f"xn_{c}_3")
            sq.dma(xn_spill[:, c * T:(c + 1) * T], xnf_a[:, c * T:(c + 1) * T],
                   f"spill_{c}")
        sync_all()
        dump_now("xbf", xbf, BF16)
        dump_now("xnf", xnf_a, BF16)
        dump_now("rows", rows, BF16)

        for js in range(2):
            if js == 1:
                for c in range(NCH):
                    qV.wait(f"vp_15_0")
                    qV.dma(wv_sb[:, c * 512:(c + 1) * 512], wv2_blk[c][:, 512:1024],
                           f"ld_wv1_{c}")
            for ct in range(16):
                pb = AO[:, (ct % 4) * 512:(ct % 4) * 512 + 512]
                for c in range(NCH):
                    if c == 0:
                        pe.wait(f"ld_wv{js}_{NCH-1}" if js == 1 else f"ld_wv0_{NCH-1}")
                        if ct >= 4 or js == 1:
                            pv = (js, ct - 4) if ct >= 4 else (0, 12 + ct)
                            pe.wait(f"vev_{pv[1]}_{pv[0]}")
                    pe.mark(pe.matmul(pb, x2f[:, c * T + ct * 128:c * T + (ct + 1) * 128],
                                      wv_sb[:, c * 512:(c + 1) * 512],
                                      start=(c == 0), stop=(c == NCH - 1)),
                            f"vp_{ct}_{js}" if c == NCH - 1 else None)
                dve.wait(f"vp_{ct}_{js}", "ld_bvb")
                dve.mark(dve.tensor_tensor(rb_bf, pb,
                                           bvb[:, js * 512:(js + 1) * 512], ALU.add),
                         f"vev_{ct}_{js}")
                vdst = v_sb[:, ct * 1040 + js * 8 * VGRP:ct * 1040 + (js + 1) * 8 * VGRP]
                dve.mark(dve.tensor_copy(
                    vdst.rearrange("p (g c) -> p g c", c=VGRP)[:, :, 0:64],
                    rb_bf.rearrange("p (g c) -> p g c", c=64)), f"vevs_{ct}_{js}")
        dve.mark(dve.memset(v_sb.rearrange("p (g c) -> p g c", c=VGRP)[:, :, 64:65], 1.0),
                 "v_ones")

        def emit_qw_loads(jj):
            par2 = jj % 2
            qW.wait(f"qkp_{jj-2}_1_3" if jj >= 2 else None)
            qW.dma(wqk_sb[:, par2 * 2048:par2 * 2048 + 1024], winp[jj], f"ld_wq{jj}")
            qW.dma(wqk_sb[:, par2 * 2048 + 1024:par2 * 2048 + 2048], winp[8 + jj],
                   f"ld_wk{jj}")

        def emit_in_proj(jj):
            par2 = jj % 2
            for part in range(2):
                wofs = par2 * 2048 + part * 1024
                for t4 in range(4):
                    pb = (SC_A if part == 0 else SC_B)[:, (t4 % 2) * 512:(t4 % 2) * 512 + 512]
                    for c in range(NCH):
                        if c == 0:
                            pe.wait(f"ld_wq{jj}" if part == 0 else f"ld_wk{jj}")
                            if jj >= 1 and t4 < 2:
                                pe.wait(f"exp_{jj-1}_1_15_{part}")
                            if t4 >= 2:
                                pe.wait(f"qkev_{jj}_{part}_{t4-2}")
                            elif jj >= 1:
                                pe.wait(f"qkev_{jj-1}_{part}_{t4+2}")
                        pe.mark(pe.matmul(pb,
                                          wqk_sb[:, wofs + c * 128:wofs + (c + 1) * 128],
                                          x2f[:, c * T + t4 * 512:c * T + (t4 + 1) * 512],
                                          start=(c == 0), stop=(c == NCH - 1)),
                                f"qkp_{jj}_{part}_{t4}" if c == NCH - 1 else None)
                    act.wait(f"qkp_{jj}_{part}_{t4}", "qb_ready")
                    if part == 0 and t4 == 0 and jj >= 1:
                        act.wait(f"bsq_{jj-1}")
                    dst = qk_sb[:, part * T + t4 * 512:part * T + (t4 + 1) * 512]
                    if part == 0:
                        ins = act.activation(dst, pb, AF.Identity,
                                             bias=qb_sb[:, jj:jj + 1], scale=0.125)
                    else:
                        ins = act.activation(dst, pb, AF.Identity,
                                             bias=binp_sb[:, 8 + jj:9 + jj])
                    act.mark(ins, f"qkev_{jj}_{part}_{t4}")

        def emit_bounds(jj):
            dve.wait(f"qkev_{jj}_0_3")
            dve.mark(dve.scalar_tensor_tensor(exp_sb[:, 0:T], qk_sb[:, 0:T], 1.0,
                                              qk_sb[:, 0:T], ALU.mult, ALU.mult,
                                              accum_out=bacc[:, 0:1]), None)
            dve.wait(f"qkev_{jj}_1_3")
            dve.mark(dve.scalar_tensor_tensor(exp_sb[:, 0:T], qk_sb[:, T:2 * T], 1.0,
                                              qk_sb[:, T:2 * T], ALU.mult, ALU.mult,
                                              accum_out=bacc[:, 1:2]), f"bsq_{jj}")
            dve.wait_self()
            dve.mark(dve.tensor_copy(bacc_bf[:], bacc[:, 0:2]), f"bacc_{jj}")
            pe.wait(f"bacc_{jj}", "ld_e2")
            if jj >= 1:
                pe.wait(f"ubias_{jj-1}", f"srow_{jj-1}_1_3", f"aoev_{jj-1}_1")
            pe.mark(pe.matmul(AO[0:1, 0:2], bacc_bf[:, 0:1], e2_sb[:],
                              start=True, stop=True), None)
            pe.mark(pe.matmul(AO[0:1, 512:514], bacc_bf[:, 1:2], e2_sb[:],
                              start=True, stop=True), f"bmm_{jj}")
            dve.wait(f"bmm_{jj}")
            dve.mark(dve.tensor_copy(nrow[:, 8:10], AO[0:1, 512:514]), None)
            dve.wait_self()
            dve.mark(dve.tensor_tensor(nrow[:, 0:2], AO[0:1, 0:2], nrow[:, 8:10],
                                       ALU.mult), f"bm1_{jj}")
            act.wait(f"bm1_{jj}")
            act.mark(act.activation(nrow[:, 2:4], nrow[:, 0:2], AF.Ln), None)
            act.wait("ld_fc")
            act.wait_self()
            act.mark(act.activation(nrow[:, 4:6], nrow[:, 2:4], AF.Exp, scale=0.5,
                                    bias=fc_sb[:, 1:2]), f"bc_{jj}")
            dve.wait(f"bc_{jj}")
            dve.mark(dve.tensor_single_scalar(nrow_bf[:, 0:2], nrow[:, 4:6], -1.0,
                                              ALU.mult), f"bneg_{jj}")
            pe.wait(f"bneg_{jj}")
            pe.mark(pe.matmul(AO[:, 4:6], ones1_sb[:], nrow_bf[:, 0:2],
                              start=True, stop=True), f"bbc_{jj}")
            dve.wait(f"bbc_{jj}")
            dve.mark(dve.tensor_copy(ubias[:, 2 * jj:2 * jj + 2], AO[:, 4:6]),
                     f"ubias_{jj}")

        emit_qw_loads(0)
        emit_in_proj(0)
        emit_bounds(0)
        for j in range(8):
            if j + 1 < 8:
                emit_qw_loads(j + 1)
            for h in range(2):
                hg = 2 * j + h
                po = h * 64
                for rnd in range(2):
                    base = rnd * 8
                    for C in range(base, base + 8):
                        for half in range(2):
                            piece = SC_A if half == 0 else SC_B
                            if C >= 1:
                                pe.wait(f"exp_{j}_{h}_{C-1}_{half}")
                            elif h == 1:
                                pe.wait(f"exp_{j}_0_15_{half}")
                            for th in range(2):
                                t4 = half * 2 + th
                                pe.mark(pe.matmul(
                                    piece[:, th * 512:(th + 1) * 512],
                                    qk_sb[po:po + 64, T + C * 128:T + (C + 1) * 128],
                                    qk_sb[po:po + 64, t4 * 512:(t4 + 1) * 512],
                                    start=True, stop=True),
                                    f"sc_{j}_{h}_{C}_{half}" if th == 1 else None)
                            ubcol = 2 * max(j - 1, 0) + h
                            act.wait(f"sc_{j}_{h}_{C}_{half}", f"ubias_{max(j-1, 0)}")
                            act.mark(act.activation(
                                s_sb[:, (C % 8) * T + half * 1024:(C % 8) * T + (half + 1) * 1024],
                                piece[:], AF.Exp, bias=ubias[:, ubcol:ubcol + 1]),
                                f"exp_{j}_{h}_{C}_{half}")
                    for Ca in range(base, base + 8):
                        for t4 in range(4):
                            if Ca == base and t4 == 0:
                                pe.wait(f"exp_{j}_{h}_{base + 7}_1")
                            pe.wait(f"exp_{j}_{h}_{Ca}_{t4 // 2}")
                            if Ca == 0 and t4 == 0:
                                pe.wait("v_ones", "vevs_15_1")
                                if h == 0:
                                    pe.wait(f"ubias_{j}")
                                if not (j == 0 and h == 0):
                                    ph = (j, 0) if h == 1 else (j - 1, 1)
                                    pe.wait(f"aoev_{ph[0]}_{ph[1]}",
                                            f"srow_{ph[0]}_{ph[1]}_3")
                            pe.mark(pe.matmul(
                                AO[0:VGRP, t4 * 512:(t4 + 1) * 512],
                                v_sb[:, Ca * 1040 + hg * VGRP:Ca * 1040 + (hg + 1) * VGRP],
                                s_sb[:, (Ca % 8) * T + t4 * 512:(Ca % 8) * T + (t4 + 1) * 512],
                                start=(Ca == 0), stop=(Ca == 15)),
                                f"av_{j}_{h}_{Ca}" if t4 == 3 else None)
                for t4 in range(4):
                    act.wait(f"av_{j}_{h}_15")
                    act.mark(act.activation(
                        aou[h * 64:(h + 1) * 64, :][:, j * T + t4 * 512:j * T + (t4 + 1) * 512],
                        AO[0:64, t4 * 512:(t4 + 1) * 512], AF.Identity),
                        f"aoev_{j}_{h}" if t4 == 3 else None)
                    dve.wait(f"av_{j}_{h}_15")
                    if t4 > 0 or h > 0 or j > 0:
                        dve.wait_counts("qS2", B.counts.get("qS2", 0))
                    dve.mark(dve.tensor_copy(srow_tmp[:],
                                             AO[64:65, t4 * 512:(t4 + 1) * 512]),
                             f"srow_{j}_{h}_{t4}")
                    qS2.wait(f"srow_{j}_{h}_{t4}")
                    qS2.dma(srows[hg:hg + 1, t4 * 512:(t4 + 1) * 512], srow_tmp[:],
                           f"sdma_{j}_{h}_{t4}")
            if j + 1 < 8:
                emit_in_proj(j + 1)
                emit_bounds(j + 1)

        act.wait("sdma_7_1_3")
        act.mark(act.activation(srows[:], srows[:], AF.Ln), None)
        act.wait_self()
        act.mark(act.activation(rall[:], srows[:], AF.Exp, scale=-1.0), "rall")
        _prev_rb = []
        for t4 in range(4):
            for j in range(8):
                pe.wait("rall", "ld_selp")
                if len(_prev_rb) >= 2:
                    pe.wait(f"rbc_{_prev_rb[-2][0]}_{_prev_rb[-2][1]}")
                else:
                    pe.wait("srow_7_1_3", f"aoev_7_1", f"ubias_7")
                k = t4 * 8 + j
                slot = AO[:, 1024 + (k % 2) * 512:1536 + (k % 2) * 512]
                _prev_rb.append((j, t4))
                pe.mark(pe.matmul(slot, selp_sb[:, j * 128:(j + 1) * 128],
                                  rall[:, t4 * 512:(t4 + 1) * 512],
                                  start=True, stop=True), f"rbmm_{j}_{t4}")
                dve.wait(f"rbmm_{j}_{t4}", f"aoev_{j}_1")
                dve.mark(dve.tensor_copy(rb_bf2[k % 2], slot), f"rbc_{j}_{t4}")
                sl = slice(j * T + t4 * 512, j * T + (t4 + 1) * 512)
                dve.mark(dve.tensor_tensor(aou[:, sl], aou[:, sl], rb_bf2[k % 2],
                                           ALU.mult), f"aon_{j}_{t4}")
        dump_now("x2fB", x2f, BF16)
        dump_now("vB", v_sb, BF16)
        dump_now("aouB", aou, BF16)
        dump_now("qk7", qk_sb, BF16)
        qS2.wait("qkp_7_1_3")
        qS2.dma(bob, bout_b, "ld_bob")
        for jh in range(2):
            wob = wv_sb if jh == 0 else qk_sb
            for c in range(NCH):
                qV.wait("vp_15_1" if jh == 0 else "av_7_1_15")
                qV.dma(wob[:, c * 512:(c + 1) * 512],
                       wo2_blk[c][:, jh * 512:(jh + 1) * 512], f"ld_wo{jh}_{c}")
            for ct in range(16):
                pb = (SC_A if ct % 2 == 0 else SC_B)[:, 0:512]
                for c in range(NCH):
                    pe.wait(f"aon_{c}_{ct // 4}")
                    if c == 0:
                        pe.wait(f"ld_wo{jh}_{NCH-1}")
                        if jh == 0 and ct < 2:
                            pe.wait(f"exp_7_1_15_{ct}")
                        if ct >= 2:
                            pe.wait(f"opev_{ct-2}_{jh}")
                        elif jh == 1:
                            pe.wait(f"opev_{14+ct}_0")
                    pe.mark(pe.matmul(pb, aou[:, c * T + ct * 128:c * T + (ct + 1) * 128],
                                      wob[:, c * 512:(c + 1) * 512],
                                      start=(c == 0), stop=(c == NCH - 1)),
                            f"opp_{ct}_{jh}" if c == NCH - 1 else None)
                dve.wait(f"opp_{ct}_{jh}", "ld_bob")
                dve.mark(dve.tensor_tensor(
                    s_sb[:, ct * D + jh * 512:ct * D + (jh + 1) * 512],
                    pb, bob[:, jh * 512:(jh + 1) * 512], ALU.add),
                    f"opev_{ct}_{jh}")

        sync_all()
        dump_now("sB2", s_sb[:], BF16)
        NPF = 4

        def ffn_wload(q, ht):
            li = q * NHC + ht
            if li >= 4 * NHC:
                return
            if li >= NPF:
                pq, pt = divmod(li - NPF, NHC)
                qF.wait(f"gg_d_{pq}_{pt}", f"uu_d_{pq}_{pt}")
            qF.dma(wg_sb[ht % NPF], wg_blk[ht], f"ld_wg_{q}_{ht}")
            qF.dma(wu_sb[ht % NPF], wu_blk[ht], f"ld_wu_{q}_{ht}")

        gq.dma(w02_sb.rearrange("p (h c f) -> p h c f", h=NL, c=2), w02T.rearrange("h c p f -> p h c f"), "ld_w02")
        gq.dma(w1r_sb.rearrange("p (h c f) -> p h c f", h=NL, c=2), w1r.rearrange("h c p f -> p h c f"), "ld_w1r")
        for li in range(NPF):
            ffn_wload(li // NHC, li % NHC)
        sq.dma(w0n_sb.rearrange("p (h c f) -> p h c f", h=NL, c=2), w0nat.rearrange("h c p f -> p h c f"), "ld_w0n")
        sq.dma(w1n_sb.rearrange("p (h c f) -> p h c f", h=NL, c=2), w1nat.rearrange("h c p f -> p h c f"), "ld_w1n")
        sq.dma(w2n_sb.rearrange("p (h c f) -> p h c f", h=NL, c=2), w2nat.rearrange("h c p f -> p h c f"), "ld_w2n")

        for h in range(NL):
            hb = h % 2
            for dc in range(2):
                gc = 2 * h + dc
                for ct in range(16):
                    pe.wait("ld_id")
                    if h >= 2:
                        pe.wait(f"lact_done_{h-2}")
                    xti = dc * 16 + ct
                    xts = SC_A[0:128, 0:64] if xti % 2 == 0 else SC_A[0:128, 512:576]
                    if xti >= 2:
                        pe.wait(f"xtev_{h}_{(xti - 2) // 16}_{(xti - 2) % 16}")
                    elif xti == 0 and h >= 1:
                        pe.wait(f"z_1_{h-1}_3")
                    pe.mark(pe.transpose(xts.bitcast(BF16),
                                         x2f[:, gc * T + ct * 128:gc * T + (ct + 1) * 128],
                                         id_sb[:]), f"xtp_{h}_{dc}_{ct}")
                    dve.wait(f"xtp_{h}_{dc}_{ct}")
                    dve.mark(dve.tensor_copy(
                        xt_sb[hb][:, ct * 256 + dc * 128:ct * 256 + (dc + 1) * 128],
                        xts.bitcast(BF16)), f"xtev_{h}_{dc}_{ct}")
            if h == 0:
                dump_now("xt0", xt_sb[0], BF16)
            for n in range(2):
                nh = f"{n}_{h}"
                def emit_dw(ct):
                    gt = n * 8 + ct
                    gtt = slice(gt * 256, (gt + 1) * 256)
                    pe.wait(f"dvt_{nh}_{ct}", f"le_{nh}_{ct}", f"xtev_{h}_1_15")
                    if ct == 0 and prevnh is not None:
                        pe.wait(f"oqev_{prevnh}_7")
                    for ds in range(2):
                        pe.mark(pe.matmul(AO[:, ds * 512:(ds + 1) * 512],
                                          xt_sb[hb][:, gt * 256 + ds * 128:gt * 256 + (ds + 1) * 128],
                                          pg_t[:, ct * 512:(ct + 1) * 512],
                                          start=(ct == 0), stop=(ct == 7)),
                                f"dw02_{nh}_{ds}" if ct == 7 else None)
                    for es in range(2):
                        pe.mark(pe.matmul(AO[:, 1024 + es * 512:1024 + es * 512 + 256],
                                          hid_t[:, ct * 256 + es * 128:ct * 256 + (es + 1) * 128],
                                          xt_sb[hb][:, gtt],
                                          start=(ct == 0), stop=(ct == 7)),
                                f"dw1_{nh}_{es}" if ct == 7 else None)

                for ct in range(8):
                    cp = ct % 2
                    tok = n * 1024 + ct * 128
                    ggp = SC_A[:, cp * 512:(cp + 1) * 512]
                    mp = SC_B[:, cp * 512:cp * 512 + 256]
                    prevnh = None if (n == 0 and h == 0) else (f"1_{h-1}" if n == 0 else f"0_{h}")
                    for dc in range(2):
                        gc = 2 * h + dc
                        if dc == 0:
                            pe.wait("ld_w02", "ld_w1r")
                            if ct >= 2:
                                pe.wait(f"dvele_{nh}_{ct-2}")
                            else:
                                pe.wait(f"xtev_{h}_1_15")
                                if prevnh is not None:
                                    pe.wait(f"z_{prevnh}_3")
                        pe.mark(pe.matmul(ggp,
                                          x2f[:, gc * T + tok:gc * T + tok + 128],
                                          w02_sb[:, (h * 2 + dc) * 512:(h * 2 + dc + 1) * 512],
                                          start=(dc == 0), stop=(dc == 1)),
                                f"gg_{nh}_{ct}" if dc == 1 else None)
                        pe.mark(pe.matmul(mp,
                                          x2f[:, gc * T + tok:gc * T + tok + 128],
                                          w1r_sb[:, (h * 2 + dc) * 256:(h * 2 + dc + 1) * 256],
                                          start=(dc == 0), stop=(dc == 1)),
                                f"m_{nh}_{ct}" if dc == 1 else None)
                    if ct >= 1:
                        emit_dw(ct - 1)
                    ctt = slice(ct * 256, (ct + 1) * 256)
                    pool_prev = (f"le_{nh}_{ct-2}" if ct >= 2 else
                                 (f"le_{prevnh}_{6+ct}" if prevnh is not None else None))
                    dvt_prev = (f"dvt_{nh}_{ct-2}" if ct >= 2 else
                                (f"dvt_{prevnh}_{6+ct}" if prevnh is not None else None))
                    act.wait(f"gg_{nh}_{ct}")
                    act.wait(pool_prev)
                    act.mark(act.activation(sg_f[cp], ggp[:, 0:256], AF.Sigmoid),
                             f"sg_{nh}_{ct}")
                    if ct >= 1:
                        ccp = (ct - 1) % 2
                        dve.wait(f"le_{nh}_{ct-1}")
                        dve.mark(dve.tensor_tensor(
                            pg_t[:, (ct - 1) * 512:(ct - 1) * 512 + 256],
                            dh_f[ccp], u_f[ccp], ALU.mult), f"dvt_{nh}_{ct-1}")
                    dve.wait(f"sg_{nh}_{ct}")
                    dve.wait(pool_prev, dvt_prev)
                    dve.mark(dve.tensor_tensor(hh_f[cp], ggp[:, 0:256], sg_f[cp],
                                               ALU.mult), None)
                    dve.mark(dve.tensor_copy(g_f[cp], ggp[:, 256:512]), None)
                    dve.wait(f"m_{nh}_{ct}")
                    dve.mark(dve.tensor_tensor(pg_t[:, ct * 512 + 256:(ct + 1) * 512],
                                               mp, hh_f[cp],
                                               ALU.mult), None)
                    dve.mark(dve.tensor_tensor(dh_f[cp], mp, g_f[cp],
                                               ALU.mult), None)
                    dve.mark(dve.scalar_tensor_tensor(v2_f[cp], ggp[:, 0:256], 1.0,
                                                      hh_f[cp], ALU.add, ALU.subtract),
                             f"dvele_{nh}_{ct}")
                    pool.wait(f"dvele_{nh}_{ct}", dvt_prev)
                    if ct < 2 and prevnh is not None:
                        pool.wait(f"dw1_{prevnh}_1")
                    pool.mark(pool.tensor_tensor(hid_t[:, ctt], hh_f[cp], g_f[cp],
                                                 ALU.mult), None)
                    pool.mark(pool.tensor_tensor(u_f[cp], sg_f[cp], v2_f[cp], ALU.mult),
                              f"le_{nh}_{ct}")
                    if ct == 7:
                        dve.wait(f"le_{nh}_7")
                        dve.mark(dve.tensor_tensor(pg_t[:, 7 * 512:7 * 512 + 256],
                                                   dh_f[cp], u_f[cp],
                                                   ALU.mult), f"dvt_{nh}_7")

                emit_dw(7)
                if n == 0 and h == 0:
                    dump_now("hid0", hid_t, BF16)
                    dump_now("pg0", pg_t, BF16)
                    dump_now("wnT0", wnT_bf, BF16)
                upd = [("dw02", 0, w0n_sb, 0), ("dw1", 1024 + 512, w1n_sb, 512),
                       ("dw02", 256, w2n_sb, 1024)]
                for k, (nm, po_, nat, wofs) in enumerate(upd):
                    for cs in range(2):
                        dve.wait(f"{nm}_{nh}_{cs}", "ld_w2n")
                        psl = (AO[:, cs * 512 + po_:cs * 512 + po_ + 256] if nm == "dw02"
                               else AO[:, 1024 + cs * 512:1024 + cs * 512 + 256])
                        dve.mark(dve.tensor_tensor(
                            wnT_bf[:, wofs + cs * 256:wofs + (cs + 1) * 256],
                            psl,
                            nat[:, (h * 2 + cs) * 256:(h * 2 + cs + 1) * 256],
                            ALU.add), f"wnev_{nh}_{2 * k + cs}")
                for ec in range(2):
                    for c2 in range(2):
                        cp = (ec * 2 + c2) % 2
                        gqp = SC_A[:, cp * 512:(cp + 1) * 512]
                        g2p = SC_B[:, cp * 512:(cp + 1) * 512]
                        csl = slice(n * 1024 + c2 * 512, n * 1024 + (c2 + 1) * 512)
                        for ds in range(2):
                            gc = 2 * h + ds
                            if ds == 0:
                                pe.wait(f"wnev_{nh}_5")
                                if ec * 2 + c2 >= 2:
                                    pe.wait(f"z_{nh}_{ec * 2 + c2 - 2}")
                            pe.mark(pe.matmul(gqp,
                                              wnT_bf[:, 0 + ds * 256 + ec * 128:0 + ds * 256 + (ec + 1) * 128],
                                              x2f[:, gc * T + csl.start:gc * T + csl.stop],
                                              start=(ds == 0), stop=(ds == 1)),
                                    f"gq_{nh}_{ec}_{c2}" if ds == 1 else None)
                            pe.mark(pe.matmul(g2p,
                                              wnT_bf[:, 1024 + ds * 256 + ec * 128:1024 + ds * 256 + (ec + 1) * 128],
                                              x2f[:, gc * T + csl.start:gc * T + csl.stop],
                                              start=(ds == 0), stop=(ds == 1)),
                                    f"gq2_{nh}_{ec}_{c2}" if ds == 1 else None)
                        act.wait(f"gq_{nh}_{ec}_{c2}")
                        act.mark(act.activation(hq_f[cp], gqp, AF.Sigmoid), f"hqs_{nh}_{ec}_{c2}")
                        dve.wait(f"hqs_{nh}_{ec}_{c2}")
                        dve.mark(dve.tensor_tensor(hq_f[cp], gqp, hq_f[cp], ALU.mult),
                                 None)
                        dve.wait(f"gq2_{nh}_{ec}_{c2}")
                        dve.mark(dve.tensor_tensor(
                            z_sb[:, ec * 1024 + c2 * 512:ec * 1024 + (c2 + 1) * 512],
                            hq_f[cp], g2p, ALU.mult), f"z_{nh}_{ec * 2 + c2}")
                if n == 0 and h == 0:
                    dump_now("z0", z_sb, BF16)
                for ct in range(8):
                    op = AO[:, (ct % 4) * 512:(ct % 4) * 512 + 256]
                    for ec in range(2):
                        if ec == 0:
                            pe.wait(f"z_{nh}_{ct // 4}", f"z_{nh}_{2 + ct // 4}")
                            if ct >= 4:
                                pe.wait(f"oqev_{nh}_{ct-4}")
                            elif not (n == 0 and h == 0):
                                pn = f"1_{h-1}" if n == 0 else f"0_{h}"
                                pe.wait(f"oqev_{pn}_{4 + ct}")
                        pe.mark(pe.matmul(op,
                                          z_sb[:, ec * 1024 + ct * 128:ec * 1024 + (ct + 1) * 128],
                                          wnT_bf[:, 512 + ec * 256:512 + (ec + 1) * 256],
                                          start=(ec == 0), stop=(ec == 1)),
                                f"oq_{nh}_{ct}" if ec == 1 else None)
                    dve.wait(f"oq_{nh}_{ct}", f"opev_{n * 8 + ct}_1")
                    ssl = slice((n * 8 + ct) * D + h * 256, (n * 8 + ct) * D + h * 256 + 256)
                    dve.mark(dve.tensor_tensor(s_sb[:, ssl], op, s_sb[:, ssl], ALU.add),
                             f"oqev_{nh}_{ct}")
            if True:
                alias(f"lact_done_{h}", f"oqev_1_{h}_7")

        sync_all()
        for hc in range(NHC):
            qD.dma(wd_sb[:, hc * D:(hc + 1) * D], wdT_blk[hc], f"ld_wd{hc}")
        st_count = [0]
        for q in range(4):
            xb = xnf_d[q % 2]
            for c in range(NCH):
                qN.wait(f"spill_{NCH-1}")
                if q > 1:
                    qN.wait(f"uu_d_{q-2}_{NHC-1}")
                qN.dma(xb[:, c * 512:(c + 1) * 512],
                       xn_spill[:, c * T + q * 512:c * T + (q + 1) * 512],
                       f"ld_xn_{q}_{c}")
            for ht in range(NHC):
                li = q * NHC + ht
                ffn_wload((li + NPF) // NHC, (li + NPF) % NHC)
                p = ht % 2
                gp = SC_A[:, p * 512:(p + 1) * 512]
                up = SC_B[:, p * 512:(p + 1) * 512]
                for c in range(NCH):
                    if c == 0:
                        pe.wait(f"ld_wg_{q}_{ht}", f"ld_wu_{q}_{ht}", f"ld_xn_{q}_{NCH-1}")
                        if ht >= 2:
                            pe.wait(f"guev_{q}_{ht-2}_1")
                        elif q >= 1:
                            pe.wait(f"guev_{q-1}_{NHC-2+ht}_1")
                    pe.mark(pe.matmul(gp, wg_sb[ht % NPF][:, c * 128:(c + 1) * 128],
                                      xb[:, c * 512:(c + 1) * 512],
                                      start=(c == 0), stop=(c == NCH - 1)),
                            f"gg_d_{q}_{ht}" if c == NCH - 1 else None)
                    pe.mark(pe.matmul(up, wu_sb[ht % NPF][:, c * 128:(c + 1) * 128],
                                      xb[:, c * 512:(c + 1) * 512],
                                      start=(c == 0), stop=(c == NCH - 1)),
                            f"uu_d_{q}_{ht}" if c == NCH - 1 else None)
                act.wait(f"gg_d_{q}_{ht}")
                act.mark(act.activation(silu_f, gp, AF.Silu, scale=S127),
                         f"guev_{q}_{ht}_0")
                dve.wait(f"guev_{q}_{ht}_0", f"uu_d_{q}_{ht}")
                dve.mark(dve.scalar_tensor_tensor(
                    ht_sb[:, ht * 512:(ht + 1) * 512], up, S127, silu_f,
                    ALU.mult, ALU.mult), f"guev_{q}_{ht}_1")
            for tt in range(4):
                for dh2 in range(2):
                    ti = tt * 2 + dh2
                    p2 = ti % 2
                    tok0 = q * 512 + tt * 128
                    dp = AO[:, (ti % 4) * 512:(ti % 4) * 512 + 512]
                    qX.wait(f"ob_{q}_{ti-4}" if ti >= 4 else
                            (f"ob_{q-1}_{4+ti}" if q >= 1 else None))
                    qX.dma(x_f[ti % 4], x_t[tok0:tok0 + 128, dh2 * 512:(dh2 + 1) * 512],
                           f"ld_x_{q}_{ti}")
                    for hc in range(NHC):
                        if hc == 0:
                            pe.wait(f"guev_{q}_{NHC-1}_1", f"ld_wd{NHC-1}")
                            if ti >= 4:
                                pe.wait(f"dnev_{q}_{ti-4}")
                            elif q >= 1:
                                pe.wait(f"dnev_{q-1}_{4+ti}")
                        pe.mark(pe.matmul(dp, ht_sb[:, hc * 512 + tt * 128:hc * 512 + (tt + 1) * 128],
                                          wd_sb[:, hc * D + dh2 * 512:hc * D + (dh2 + 1) * 512],
                                          start=(hc == 0), stop=(hc == NHC - 1)),
                                f"dn_{q}_{ti}" if hc == NHC - 1 else None)
                    act.wait(f"dn_{q}_{ti}")
                    if ti >= 2:
                        act.wait(f"dnev_{q}_{ti-2}")
                    elif q >= 1:
                        act.wait(f"dnev_{q-1}_{6+ti}")
                    act.mark(act.activation(sig_f[p2], dp, AF.Sigmoid, scale=S127),
                             f"dsig_{q}_{ti}")
                    dve.wait(f"dsig_{q}_{ti}")
                    dve.mark(dve.scalar_tensor_tensor(gg_f[p2], dp, S127, sig_f[p2],
                                                      ALU.mult, ALU.mult),
                             f"dnev_{q}_{ti}")
                    ssl = slice((q * 4 + tt) * D + dh2 * 512, (q * 4 + tt) * D + (dh2 + 1) * 512)
                    dve.mark(dve.tensor_tensor(sgp_f[p2], s_sb[:, ssl], gg_f[p2],
                                               ALU.mult), None)
                    dve.wait(f"ld_x_{q}_{ti}")
                    if ti >= 2:
                        dve.wait(f"st_{q}_{ti-2}")
                    elif q >= 1:
                        dve.wait(f"st_{q-1}_{6+ti}")
                    dve.mark(dve.tensor_tensor(outb[p2], sgp_f[p2], x_f[ti % 4],
                                               ALU.add), f"ob_{q}_{ti}")
                    qO.wait(f"ob_{q}_{ti}")
                    qO.dma(out[tok0:tok0 + 128, dh2 * 512:(dh2 + 1) * 512], outb[p2],
                           f"st_{q}_{ti}")
                    st_count[0] += 1

        if not B.dry:
            B.engines["sq"].wait_ge(B.sems["qO"], B.counts["qO"])

        if KDBG:
            dump_map = {
                "x2f": (x2f, BF16), "aou": (aou, BF16), "s_sb": (s_sb[:], BF16),
                "v_sb": (v_sb, BF16), "srows": (srows[:], F32), "rall": (rall[:], BF16),
            }
            for nm in KDBG.split(","):
                ap, dt = dump_map[nm]
                if B.dry:
                    dtn = nc.dram_tensor("dbg_" + nm, list(ap.shape), dt,
                                         kind="ExternalOutput").ap()
                    B.marks["dbgt_" + nm] = dtn
                dtn = B.marks["dbgt_" + nm]
                sq.dma(dtn, ap, None)
            if not B.dry:
                B.engines["sq"].wait_ge(B.sems["sq"], B.counts["sq"])
            else:
                pass

    program()
    B.dry = False
    B.counts = {}
    B.hiwater = {}
    with nc.Block() as blk:
        @blk.sync
        def _(s):
            gx, sx = nc.gpsimd, nc.sync
            B.engines = {"pe": nc.tensor, "act": nc.scalar, "dve": nc.vector,
                         "gq": gx, "sq": sx, "qC": gx, "qW": gx, "qV": gx,
                         "qF": gx, "qD": gx, "qS2": sx, "qN": sx, "qX": sx,
                         "qO": sx, "qP": gx}
            program()
    return nc


def _blockT(w):
    o, i = w.shape
    return np.ascontiguousarray(
        w.reshape(o // 128, 128, i // 128, 128).transpose(0, 3, 2, 1).reshape(o // 128, 128, i))


_NC_CACHE = [None]

def kernel(**inp):
    f32 = np.float32
    xs = np.asarray(inp["L0_attn_x"], f32)
    ipw = np.asarray(inp["L0_attn_in_proj_w"], f32)
    ipb = np.asarray(inp["L0_attn_in_proj_b"], f32)
    opw = np.asarray(inp["L0_attn_out_proj_w"], f32)
    opb = np.asarray(inp["L0_attn_out_proj_b"], f32)
    lnw = np.asarray(inp["L0_attn_ln_w"], f32)
    lnb = np.asarray(inp["L0_attn_ln_b"], f32)
    w0 = np.asarray(inp["L0_attn_w0"], f32)
    w1 = np.asarray(inp["L0_attn_w1"], f32)
    w2 = np.asarray(inp["L0_attn_w2"], f32)
    wg = np.ascontiguousarray(np.asarray(inp["L0_ffn_w_gate"], np.int32))
    wu = np.ascontiguousarray(np.asarray(inp["L0_ffn_w_up"], np.int32))
    wd = np.ascontiguousarray(np.asarray(inp["L0_ffn_w_down"], np.int32))

    shared = {
        "winp": _blockT(ipw),
        "binp": np.ascontiguousarray(ipb.reshape(24, 128).T),
        "wout": _blockT(opw),
        "wv2_blk": np.ascontiguousarray(ipw[2048:].T.reshape(8, 128, D)),
        "wo2_blk": np.ascontiguousarray(opw.T.reshape(8, 128, D)),
        "bout_b": np.ascontiguousarray(np.broadcast_to(opb, (128, D))),
        "bv_b": np.ascontiguousarray(np.broadcast_to(ipb[2048:], (128, D))),
        "lnw_c": np.ascontiguousarray(lnw.reshape(8, 128).T),
        "lnb_c": np.ascontiguousarray(lnb.reshape(8, 128).T),
        "w02T": np.ascontiguousarray(np.concatenate(
            [np.stack([w0[h].T.reshape(2, 128, 256) for h in range(NL)]),
             np.stack([w2[h].T.reshape(2, 128, 256) for h in range(NL)])], axis=3)),
        "w1r": np.ascontiguousarray(w1.reshape(NL, 2, 128, 256)),
        "w0nat": np.ascontiguousarray(
            np.stack([w0[hh].T.reshape(2, 128, 256) for hh in range(NL)])),
        "w1nat": np.ascontiguousarray(
            np.stack([w1[hh].T.reshape(2, 128, 256) for hh in range(NL)])),
        "w2nat": np.ascontiguousarray(
            np.stack([w2[hh].T.reshape(2, 128, 256) for hh in range(NL)])),
        "wg_blk": _blockT(wg),
        "wu_blk": _blockT(wu),
        "wdT_blk": np.ascontiguousarray(wd.T.reshape(24, 128, D)),
        "id128": np.eye(128, dtype=f32),
        "e2c": np.ascontiguousarray(
            np.stack([(np.arange(128) < 64).astype(f32),
                      (np.arange(128) >= 64).astype(f32)], axis=1)),
        "ones1": np.ones((1, 128), f32),
        "selp": np.ascontiguousarray(np.stack([
            np.stack([((np.arange(128) < 64) if hh == 2 * jj else
                       ((np.arange(128) >= 64) if hh == 2 * jj + 1 else
                        np.zeros(128, bool))).astype(f32)
                      for hh in range(16)])
            for jj in range(8)])),
        "onescol": np.ones((128, 1), f32),
        "fconst": np.array([[EPS, np.log(KAPPA / T)]], f32),
    }
    if _NC_CACHE[0] is None:
        _NC_CACHE[0] = build_nc()
    nc = _NC_CACHE[0]
    in_maps = []
    for b in range(8):
        d = dict(shared)
        d["x_t"] = np.ascontiguousarray(xs[b])
        d["xT_blk"] = np.ascontiguousarray(xs[b].T.reshape(NCH, 128, T))
        in_maps.append(d)
    res = run_bass_kernel_spmd(nc, in_maps, core_ids=list(range(8)))
    return np.stack([r["out"] for r in res.results])


if __name__ == "__main__":
    nc = build_nc()
    print("build ok; sbuf remaining:", nc.sbuf_bytes_remaining,
          "psum banks remaining:", nc.psum_banks_remaining)


# version 35
# speedup vs baseline: 1.1077x; 1.0273x over previous
import os
import numpy as np
import ml_dtypes
import concourse.bass as bass
import concourse.mybir as mybir
from concourse.bass_utils import run_bass_kernel_spmd

F32 = mybir.dt.float32
BF16 = mybir.dt.bfloat16
I32 = mybir.dt.int32
AF = mybir.ActivationFunctionType
ALU = mybir.AluOpType

T = 2048
D = 1024
NCH = 8
NH = 16
HD = 64
NL = 4
HID = 3072
NHC = 24
EPS = 1e-5
S127 = 1.0 / 127.0
KAPPA = 2.0
VGRP = 65

KDBG = os.environ.get("KDBG", "")

class _Dummy:
    def then_inc(self, *a, **k):
        return self

_DUMMY = _Dummy()

class Builder:
    def __init__(self):
        self.dry = True
        self.counts = {}
        self.marks = {}
        self.hiwater = {}
        self.engines = {}
        self.sems = {}

class Prox:
    def __init__(self, B, key):
        self._B = B
        self._key = key

    def __getattr__(self, name):
        B = self._B
        key = self._key
        def call(*a, **k):
            if B.dry:
                return _DUMMY
            return getattr(B.engines[key], name)(*a, **k)
        return call

    def mark(self, ins, name=None):
        B = self._B
        key = self._key
        B.counts[key] = B.counts.get(key, 0) + 1
        if name is not None:
            if B.dry:
                assert name not in B.marks, f"dup mark {name}"
                B.marks[name] = (key, B.counts[key])
            else:
                assert B.marks[name] == (key, B.counts[key]), f"mark drift {name}"
        if not B.dry:
            ins.then_inc(B.sems[key], 1)

    def dma(self, out, in_, name=None, **kw):
        B = self._B
        key = self._key
        B.counts[key] = B.counts.get(key, 0) + 16
        if name is not None:
            if B.dry:
                assert name not in B.marks, f"dup mark {name}"
                B.marks[name] = (key, B.counts[key])
            else:
                assert B.marks[name] == (key, B.counts[key]), f"mark drift {name}"
        if not B.dry:
            B.engines[key].dma_start(out=out, in_=in_, **kw).then_inc(B.sems[key], 16)

    def wait(self, *names):
        B = self._B
        for nm in names:
            if nm is None:
                continue
            if B.dry:
                continue
            skey, val = B.marks[nm]
            self.wait_counts(skey, val)

    def wait_counts(self, skey, val):
        B = self._B
        if B.dry or val <= 0:
            return
        if skey == self._key:
            return
        hw = B.hiwater.get((self._key, skey), 0)
        if val <= hw:
            return
        B.hiwater[(self._key, skey)] = val
        B.engines[self._key].wait_ge(B.sems[skey], val)

    def wait_self(self):
        B = self._B
        if B.dry:
            return
        B.engines[self._key].wait_ge(B.sems[self._key], B.counts.get(self._key, 0))


def build_nc():
    nc = bass.Bass("TRN2", target_bir_lowering=False)

    DT = nc.dram_tensor
    x_t = DT("x_t", [T, D], F32, kind="ExternalInput").ap()
    xT_blk = DT("xT_blk", [NCH, 128, T], F32, kind="ExternalInput").ap()
    winp = DT("winp", [24, 128, D], F32, kind="ExternalInput").ap()
    binp = DT("binp", [128, 24], F32, kind="ExternalInput").ap()
    wout = DT("wout", [8, 128, D], F32, kind="ExternalInput").ap()
    wv2_blk = DT("wv2_blk", [8, 128, D], F32, kind="ExternalInput").ap()
    wo2_blk = DT("wo2_blk", [8, 128, D], F32, kind="ExternalInput").ap()
    bout_b = DT("bout_b", [128, D], F32, kind="ExternalInput").ap()
    bv_b = DT("bv_b", [128, D], F32, kind="ExternalInput").ap()
    lnw_c = DT("lnw_c", [128, NCH], F32, kind="ExternalInput").ap()
    lnb_c = DT("lnb_c", [128, NCH], F32, kind="ExternalInput").ap()
    w02T = DT("w02T", [NL, 2, 128, 512], F32, kind="ExternalInput").ap()
    w1r = DT("w1r", [NL, 2, 128, 256], F32, kind="ExternalInput").ap()
    w0nat = DT("w0nat", [NL, 2, 128, 256], F32, kind="ExternalInput").ap()
    w1nat = DT("w1nat", [NL, 2, 128, 256], F32, kind="ExternalInput").ap()
    w2nat = DT("w2nat", [NL, 2, 128, 256], F32, kind="ExternalInput").ap()
    wg_blk = DT("wg_blk", [NHC, 128, D], I32, kind="ExternalInput").ap()
    wu_blk = DT("wu_blk", [NHC, 128, D], I32, kind="ExternalInput").ap()
    wdT_blk = DT("wdT_blk", [NHC, 128, D], I32, kind="ExternalInput").ap()
    id128 = DT("id128", [128, 128], F32, kind="ExternalInput").ap()
    e2c = DT("e2c", [128, 2], F32, kind="ExternalInput").ap()
    ones1 = DT("ones1", [1, 128], F32, kind="ExternalInput").ap()
    selp = DT("selp", [8, 16, 128], F32, kind="ExternalInput").ap()
    onescol = DT("onescol", [128, 1], F32, kind="ExternalInput").ap()
    fconst = DT("fconst", [1, 2], F32, kind="ExternalInput").ap()

    out = DT("out", [T, D], F32, kind="ExternalOutput").ap()
    xn_spill = DT("xn_spill", [128, NCH * T], BF16, kind="Internal").ap()

    S = nc.alloc_sbuf_tensor
    id_sb = S("id_sb", [128, 128], BF16)
    e2_sb = S("e2_sb", [128, 2], BF16)
    ones1_sb = S("ones1_sb", [1, 128], BF16)
    onescol_sb = S("onescol_sb", [128, 1], BF16)
    selp_sb = S("selp_sb", [16, 8 * 128], BF16)
    binp_sb = S("binp_sb", [128, 24], F32)
    qb_sb = S("qb_sb", [128, 8], F32)
    lnw_sb = S("lnw_sb", [128, NCH], F32)
    lnb_sb = S("lnb_sb", [128, NCH], F32)
    bacc = S("bacc", [128, 4], F32)
    bacc_bf = S("bacc_bf", [128, 4], BF16)
    nrow = S("nrow", [1, 16], F32)
    nrow_bf = S("nrow_bf", [1, 16], BF16)
    srow_tmp = S("srow_tmp", [1, 1024], F32)
    srows = S("srows", [16, 2048], F32)
    rall = S("rall", [16, 2048], BF16)
    fc_sb = S("fc_sb", [1, 2], F32)

    s_sb = S("s_sb", [128, 16 * D], BF16)

    AR_N = 75008
    arena = S("arena", [128, AR_N], BF16)

    class Carver:
        def __init__(self):
            self.off = 0
        def take(self, n, name=""):
            n = (n + 15) & ~15
            ap = arena[:, self.off:self.off + n]
            self.off += n
            assert self.off <= AR_N, f"arena overflow at {name}: {self.off}"
            return ap
        def take_f32(self, n, name=""):
            return self.take(2 * n, name).bitcast(F32)

    cv = Carver()
    off0 = cv.off
    x2f = cv.take(NCH * T, "x2f")
    offX = cv.off

    cv.off = offX
    xbf = cv.take(NCH * T, "xbf")
    xnf_a = cv.take(NCH * T, "xnf_a")
    sqs = cv.take(T, "sqs")
    tmpA = cv.take_f32(512, "tmpA")
    rows = cv.take(3 * 2048, "rows")
    dstat = cv.take_f32(1536, "dstat")
    offA = cv.off

    cv.off = offX
    v_sb = cv.take(16 * 16 * VGRP, "v_sb")
    aou = cv.take(NCH * T, "aou")
    qk_sb = cv.take(2 * T, "qk")
    exp_sb = cv.take(4 * T, "exp")
    ubias = cv.take_f32(16, "ubias")
    rb_bf = cv.take(512, "rb")
    rb_bf_b = cv.take(512, "rb2")
    cv.off = max(cv.off, offA)
    bvb = cv.take_f32(D, "bvb")
    wv_sb = cv.take(NCH * 512, "wv")
    wqk_sb = cv.take(2 * 2048, "wqk")
    offB = cv.off
    assert offB <= AR_N

    cv.off = offX
    xt_sb = [cv.take(16 * 256, f"xt{i}") for i in range(2)]
    w02_sb = cv.take(NL * 2 * 512, "w02")
    w1r_sb = cv.take(NL * 2 * 256, "w1r")
    w0n_sb = cv.take_f32(NL * 2 * 256, "w0n")
    w1n_sb = cv.take_f32(NL * 2 * 256, "w1n")
    w2n_sb = cv.take_f32(NL * 2 * 256, "w2n")
    hid_t = cv.take(8 * 256, "hid_t")
    pg_t = cv.take(8 * 512, "pg_t")
    sg_f = [cv.take_f32(256, f"sg{i}") for i in range(2)]
    hh_f = [cv.take_f32(256, f"hh{i}") for i in range(2)]
    g_f = [cv.take_f32(256, f"g{i}") for i in range(2)]
    dh_f = [cv.take_f32(256, f"dh{i}") for i in range(2)]
    v2_f = [cv.take_f32(256, f"v2{i}") for i in range(2)]
    u_f = [cv.take_f32(256, f"u{i}") for i in range(2)]
    wn_bf = cv.take(3 * 2 * 256, "wn_bf")
    wnT_bf = cv.take(3 * 2 * 256, "wnT_bf")
    hq_f = [cv.take_f32(512, f"hq{i}") for i in range(2)]
    z_sb = cv.take(2 * 1024, "z_sb")
    offC = cv.off

    cv.off = off0
    ht_sb = cv.take(NHC * 512, "ht")
    wd_sb = cv.take(NHC * D, "wd")
    xnf_d = [cv.take(NCH * 512, f"xnf_d{i}") for i in range(2)]
    silu_f = cv.take_f32(512, "silu")
    sig_f = [cv.take_f32(512, f"sig{i}") for i in range(2)]
    gg_f = [cv.take_f32(512, f"gg{i}") for i in range(2)]
    sgp_f = [cv.take_f32(512, f"sgp{i}") for i in range(2)]
    x_f = [cv.take_f32(512, f"xf{i}") for i in range(4)]
    outb = [cv.take_f32(512, f"ob{i}") for i in range(2)]
    offD = cv.off
    cv.off = max(offC, offD)
    wg_sb = [cv.take(D, f"wg{i}") for i in range(4)]
    wu_sb = [cv.take(D, f"wu{i}") for i in range(4)]
    assert max(offA, offB, offC, offD, cv.off) <= AR_N

    bob = wqk_sb[:, 0:2048].bitcast(F32)
    rb_bf2 = [rb_bf, rb_bf_b]

    P = nc.alloc_psum_tensor
    SC = P("SC", [128, 2048], F32)
    SC_A = SC[:, 0:1024]
    SC_B = SC[:, 1024:2048]
    AO = P("AO", [128, 2048], F32)

    B = Builder()
    STREAMS = ("pe", "act", "dve", "gq", "sq",
               "qC", "qW", "qV", "qF", "qD",
               "qS2", "qN", "qX", "qO",
               "qP")
    for key in STREAMS:
        B.sems[key] = nc.alloc_semaphore("sem_" + key)
    pe = Prox(B, "pe")
    act = Prox(B, "act")
    dve = Prox(B, "dve")
    gq = Prox(B, "gq")
    sq = Prox(B, "sq")
    qC = Prox(B, "qC")
    qW = Prox(B, "qW")
    qV = Prox(B, "qV")
    qF = Prox(B, "qF")
    qD = Prox(B, "qD")
    qS2 = Prox(B, "qS2")
    qN = Prox(B, "qN")
    qX = Prox(B, "qX")
    qO = Prox(B, "qO")
    pool = Prox(B, "qP")
    ALL = (pe, act, dve, gq, sq, qC, qW, qV, qF, qD, qS2, qN, qX, qO, pool)

    def alias(dst, src):
        if B.dry:
            assert dst not in B.marks, f"dup alias {dst}"
            B.marks[dst] = B.marks[src]
        else:
            assert B.marks[dst] == B.marks[src]

    def sync_all(engines=ALL):
        snap = dict(B.counts)
        for e in engines:
            for k in ("pe", "act", "dve", "gq", "sq"):
                if k != e._key and snap.get(k):
                    e.wait_counts(k, snap[k])

    KDBG2 = os.environ.get("KDBG2", "")

    def dump_now(name, ap, dt):
        if name not in KDBG2.split(","):
            return
        key = "dbg2t_" + name
        if B.dry:
            B.marks[key] = nc.dram_tensor("dbg2_" + name, list(ap.shape), dt,
                                          kind="ExternalOutput").ap()
        snap = dict(B.counts)
        for k in ("pe", "act", "dve", "gq"):
            if snap.get(k):
                sq.wait_counts(k, snap[k])
        sq.dma(B.marks[key], ap, None)

    def program():
        qC.dma(id_sb[:], id128, "ld_id")
        sq.dma(fc_sb[:], fconst, "ld_fc")
        qC.dma(e2_sb[:], e2c, "ld_e2")
        qC.dma(ones1_sb[:], ones1, "ld_ones1")
        qC.dma(onescol_sb[:], onescol, "ld_onescol")
        sq.dma(binp_sb[:], binp, "ld_binp")
        sq.dma(lnw_sb[:], lnw_c, "ld_lnw")
        sq.dma(lnb_sb[:], lnb_c, "ld_lnb")
        sq.dma(bvb, bv_b, "ld_bvb")
        for c in range(NCH):
            (qC if c < 4 else gq).dma(xbf[:, c * T:(c + 1) * T], xT_blk[c],
                                      f"ld_xbf{c}")

        dve.wait("ld_bvb")
        dve.mark(dve.tensor_single_scalar(qb_sb[:], binp_sb[:, 0:8], 0.125, ALU.mult),
                 "qb_ready")

        mu_sl = rows[:1, 0:2048]
        var_sl = rows[:1, 2048:4096]
        r_sl = rows[:1, 4096:6144]

        def ln_stats(src, tag):
            for c in range(NCH):
                act.wait(f"src_{tag}{c}")
                if c > 0:
                    act.wait_counts("pe", B.counts["pe"])
                act.mark(act.activation(sqs, src[:, c * T:(c + 1) * T], AF.Square),
                         f"sq_{tag}{c}")
                for t4 in range(4):
                    s2p = (SC_A if t4 < 2 else SC_B)[0:1, (t4 % 2) * 512:(t4 % 2) * 512 + 512]
                    pe.wait(f"src_{tag}{c}")
                    if c == 0 and t4 == 0:
                        pe.wait("ld_onescol", "ld_ones1")
                        pe.wait_counts("dve", B.counts["dve"])
                    pe.mark(pe.matmul(AO[0:1, t4 * 512:(t4 + 1) * 512], onescol_sb[:],
                                      src[:, c * T + t4 * 512:c * T + (t4 + 1) * 512],
                                      start=(c == 0), stop=(c == NCH - 1)),
                            f"s1_{tag}{t4}" if c == NCH - 1 else None)
                    pe.wait(f"sq_{tag}{c}")
                    pe.mark(pe.matmul(s2p, onescol_sb[:],
                                      sqs[:, t4 * 512:(t4 + 1) * 512],
                                      start=(c == 0), stop=(c == NCH - 1)),
                            f"s2_{tag}{t4}" if c == NCH - 1 else None)
            for t4 in range(4):
                sl = slice(t4 * 512, (t4 + 1) * 512)
                s2p = (SC_A if t4 < 2 else SC_B)[0:1, (t4 % 2) * 512:(t4 % 2) * 512 + 512]
                dve.wait(f"s1_{tag}{t4}", f"s2_{tag}{t4}")
                dve.mark(dve.tensor_single_scalar(mu_sl[:, sl], AO[0:1, sl], 1.0 / D,
                                                  ALU.mult), f"mu_{tag}{t4}")
                dve.wait_self()
                dve.mark(dve.tensor_tensor(var_sl[:, sl], mu_sl[:, sl], mu_sl[:, sl],
                                           ALU.mult), None)
                dve.wait_self()
                dve.mark(dve.scalar_tensor_tensor(var_sl[:, sl], s2p, 1.0 / D,
                                                  var_sl[:, sl], ALU.mult, ALU.subtract),
                         f"var_{tag}{t4}")
                act.wait(f"var_{tag}{t4}", "ld_fc")
                act.mark(act.activation(r_sl[:, sl], var_sl[:, sl], AF.Ln,
                                        bias=fc_sb[:, 0:1]), None)
                act.wait_self()
                act.mark(act.activation(r_sl[:, sl], r_sl[:, sl], AF.Exp, scale=-0.5),
                         f"r_{tag}{t4}")

        for c in range(NCH):
            alias(f"src_x{c}", "ld_xbf3" if c < 4 else f"ld_xbf{NCH-1}")
        for c in range(NCH):
            qV.dma(wv_sb[:, c * 512:(c + 1) * 512], wv2_blk[c][:, 0:512],
                   f"ld_wv0_{c}")
        qC.dma(selp_sb[:].rearrange("h (j p) -> h j p", j=8),
               selp.rearrange("j h p -> h j p"), "ld_selp")
        ln_stats(xbf, "x")
        u_row = dstat[0:1, 0:512]
        rec_row = dstat[0:1, 512:1024]
        t_row = dstat[0:1, 1024:1536]
        for t4 in range(4):
            sl = slice(t4 * 512, (t4 + 1) * 512)
            act.wait(f"var_x{t4}", "ld_fc")
            act.mark(act.activation(u_row, var_sl[:, sl], AF.Ln,
                                    bias=fc_sb[:, 0:1]), None)
            act.wait_self()
            act.mark(act.activation(rec_row, u_row, AF.Exp, scale=-1.0),
                     f"rec_{t4}")
            dve.wait(f"rec_{t4}")
            dve.mark(dve.tensor_tensor(t_row, var_sl[:, sl], rec_row, ALU.mult),
                     f"t2_{t4}")
            act.wait(f"t2_{t4}")
            act.mark(act.activation(u_row, t_row, AF.Ln, bias=fc_sb[:, 0:1]),
                     None)
            act.wait_self()
            act.wait(f"mub_x{t4}")
            act.mark(act.activation(mu_sl[:, sl], u_row, AF.Exp, scale=-0.5),
                     f"r2_{t4}")
            pe.wait(f"mu_x{t4}", "mu_x1")
            if t4 >= 1:
                pe.wait(f"xn_{NCH-1}_{t4-1}")
            pe.mark(pe.matmul(AO[:, 0:512],
                              ones1_sb[:], mu_sl[:, sl], start=True, stop=True),
                    f"mub_x{t4}")
            pe.wait(f"r_x{t4}")
            pe.mark(pe.matmul(AO[:, 512:1024], ones1_sb[:], r_sl[:, sl],
                              start=True, stop=True), f"rb_x{t4}")
            r2b = SC_A[:, (t4 % 2) * 512:(t4 % 2) * 512 + 512]
            pe.wait(f"r2_{t4}", "var_x0", "var_x1")
            if t4 >= 2:
                pe.wait(f"x2_{NCH-1}_{t4-2}")
            pe.mark(pe.matmul(r2b, ones1_sb[:], mu_sl[:, sl],
                              start=True, stop=True), f"r2b_{t4}")
            for c in range(NCH):
                csl = slice(c * T + t4 * 512, c * T + (t4 + 1) * 512)
                dve.wait(f"mub_x{t4}", f"rb_x{t4}", "ld_bvb")
                dve.mark(dve.tensor_tensor(tmpA, xbf[:, csl], AO[:, 0:512], ALU.subtract),
                         None)
                dve.mark(dve.tensor_tensor(xnf_a[:, csl], tmpA, AO[:, 512:1024], ALU.mult),
                         f"xn_{c}_{t4}")
            for c in range(NCH):
                csl = slice(c * T + t4 * 512, c * T + (t4 + 1) * 512)
                dve.wait(f"r2b_{t4}")
                dve.mark(dve.scalar_tensor_tensor(tmpA, xnf_a[:, csl],
                                                  lnw_sb[:, c:c + 1], r2b,
                                                  ALU.mult, ALU.mult), None)
                dve.mark(dve.tensor_scalar_add(x2f[:, csl], tmpA, lnb_sb[:, c:c + 1]),
                         f"x2_{c}_{t4}")
        for c in range(NCH):
            sq.wait(f"xn_{c}_3")
            sq.dma(xn_spill[:, c * T:(c + 1) * T], xnf_a[:, c * T:(c + 1) * T],
                   f"spill_{c}")
        sync_all()
        dump_now("xbf", xbf, BF16)
        dump_now("xnf", xnf_a, BF16)
        dump_now("rows", rows, BF16)

        for js in range(2):
            if js == 1:
                for c in range(NCH):
                    qV.wait(f"vp_15_0")
                    qV.dma(wv_sb[:, c * 512:(c + 1) * 512], wv2_blk[c][:, 512:1024],
                           f"ld_wv1_{c}")
            for ct in range(16):
                pb = AO[:, (ct % 4) * 512:(ct % 4) * 512 + 512]
                for c in range(NCH):
                    if c == 0:
                        pe.wait(f"ld_wv{js}_{NCH-1}" if js == 1 else f"ld_wv0_{NCH-1}")
                        if ct >= 4 or js == 1:
                            pv = (js, ct - 4) if ct >= 4 else (0, 12 + ct)
                            pe.wait(f"vev_{pv[1]}_{pv[0]}")
                    pe.mark(pe.matmul(pb, x2f[:, c * T + ct * 128:c * T + (ct + 1) * 128],
                                      wv_sb[:, c * 512:(c + 1) * 512],
                                      start=(c == 0), stop=(c == NCH - 1)),
                            f"vp_{ct}_{js}" if c == NCH - 1 else None)
                dve.wait(f"vp_{ct}_{js}", "ld_bvb")
                dve.mark(dve.tensor_tensor(rb_bf, pb,
                                           bvb[:, js * 512:(js + 1) * 512], ALU.add),
                         f"vev_{ct}_{js}")
                vdst = v_sb[:, ct * 1040 + js * 8 * VGRP:ct * 1040 + (js + 1) * 8 * VGRP]
                dve.mark(dve.tensor_copy(
                    vdst.rearrange("p (g c) -> p g c", c=VGRP)[:, :, 0:64],
                    rb_bf.rearrange("p (g c) -> p g c", c=64)), f"vevs_{ct}_{js}")
        dve.mark(dve.memset(v_sb.rearrange("p (g c) -> p g c", c=VGRP)[:, :, 64:65], 1.0),
                 "v_ones")

        def emit_qw_loads(jj):
            par2 = jj % 2
            qW.wait(f"qkp_{jj-2}_1_3" if jj >= 2 else None)
            qW.dma(wqk_sb[:, par2 * 2048:par2 * 2048 + 1024], winp[jj], f"ld_wq{jj}")
            qW.dma(wqk_sb[:, par2 * 2048 + 1024:par2 * 2048 + 2048], winp[8 + jj],
                   f"ld_wk{jj}")

        def emit_in_proj(jj):
            par2 = jj % 2
            for part in range(2):
                wofs = par2 * 2048 + part * 1024
                for t4 in range(4):
                    pb = (SC_A if part == 0 else SC_B)[:, (t4 % 2) * 512:(t4 % 2) * 512 + 512]
                    for c in range(NCH):
                        if c == 0:
                            pe.wait(f"ld_wq{jj}" if part == 0 else f"ld_wk{jj}")
                            if jj >= 1 and t4 < 2:
                                pe.wait(f"exp_{jj-1}_1_15_{part}")
                            if t4 >= 2:
                                pe.wait(f"qkev_{jj}_{part}_{t4-2}")
                            elif jj >= 1:
                                pe.wait(f"qkev_{jj-1}_{part}_{t4+2}")
                        pe.mark(pe.matmul(pb,
                                          wqk_sb[:, wofs + c * 128:wofs + (c + 1) * 128],
                                          x2f[:, c * T + t4 * 512:c * T + (t4 + 1) * 512],
                                          start=(c == 0), stop=(c == NCH - 1)),
                                f"qkp_{jj}_{part}_{t4}" if c == NCH - 1 else None)
                    act.wait(f"qkp_{jj}_{part}_{t4}", "qb_ready")
                    if part == 0 and t4 == 0 and jj >= 1:
                        act.wait(f"bsq_{jj-1}")
                    dst = qk_sb[:, part * T + t4 * 512:part * T + (t4 + 1) * 512]
                    if part == 0:
                        ins = act.activation(dst, pb, AF.Identity,
                                             bias=qb_sb[:, jj:jj + 1], scale=0.125)
                    else:
                        ins = act.activation(dst, pb, AF.Identity,
                                             bias=binp_sb[:, 8 + jj:9 + jj])
                    act.mark(ins, f"qkev_{jj}_{part}_{t4}")

        def emit_bounds(jj):
            for part in range(2):
                for hf in range(2):
                    dve.wait(f"qkev_{jj}_{part}_{2 * hf + 1}")
                    bsrc = qk_sb[:, part * T + hf * 1024:part * T + (hf + 1) * 1024]
                    dve.mark(dve.scalar_tensor_tensor(
                        exp_sb[:, 0:1024], bsrc, 1.0, bsrc, ALU.mult, ALU.mult,
                        accum_out=bacc[:, 2 * part + hf:2 * part + hf + 1]),
                        f"bsq_{jj}" if (part, hf) == (1, 1) else None)
            dve.wait_self()
            dve.mark(dve.tensor_copy(bacc_bf[:], bacc[:, 0:4]), f"bacc_{jj}")
            pe.wait(f"bacc_{jj}", "ld_e2")
            if jj >= 1:
                pe.wait(f"ubias_{jj-1}", f"srow_{jj-1}_1_3", f"aoev_{jj-1}_1")
            for part in range(2):
                po2 = part * 512
                pe.mark(pe.matmul(AO[0:1, po2:po2 + 2], bacc_bf[:, 2 * part:2 * part + 1],
                                  e2_sb[:], start=True, stop=False), None)
                pe.mark(pe.matmul(AO[0:1, po2:po2 + 2], bacc_bf[:, 2 * part + 1:2 * part + 2],
                                  e2_sb[:], start=False, stop=True),
                        f"bmm_{jj}" if part == 1 else None)
            dve.wait(f"bmm_{jj}")
            dve.mark(dve.tensor_copy(nrow[:, 8:10], AO[0:1, 512:514]), None)
            dve.wait_self()
            dve.mark(dve.tensor_tensor(nrow[:, 0:2], AO[0:1, 0:2], nrow[:, 8:10],
                                       ALU.mult), f"bm1_{jj}")
            act.wait(f"bm1_{jj}")
            act.mark(act.activation(nrow[:, 2:4], nrow[:, 0:2], AF.Ln), None)
            act.wait("ld_fc")
            act.wait_self()
            act.mark(act.activation(nrow[:, 4:6], nrow[:, 2:4], AF.Exp, scale=0.5,
                                    bias=fc_sb[:, 1:2]), f"bc_{jj}")
            dve.wait(f"bc_{jj}")
            dve.mark(dve.tensor_single_scalar(nrow_bf[:, 0:2], nrow[:, 4:6], -1.0,
                                              ALU.mult), f"bneg_{jj}")
            pe.wait(f"bneg_{jj}")
            pe.mark(pe.matmul(AO[:, 4:6], ones1_sb[:], nrow_bf[:, 0:2],
                              start=True, stop=True), f"bbc_{jj}")
            dve.wait(f"bbc_{jj}")
            dve.mark(dve.tensor_copy(ubias[:, 2 * jj:2 * jj + 2], AO[:, 4:6]),
                     f"ubias_{jj}")

        emit_qw_loads(0)
        emit_in_proj(0)
        emit_bounds(0)
        for j in range(8):
            if j + 1 < 8:
                emit_qw_loads(j + 1)
            for h in range(2):
                hg = 2 * j + h
                po = h * 64
                for rnd in range(2):
                    base = rnd * 8
                    for C in range(base, base + 8):
                        for half in range(2):
                            piece = SC_A if half == 0 else SC_B
                            if C >= 1:
                                pe.wait(f"exp_{j}_{h}_{C-1}_{half}")
                            elif h == 1:
                                pe.wait(f"exp_{j}_0_15_{half}")
                            for th in range(2):
                                t4 = half * 2 + th
                                pe.mark(pe.matmul(
                                    piece[:, th * 512:(th + 1) * 512],
                                    qk_sb[po:po + 64, T + C * 128:T + (C + 1) * 128],
                                    qk_sb[po:po + 64, t4 * 512:(t4 + 1) * 512],
                                    start=True, stop=True),
                                    f"sc_{j}_{h}_{C}_{half}" if th == 1 else None)
                            ubcol = 2 * max(j - 1, 0) + h
                            act.wait(f"sc_{j}_{h}_{C}_{half}", f"ubias_{max(j-1, 0)}")
                            act.mark(act.activation(
                                s_sb[:, (C % 8) * T + half * 1024:(C % 8) * T + (half + 1) * 1024],
                                piece[:], AF.Exp, bias=ubias[:, ubcol:ubcol + 1]),
                                f"exp_{j}_{h}_{C}_{half}")
                    for Ca in range(base, base + 8):
                        for t4 in range(4):
                            if Ca == base and t4 == 0:
                                pe.wait(f"exp_{j}_{h}_{base + 6}_1")
                            pe.wait(f"exp_{j}_{h}_{Ca}_{t4 // 2}")
                            if Ca == 0 and t4 == 0:
                                pe.wait("v_ones", "vevs_15_1")
                                if h == 0:
                                    pe.wait(f"ubias_{j}")
                                if not (j == 0 and h == 0):
                                    ph = (j, 0) if h == 1 else (j - 1, 1)
                                    pe.wait(f"aoev_{ph[0]}_{ph[1]}",
                                            f"srow_{ph[0]}_{ph[1]}_3")
                            pe.mark(pe.matmul(
                                AO[0:VGRP, t4 * 512:(t4 + 1) * 512],
                                v_sb[:, Ca * 1040 + hg * VGRP:Ca * 1040 + (hg + 1) * VGRP],
                                s_sb[:, (Ca % 8) * T + t4 * 512:(Ca % 8) * T + (t4 + 1) * 512],
                                start=(Ca == 0), stop=(Ca == 15)),
                                f"av_{j}_{h}_{Ca}" if t4 == 3 else None)
                for t4 in range(4):
                    act.wait(f"av_{j}_{h}_15")
                    act.mark(act.activation(
                        aou[h * 64:(h + 1) * 64, :][:, j * T + t4 * 512:j * T + (t4 + 1) * 512],
                        AO[0:64, t4 * 512:(t4 + 1) * 512], AF.Identity),
                        f"aoev_{j}_{h}" if t4 == 3 else None)
                    dve.wait(f"av_{j}_{h}_15")
                    if t4 >= 2:
                        dve.wait(f"sdma_{j}_{h}_{t4-2}")
                    elif h == 1:
                        dve.wait(f"sdma_{j}_0_{2+t4}")
                    elif j >= 1:
                        dve.wait(f"sdma_{j-1}_1_{2+t4}")
                    sslot = srow_tmp[:, (t4 % 2) * 512:(t4 % 2) * 512 + 512]
                    dve.mark(dve.tensor_copy(sslot,
                                             AO[64:65, t4 * 512:(t4 + 1) * 512]),
                             f"srow_{j}_{h}_{t4}")
                    qS2.wait(f"srow_{j}_{h}_{t4}")
                    qS2.dma(srows[hg:hg + 1, t4 * 512:(t4 + 1) * 512], sslot,
                           f"sdma_{j}_{h}_{t4}")
            if j + 1 < 8:
                emit_in_proj(j + 1)
                emit_bounds(j + 1)

        act.wait("sdma_7_1_3")
        act.mark(act.activation(srows[:], srows[:], AF.Ln), None)
        act.wait_self()
        act.mark(act.activation(rall[:], srows[:], AF.Exp, scale=-1.0), "rall")
        _prev_rb = []
        for t4 in range(4):
            for j in range(8):
                pe.wait("rall", "ld_selp")
                if len(_prev_rb) >= 2:
                    pe.wait(f"rbc_{_prev_rb[-2][0]}_{_prev_rb[-2][1]}")
                else:
                    pe.wait("srow_7_1_3", f"aoev_7_1", f"ubias_7")
                k = t4 * 8 + j
                slot = AO[:, 1024 + (k % 2) * 512:1536 + (k % 2) * 512]
                _prev_rb.append((j, t4))
                pe.mark(pe.matmul(slot, selp_sb[:, j * 128:(j + 1) * 128],
                                  rall[:, t4 * 512:(t4 + 1) * 512],
                                  start=True, stop=True), f"rbmm_{j}_{t4}")
                dve.wait(f"rbmm_{j}_{t4}", f"aoev_{j}_1")
                dve.mark(dve.tensor_copy(rb_bf2[k % 2], slot), f"rbc_{j}_{t4}")
                sl = slice(j * T + t4 * 512, j * T + (t4 + 1) * 512)
                dve.mark(dve.tensor_tensor(aou[:, sl], aou[:, sl], rb_bf2[k % 2],
                                           ALU.mult), f"aon_{j}_{t4}")
        dump_now("x2fB", x2f, BF16)
        dump_now("vB", v_sb, BF16)
        dump_now("aouB", aou, BF16)
        dump_now("qk7", qk_sb, BF16)
        qS2.wait("qkp_7_1_3")
        qS2.dma(bob, bout_b, "ld_bob")
        for jh in range(2):
            wob = wv_sb if jh == 0 else qk_sb
            for c in range(NCH):
                qV.wait("vp_15_1" if jh == 0 else "av_7_1_15")
                qV.dma(wob[:, c * 512:(c + 1) * 512],
                       wo2_blk[c][:, jh * 512:(jh + 1) * 512], f"ld_wo{jh}_{c}")
            for ct in range(16):
                pb = (SC_A if ct % 2 == 0 else SC_B)[:, 0:512]
                for c in range(NCH):
                    pe.wait(f"aon_{c}_{ct // 4}")
                    if c == 0:
                        pe.wait(f"ld_wo{jh}_{NCH-1}")
                        if jh == 0 and ct < 2:
                            pe.wait(f"exp_7_1_15_{ct}")
                        if ct >= 2:
                            pe.wait(f"opev_{ct-2}_{jh}")
                        elif jh == 1:
                            pe.wait(f"opev_{14+ct}_0")
                    pe.mark(pe.matmul(pb, aou[:, c * T + ct * 128:c * T + (ct + 1) * 128],
                                      wob[:, c * 512:(c + 1) * 512],
                                      start=(c == 0), stop=(c == NCH - 1)),
                            f"opp_{ct}_{jh}" if c == NCH - 1 else None)
                dve.wait(f"opp_{ct}_{jh}", "ld_bob")
                dve.mark(dve.tensor_tensor(
                    s_sb[:, ct * D + jh * 512:ct * D + (jh + 1) * 512],
                    pb, bob[:, jh * 512:(jh + 1) * 512], ALU.add),
                    f"opev_{ct}_{jh}")

        sync_all()
        dump_now("sB2", s_sb[:], BF16)
        NPF = 4

        def ffn_wload(q, ht):
            li = q * NHC + ht
            if li >= 4 * NHC:
                return
            if li >= NPF:
                pq, pt = divmod(li - NPF, NHC)
                qF.wait(f"gg_d_{pq}_{pt}", f"uu_d_{pq}_{pt}")
            qF.dma(wg_sb[ht % NPF], wg_blk[ht], f"ld_wg_{q}_{ht}")
            qF.dma(wu_sb[ht % NPF], wu_blk[ht], f"ld_wu_{q}_{ht}")

        gq.dma(w02_sb.rearrange("p (h c f) -> p h c f", h=NL, c=2), w02T.rearrange("h c p f -> p h c f"), "ld_w02")
        gq.dma(w1r_sb.rearrange("p (h c f) -> p h c f", h=NL, c=2), w1r.rearrange("h c p f -> p h c f"), "ld_w1r")
        for li in range(NPF):
            ffn_wload(li // NHC, li % NHC)
        sq.dma(w0n_sb.rearrange("p (h c f) -> p h c f", h=NL, c=2), w0nat.rearrange("h c p f -> p h c f"), "ld_w0n")
        sq.dma(w1n_sb.rearrange("p (h c f) -> p h c f", h=NL, c=2), w1nat.rearrange("h c p f -> p h c f"), "ld_w1n")
        sq.dma(w2n_sb.rearrange("p (h c f) -> p h c f", h=NL, c=2), w2nat.rearrange("h c p f -> p h c f"), "ld_w2n")

        for h in range(NL):
            hb = h % 2
            for dc in range(2):
                gc = 2 * h + dc
                for ct in range(16):
                    pe.wait("ld_id")
                    if h >= 2:
                        pe.wait(f"lact_done_{h-2}")
                    xti = dc * 16 + ct
                    xts = SC_A[0:128, 0:64] if xti % 2 == 0 else SC_A[0:128, 512:576]
                    if xti >= 2:
                        pe.wait(f"xtev_{h}_{(xti - 2) // 16}_{(xti - 2) % 16}")
                    elif xti == 0 and h >= 1:
                        pe.wait(f"z_1_{h-1}_3")
                    pe.mark(pe.transpose(xts.bitcast(BF16),
                                         x2f[:, gc * T + ct * 128:gc * T + (ct + 1) * 128],
                                         id_sb[:]), f"xtp_{h}_{dc}_{ct}")
                    dve.wait(f"xtp_{h}_{dc}_{ct}")
                    dve.mark(dve.tensor_copy(
                        xt_sb[hb][:, ct * 256 + dc * 128:ct * 256 + (dc + 1) * 128],
                        xts.bitcast(BF16)), f"xtev_{h}_{dc}_{ct}")
            if h == 0:
                dump_now("xt0", xt_sb[0], BF16)
            for n in range(2):
                nh = f"{n}_{h}"
                def emit_dw(ct):
                    gt = n * 8 + ct
                    gtt = slice(gt * 256, (gt + 1) * 256)
                    pe.wait(f"dvt_{nh}_{ct}", f"le_{nh}_{ct}", f"xtev_{h}_1_15")
                    if ct == 0 and prevnh is not None:
                        pe.wait(f"oqev_{prevnh}_7")
                    for ds in range(2):
                        pe.mark(pe.matmul(AO[:, ds * 512:(ds + 1) * 512],
                                          xt_sb[hb][:, gt * 256 + ds * 128:gt * 256 + (ds + 1) * 128],
                                          pg_t[:, ct * 512:(ct + 1) * 512],
                                          start=(ct == 0), stop=(ct == 7)),
                                f"dw02_{nh}_{ds}" if ct == 7 else None)
                    for es in range(2):
                        pe.mark(pe.matmul(AO[:, 1024 + es * 512:1024 + es * 512 + 256],
                                          hid_t[:, ct * 256 + es * 128:ct * 256 + (es + 1) * 128],
                                          xt_sb[hb][:, gtt],
                                          start=(ct == 0), stop=(ct == 7)),
                                f"dw1_{nh}_{es}" if ct == 7 else None)

                for ct in range(8):
                    cp = ct % 2
                    tok = n * 1024 + ct * 128
                    ggp = SC_A[:, cp * 512:(cp + 1) * 512]
                    mp = SC_B[:, cp * 512:cp * 512 + 256]
                    prevnh = None if (n == 0 and h == 0) else (f"1_{h-1}" if n == 0 else f"0_{h}")
                    for dc in range(2):
                        gc = 2 * h + dc
                        if dc == 0:
                            pe.wait("ld_w02", "ld_w1r")
                            if ct >= 2:
                                pe.wait(f"dvele_{nh}_{ct-2}")
                            else:
                                pe.wait(f"xtev_{h}_1_15")
                                if prevnh is not None:
                                    pe.wait(f"z_{prevnh}_3")
                        pe.mark(pe.matmul(ggp,
                                          x2f[:, gc * T + tok:gc * T + tok + 128],
                                          w02_sb[:, (h * 2 + dc) * 512:(h * 2 + dc + 1) * 512],
                                          start=(dc == 0), stop=(dc == 1)),
                                f"gg_{nh}_{ct}" if dc == 1 else None)
                        pe.mark(pe.matmul(mp,
                                          x2f[:, gc * T + tok:gc * T + tok + 128],
                                          w1r_sb[:, (h * 2 + dc) * 256:(h * 2 + dc + 1) * 256],
                                          start=(dc == 0), stop=(dc == 1)),
                                f"m_{nh}_{ct}" if dc == 1 else None)
                    if ct >= 1:
                        emit_dw(ct - 1)
                    ctt = slice(ct * 256, (ct + 1) * 256)
                    pool_prev = (f"le_{nh}_{ct-2}" if ct >= 2 else
                                 (f"le_{prevnh}_{6+ct}" if prevnh is not None else None))
                    dvt_prev = (f"dvt_{nh}_{ct-2}" if ct >= 2 else
                                (f"dvt_{prevnh}_{6+ct}" if prevnh is not None else None))
                    act.wait(f"gg_{nh}_{ct}")
                    act.wait(pool_prev)
                    act.mark(act.activation(sg_f[cp], ggp[:, 0:256], AF.Sigmoid),
                             f"sg_{nh}_{ct}")
                    if ct >= 2:
                        act.wait(f"dvele_{nh}_{ct-2}")
                    elif prevnh is not None:
                        act.wait(f"dvele_{prevnh}_{6+ct}")
                    act.mark(act.activation(g_f[cp], ggp[:, 256:512], AF.Identity),
                             f"gcp_{nh}_{ct}")
                    if ct >= 1:
                        ccp = (ct - 1) % 2
                        dve.wait(f"le_{nh}_{ct-1}")
                        dve.mark(dve.tensor_tensor(
                            pg_t[:, (ct - 1) * 512:(ct - 1) * 512 + 256],
                            dh_f[ccp], u_f[ccp], ALU.mult), f"dvt_{nh}_{ct-1}")
                    dve.wait(f"gcp_{nh}_{ct}")
                    dve.wait(pool_prev, dvt_prev)
                    dve.mark(dve.tensor_tensor(hh_f[cp], ggp[:, 0:256], sg_f[cp],
                                               ALU.mult), None)
                    dve.wait(f"m_{nh}_{ct}")
                    dve.mark(dve.tensor_tensor(pg_t[:, ct * 512 + 256:(ct + 1) * 512],
                                               mp, hh_f[cp],
                                               ALU.mult), None)
                    dve.mark(dve.tensor_tensor(dh_f[cp], mp, g_f[cp],
                                               ALU.mult), None)
                    dve.mark(dve.scalar_tensor_tensor(v2_f[cp], ggp[:, 0:256], 1.0,
                                                      hh_f[cp], ALU.add, ALU.subtract),
                             f"dvele_{nh}_{ct}")
                    pool.wait(f"dvele_{nh}_{ct}", dvt_prev)
                    if ct < 2 and prevnh is not None:
                        pool.wait(f"dw1_{prevnh}_1")
                    pool.mark(pool.tensor_tensor(hid_t[:, ctt], hh_f[cp], g_f[cp],
                                                 ALU.mult), None)
                    pool.mark(pool.tensor_tensor(u_f[cp], sg_f[cp], v2_f[cp], ALU.mult),
                              f"le_{nh}_{ct}")
                    if ct == 7:
                        dve.wait(f"le_{nh}_7")
                        dve.mark(dve.tensor_tensor(pg_t[:, 7 * 512:7 * 512 + 256],
                                                   dh_f[cp], u_f[cp],
                                                   ALU.mult), f"dvt_{nh}_7")

                emit_dw(7)
                if n == 0 and h == 0:
                    dump_now("hid0", hid_t, BF16)
                    dump_now("pg0", pg_t, BF16)
                    dump_now("wnT0", wnT_bf, BF16)
                upd = [("dw02", 0, w0n_sb, 0), ("dw1", 1024 + 512, w1n_sb, 512),
                       ("dw02", 256, w2n_sb, 1024)]
                for k, (nm, po_, nat, wofs) in enumerate(upd):
                    for cs in range(2):
                        dve.wait(f"{nm}_{nh}_{cs}", "ld_w2n")
                        psl = (AO[:, cs * 512 + po_:cs * 512 + po_ + 256] if nm == "dw02"
                               else AO[:, 1024 + cs * 512:1024 + cs * 512 + 256])
                        dve.mark(dve.tensor_tensor(
                            wnT_bf[:, wofs + cs * 256:wofs + (cs + 1) * 256],
                            psl,
                            nat[:, (h * 2 + cs) * 256:(h * 2 + cs + 1) * 256],
                            ALU.add), f"wnev_{nh}_{2 * k + cs}")
                for ec in range(2):
                    for c2 in range(2):
                        cp = (ec * 2 + c2) % 2
                        gqp = SC_A[:, cp * 512:(cp + 1) * 512]
                        g2p = SC_B[:, cp * 512:(cp + 1) * 512]
                        csl = slice(n * 1024 + c2 * 512, n * 1024 + (c2 + 1) * 512)
                        for ds in range(2):
                            gc = 2 * h + ds
                            if ds == 0:
                                pe.wait(f"wnev_{nh}_5")
                                if ec * 2 + c2 >= 2:
                                    pe.wait(f"z_{nh}_{ec * 2 + c2 - 2}")
                            pe.mark(pe.matmul(gqp,
                                              wnT_bf[:, 0 + ds * 256 + ec * 128:0 + ds * 256 + (ec + 1) * 128],
                                              x2f[:, gc * T + csl.start:gc * T + csl.stop],
                                              start=(ds == 0), stop=(ds == 1)),
                                    f"gq_{nh}_{ec}_{c2}" if ds == 1 else None)
                            pe.mark(pe.matmul(g2p,
                                              wnT_bf[:, 1024 + ds * 256 + ec * 128:1024 + ds * 256 + (ec + 1) * 128],
                                              x2f[:, gc * T + csl.start:gc * T + csl.stop],
                                              start=(ds == 0), stop=(ds == 1)),
                                    f"gq2_{nh}_{ec}_{c2}" if ds == 1 else None)
                        act.wait(f"gq_{nh}_{ec}_{c2}")
                        act.mark(act.activation(hq_f[cp], gqp, AF.Sigmoid), f"hqs_{nh}_{ec}_{c2}")
                        dve.wait(f"hqs_{nh}_{ec}_{c2}")
                        dve.mark(dve.tensor_tensor(hq_f[cp], gqp, hq_f[cp], ALU.mult),
                                 None)
                        dve.wait(f"gq2_{nh}_{ec}_{c2}")
                        dve.mark(dve.tensor_tensor(
                            z_sb[:, ec * 1024 + c2 * 512:ec * 1024 + (c2 + 1) * 512],
                            hq_f[cp], g2p, ALU.mult), f"z_{nh}_{ec * 2 + c2}")
                if n == 0 and h == 0:
                    dump_now("z0", z_sb, BF16)
                for ct in range(8):
                    op = AO[:, (ct % 4) * 512:(ct % 4) * 512 + 256]
                    for ec in range(2):
                        if ec == 0:
                            pe.wait(f"z_{nh}_{ct // 4}", f"z_{nh}_{2 + ct // 4}")
                            if ct >= 4:
                                pe.wait(f"oqev_{nh}_{ct-4}")
                            elif not (n == 0 and h == 0):
                                pn = f"1_{h-1}" if n == 0 else f"0_{h}"
                                pe.wait(f"oqev_{pn}_{4 + ct}")
                        pe.mark(pe.matmul(op,
                                          z_sb[:, ec * 1024 + ct * 128:ec * 1024 + (ct + 1) * 128],
                                          wnT_bf[:, 512 + ec * 256:512 + (ec + 1) * 256],
                                          start=(ec == 0), stop=(ec == 1)),
                                f"oq_{nh}_{ct}" if ec == 1 else None)
                    dve.wait(f"oq_{nh}_{ct}", f"opev_{n * 8 + ct}_1")
                    ssl = slice((n * 8 + ct) * D + h * 256, (n * 8 + ct) * D + h * 256 + 256)
                    dve.mark(dve.tensor_tensor(s_sb[:, ssl], op, s_sb[:, ssl], ALU.add),
                             f"oqev_{nh}_{ct}")
            if True:
                alias(f"lact_done_{h}", f"oqev_1_{h}_7")

        sync_all()
        for hc in range(NHC):
            qD.dma(wd_sb[:, hc * D:(hc + 1) * D], wdT_blk[hc], f"ld_wd{hc}")
        st_count = [0]
        for q in range(4):
            xb = xnf_d[q % 2]
            for c in range(NCH):
                qN.wait(f"spill_{NCH-1}")
                if q > 1:
                    qN.wait(f"uu_d_{q-2}_{NHC-1}")
                qN.dma(xb[:, c * 512:(c + 1) * 512],
                       xn_spill[:, c * T + q * 512:c * T + (q + 1) * 512],
                       f"ld_xn_{q}_{c}")
            for ht in range(NHC):
                li = q * NHC + ht
                ffn_wload((li + NPF) // NHC, (li + NPF) % NHC)
                p = ht % 2
                gp = SC_A[:, p * 512:(p + 1) * 512]
                up = SC_B[:, p * 512:(p + 1) * 512]
                for c in range(NCH):
                    if c == 0:
                        pe.wait(f"ld_wg_{q}_{ht}", f"ld_wu_{q}_{ht}", f"ld_xn_{q}_{NCH-1}")
                        if ht >= 2:
                            pe.wait(f"guev_{q}_{ht-2}_1")
                        elif q >= 1:
                            pe.wait(f"guev_{q-1}_{NHC-2+ht}_1")
                    pe.mark(pe.matmul(gp, wg_sb[ht % NPF][:, c * 128:(c + 1) * 128],
                                      xb[:, c * 512:(c + 1) * 512],
                                      start=(c == 0), stop=(c == NCH - 1)),
                            f"gg_d_{q}_{ht}" if c == NCH - 1 else None)
                    pe.mark(pe.matmul(up, wu_sb[ht % NPF][:, c * 128:(c + 1) * 128],
                                      xb[:, c * 512:(c + 1) * 512],
                                      start=(c == 0), stop=(c == NCH - 1)),
                            f"uu_d_{q}_{ht}" if c == NCH - 1 else None)
                act.wait(f"gg_d_{q}_{ht}")
                act.mark(act.activation(silu_f, gp, AF.Silu, scale=S127),
                         f"guev_{q}_{ht}_0")
                dve.wait(f"guev_{q}_{ht}_0", f"uu_d_{q}_{ht}")
                dve.mark(dve.scalar_tensor_tensor(
                    ht_sb[:, ht * 512:(ht + 1) * 512], up, S127, silu_f,
                    ALU.mult, ALU.mult), f"guev_{q}_{ht}_1")
            for tt in range(4):
                for dh2 in range(2):
                    ti = tt * 2 + dh2
                    p2 = ti % 2
                    tok0 = q * 512 + tt * 128
                    dp = AO[:, (ti % 4) * 512:(ti % 4) * 512 + 512]
                    qX.wait(f"ob_{q}_{ti-4}" if ti >= 4 else
                            (f"ob_{q-1}_{4+ti}" if q >= 1 else None))
                    qX.dma(x_f[ti % 4], x_t[tok0:tok0 + 128, dh2 * 512:(dh2 + 1) * 512],
                           f"ld_x_{q}_{ti}")
                    for hc in range(NHC):
                        if hc == 0:
                            pe.wait(f"guev_{q}_{NHC-1}_1", f"ld_wd{NHC-1}")
                            if ti >= 4:
                                pe.wait(f"dnev_{q}_{ti-4}")
                            elif q >= 1:
                                pe.wait(f"dnev_{q-1}_{4+ti}")
                        pe.mark(pe.matmul(dp, ht_sb[:, hc * 512 + tt * 128:hc * 512 + (tt + 1) * 128],
                                          wd_sb[:, hc * D + dh2 * 512:hc * D + (dh2 + 1) * 512],
                                          start=(hc == 0), stop=(hc == NHC - 1)),
                                f"dn_{q}_{ti}" if hc == NHC - 1 else None)
                    act.wait(f"dn_{q}_{ti}")
                    if ti >= 2:
                        act.wait(f"dnev_{q}_{ti-2}")
                    elif q >= 1:
                        act.wait(f"dnev_{q-1}_{6+ti}")
                    act.mark(act.activation(sig_f[p2], dp, AF.Sigmoid, scale=S127),
                             f"dsig_{q}_{ti}")
                    dve.wait(f"dsig_{q}_{ti}")
                    dve.mark(dve.scalar_tensor_tensor(gg_f[p2], dp, S127, sig_f[p2],
                                                      ALU.mult, ALU.mult),
                             f"dnev_{q}_{ti}")
                    ssl = slice((q * 4 + tt) * D + dh2 * 512, (q * 4 + tt) * D + (dh2 + 1) * 512)
                    dve.mark(dve.tensor_tensor(sgp_f[p2], s_sb[:, ssl], gg_f[p2],
                                               ALU.mult), None)
                    dve.wait(f"ld_x_{q}_{ti}")
                    if ti >= 2:
                        dve.wait(f"st_{q}_{ti-2}")
                    elif q >= 1:
                        dve.wait(f"st_{q-1}_{6+ti}")
                    dve.mark(dve.tensor_tensor(outb[p2], sgp_f[p2], x_f[ti % 4],
                                               ALU.add), f"ob_{q}_{ti}")
                    qO.wait(f"ob_{q}_{ti}")
                    qO.dma(out[tok0:tok0 + 128, dh2 * 512:(dh2 + 1) * 512], outb[p2],
                           f"st_{q}_{ti}")
                    st_count[0] += 1

        if not B.dry:
            B.engines["sq"].wait_ge(B.sems["qO"], B.counts["qO"])

        if KDBG:
            dump_map = {
                "x2f": (x2f, BF16), "aou": (aou, BF16), "s_sb": (s_sb[:], BF16),
                "v_sb": (v_sb, BF16), "srows": (srows[:], F32), "rall": (rall[:], BF16),
            }
            for nm in KDBG.split(","):
                ap, dt = dump_map[nm]
                if B.dry:
                    dtn = nc.dram_tensor("dbg_" + nm, list(ap.shape), dt,
                                         kind="ExternalOutput").ap()
                    B.marks["dbgt_" + nm] = dtn
                dtn = B.marks["dbgt_" + nm]
                sq.dma(dtn, ap, None)
            if not B.dry:
                B.engines["sq"].wait_ge(B.sems["sq"], B.counts["sq"])
            else:
                pass

    program()
    B.dry = False
    B.counts = {}
    B.hiwater = {}
    with nc.Block() as blk:
        @blk.sync
        def _(s):
            gx, sx = nc.gpsimd, nc.sync
            B.engines = {"pe": nc.tensor, "act": nc.scalar, "dve": nc.vector,
                         "gq": gx, "sq": sx, "qC": gx, "qW": gx, "qV": gx,
                         "qF": gx, "qD": gx, "qS2": sx, "qN": sx, "qX": sx,
                         "qO": sx, "qP": gx}
            program()
    return nc


def _blockT(w):
    o, i = w.shape
    return np.ascontiguousarray(
        w.reshape(o // 128, 128, i // 128, 128).transpose(0, 3, 2, 1).reshape(o // 128, 128, i))


_NC_CACHE = [None]

def kernel(**inp):
    f32 = np.float32
    xs = np.asarray(inp["L0_attn_x"], f32)
    ipw = np.asarray(inp["L0_attn_in_proj_w"], f32)
    ipb = np.asarray(inp["L0_attn_in_proj_b"], f32)
    opw = np.asarray(inp["L0_attn_out_proj_w"], f32)
    opb = np.asarray(inp["L0_attn_out_proj_b"], f32)
    lnw = np.asarray(inp["L0_attn_ln_w"], f32)
    lnb = np.asarray(inp["L0_attn_ln_b"], f32)
    w0 = np.asarray(inp["L0_attn_w0"], f32)
    w1 = np.asarray(inp["L0_attn_w1"], f32)
    w2 = np.asarray(inp["L0_attn_w2"], f32)
    wg = np.ascontiguousarray(np.asarray(inp["L0_ffn_w_gate"], np.int32))
    wu = np.ascontiguousarray(np.asarray(inp["L0_ffn_w_up"], np.int32))
    wd = np.ascontiguousarray(np.asarray(inp["L0_ffn_w_down"], np.int32))

    shared = {
        "winp": _blockT(ipw),
        "binp": np.ascontiguousarray(ipb.reshape(24, 128).T),
        "wout": _blockT(opw),
        "wv2_blk": np.ascontiguousarray(ipw[2048:].T.reshape(8, 128, D)),
        "wo2_blk": np.ascontiguousarray(opw.T.reshape(8, 128, D)),
        "bout_b": np.ascontiguousarray(np.broadcast_to(opb, (128, D))),
        "bv_b": np.ascontiguousarray(np.broadcast_to(ipb[2048:], (128, D))),
        "lnw_c": np.ascontiguousarray(lnw.reshape(8, 128).T),
        "lnb_c": np.ascontiguousarray(lnb.reshape(8, 128).T),
        "w02T": np.ascontiguousarray(np.concatenate(
            [np.stack([w0[h].T.reshape(2, 128, 256) for h in range(NL)]),
             np.stack([w2[h].T.reshape(2, 128, 256) for h in range(NL)])], axis=3)),
        "w1r": np.ascontiguousarray(w1.reshape(NL, 2, 128, 256)),
        "w0nat": np.ascontiguousarray(
            np.stack([w0[hh].T.reshape(2, 128, 256) for hh in range(NL)])),
        "w1nat": np.ascontiguousarray(
            np.stack([w1[hh].T.reshape(2, 128, 256) for hh in range(NL)])),
        "w2nat": np.ascontiguousarray(
            np.stack([w2[hh].T.reshape(2, 128, 256) for hh in range(NL)])),
        "wg_blk": _blockT(wg),
        "wu_blk": _blockT(wu),
        "wdT_blk": np.ascontiguousarray(wd.T.reshape(24, 128, D)),
        "id128": np.eye(128, dtype=f32),
        "e2c": np.ascontiguousarray(
            np.stack([(np.arange(128) < 64).astype(f32),
                      (np.arange(128) >= 64).astype(f32)], axis=1)),
        "ones1": np.ones((1, 128), f32),
        "selp": np.ascontiguousarray(np.stack([
            np.stack([((np.arange(128) < 64) if hh == 2 * jj else
                       ((np.arange(128) >= 64) if hh == 2 * jj + 1 else
                        np.zeros(128, bool))).astype(f32)
                      for hh in range(16)])
            for jj in range(8)])),
        "onescol": np.ones((128, 1), f32),
        "fconst": np.array([[EPS, np.log(KAPPA / T)]], f32),
    }
    if _NC_CACHE[0] is None:
        _NC_CACHE[0] = build_nc()
    nc = _NC_CACHE[0]
    in_maps = []
    for b in range(8):
        d = dict(shared)
        d["x_t"] = np.ascontiguousarray(xs[b])
        d["xT_blk"] = np.ascontiguousarray(xs[b].T.reshape(NCH, 128, T))
        in_maps.append(d)
    res = run_bass_kernel_spmd(nc, in_maps, core_ids=list(range(8)))
    return np.stack([r["out"] for r in res.results])


if __name__ == "__main__":
    nc = build_nc()
    print("build ok; sbuf remaining:", nc.sbuf_bytes_remaining,
          "psum banks remaining:", nc.psum_banks_remaining)
